# version 01
# baseline (speedup 1.0000x reference)
import contextlib
import numpy as np
import ml_dtypes
import concourse.bass as bass
import concourse.mybir as mybir
from concourse.bass_utils import run_bass_kernel_spmd

F32 = mybir.dt.float32
BF16 = mybir.dt.bfloat16
AF = mybir.ActivationFunctionType
ALU = mybir.AluOpType
AX = mybir.AxisListType
BFDT = ml_dtypes.bfloat16

P = 128
D = 2048
S = 2048
B = 4
HALF = 1024
NT = HALF // P
NC = D // P
NKT = S // P
HEADS = 16
DH = 128
KV0 = 4
LN_EPS = 1e-5
RMS_EPS = 1e-6
C0 = 12.0
INV_SQRT_DH = float(1.0 / np.sqrt(128.0))
NCORE = 8


class Ctr:
    def __init__(self, sem):
        self.sem = sem
        self.v = 0


class Plan:
    ENGS = ["sync", "gpsimd", "tensor", "vector", "scalar"]

    def __init__(self, nc, es, sem_names):
        self.nc = nc
        self.ops = {e: [] for e in self.ENGS}
        self.ctr = {n: Ctr(es.enter_context(nc.semaphore(n))) for n in sem_names}

    def on(self, eng, fn):
        self.ops[eng].append(fn)

    def wait(self, eng, cname, val=None):
        c = self.ctr[cname]
        v = c.v if val is None else val
        self.on(eng, lambda e, s=c.sem, v=v: e.wait_ge(s, v))
        return v

    def inc(self, eng, cname, emit, n=1):
        c = self.ctr[cname]
        c.v += n
        self.on(eng, lambda e, emit=emit, s=c.sem, n=n: emit(e).then_inc(s, n))
        return c.v

    def run(self, eng, emit):
        self.on(eng, emit)

    def dma(self, eng, cname, out, in_):
        return self.inc(eng, cname, lambda e, o=out, i=in_: e.dma_start(out=o, in_=i), 16)

    def finalize(self, block):
        for eng in self.ENGS:
            ops = self.ops[eng]
            if eng == "sync":
                @block.sync
                def _(e, ops=ops):
                    for f in ops:
                        f(e)
            elif eng == "gpsimd":
                @block.gpsimd
                def _(e, ops=ops):
                    for f in ops:
                        f(e)
            elif eng == "tensor":
                @block.tensor
                def _(e, ops=ops):
                    for f in ops:
                        f(e)
            elif eng == "vector":
                @block.vector
                def _(e, ops=ops):
                    for f in ops:
                        f(e)
            elif eng == "scalar":
                @block.scalar
                def _(e, ops=ops):
                    for f in ops:
                        f(e)


def _bass():
    return bass.Bass("TRN2", target_bir_lowering=False)



def plan_ln(pl, xb, stats, sq_scr, xn_out, eps_col, x_ready_ev,
            stats_war_ve=None, xn_war=None):
    cn, cv = x_ready_ev
    pl.wait("scalar", cn, cv)
    if stats_war_ve is not None:
        pl.wait("scalar", "ve", stats_war_ve)
    pl.inc("scalar", "se", lambda s: s.activation(
        sq_scr[:], xb[:], AF.Square, accum_out=stats[:, 1:2]))
    se_sq = pl.ctr["se"].v
    pl.wait("vector", cn, cv)
    pl.inc("vector", "ve", lambda v: v.reduce_sum(stats[:, 0:1], xb[:], axis=AX.X))
    pl.wait("vector", "ve")
    pl.inc("vector", "ve", lambda v: v.tensor_scalar(
        stats[:, 2:3], stats[:, 0:1], 1.0 / D, None, ALU.mult))
    pl.wait("vector", "ve")
    pl.wait("vector", "se", se_sq)
    pl.inc("vector", "ve", lambda v: v.scalar_tensor_tensor(
        stats[:, 3:4], stats[:, 0:1], stats[:, 2:3], stats[:, 1:2],
        ALU.mult, ALU.subtract))
    stt_ve = pl.ctr["ve"].v
    pl.wait("scalar", "ve")
    pl.inc("scalar", "se", lambda s: s.activation(
        stats[:, 1:2], stats[:, 3:4], AF.Ln, bias=eps_col, scale=-1.0 / D))
    pl.wait("scalar", "se")
    pl.inc("scalar", "se", lambda s: s.activation(
        stats[:, 3:4], stats[:, 1:2], AF.Exp, scale=-0.5))
    pl.wait("vector", "se")
    if xn_war is not None:
        pl.wait("vector", xn_war[0], xn_war[1])
    pl.inc("vector", "ve", lambda v: v.tensor_scalar(
        xn_out[:, 0:D], xb[:], stats[:, 2:3], stats[:, 3:4],
        ALU.subtract, ALU.mult))
    return pl.ctr["ve"].v, stt_ve


def plan_rope(pl, src, out, sw, t1, t2, ropeC, ropeS, src_ready_ve,
              t2_war_ve=None):
    n = src.shape[1]
    pl.wait("gpsimd", "ve", src_ready_ve)
    pl.run("gpsimd", lambda g: g.tensor_copy(sw[0:64, 0:n], src[64:128, 0:n]))
    pl.inc("gpsimd", "ge", lambda g: g.tensor_copy(sw[64:128, 0:n], src[0:64, 0:n]))
    ge_sw = pl.ctr["ge"].v
    pl.wait("gpsimd", "ge")
    if t2_war_ve is not None:
        pl.wait("gpsimd", "ve", t2_war_ve)
    pl.inc("gpsimd", "ge", lambda g: g.tensor_mul(t2[:, 0:n], sw[:, 0:n], ropeS[:, 0:n]))
    ge_t2 = pl.ctr["ge"].v
    pl.wait("vector", "ve", src_ready_ve)
    pl.inc("vector", "ve", lambda v: v.tensor_mul(t1[:, 0:n], src[:, 0:n], ropeC[:, 0:n]))
    pl.wait("vector", "ve")
    pl.wait("vector", "ge", ge_t2)
    pl.inc("vector", "ve", lambda v: v.tensor_add(out[:, 0:n], t1[:, 0:n], t2[:, 0:n]))
    return pl.ctr["ve"].v, ge_sw



def build_A():
    nc = _bass()
    xA = nc.dram_tensor("xA", [HALF, D], F32, kind="ExternalInput").ap()
    wq_d = nc.dram_tensor("wq", [D, D], F32, kind="ExternalInput").ap()
    wk_d = nc.dram_tensor("wk", [D, 512], F32, kind="ExternalInput").ap()
    wv_d = nc.dram_tensor("wv", [D, 512], F32, kind="ExternalInput").ap()
    ropeC_d = nc.dram_tensor("ropeC", [P, HALF], F32, kind="ExternalInput").ap()
    ropeS_d = nc.dram_tensor("ropeS", [P, HALF], F32, kind="ExternalInput").ap()
    ident_d = nc.dram_tensor("ident", [P, P], F32, kind="ExternalInput").ap()
    consts_d = nc.dram_tensor("consts", [P, 4], F32, kind="ExternalInput").ap()
    xn_fm_o = nc.dram_tensor("xn_fm", [NC, P, HALF], BF16, kind="ExternalOutput").ap()
    q_o = nc.dram_tensor("q_o", [HEADS, P, HALF], BF16, kind="ExternalOutput").ap()
    k_o = nc.dram_tensor("k_o", [KV0, P, HALF], BF16, kind="ExternalOutput").ap()
    v_o = nc.dram_tensor("v_o", [HALF, 512], BF16, kind="ExternalOutput").ap()

    with contextlib.ExitStack() as es:
        E = es.enter_context
        xb = [E(nc.sbuf_tensor(f"xb{i}", [P, D], F32)) for i in range(2)]
        sq_scr = E(nc.sbuf_tensor("sq_scr", [P, D], F32))
        stats = E(nc.sbuf_tensor("stats", [P, 4], F32))
        consts = E(nc.sbuf_tensor("consts_sb", [P, 4], F32))
        xn_tm = E(nc.sbuf_tensor("xn_tm", [P, D], BF16))
        xn_fm = E(nc.sbuf_tensor("xn_fm_sb", [P, NC * HALF], BF16))
        idb = E(nc.sbuf_tensor("idb", [P, P], BF16))
        wbuf = E(nc.sbuf_tensor("wbuf", [P, NC * D], BF16))
        wkv = E(nc.sbuf_tensor("wkv", [P, NC * 1024], BF16))
        ropeC = E(nc.sbuf_tensor("ropeC_sb", [P, HALF], BF16))
        ropeS = E(nc.sbuf_tensor("ropeS_sb", [P, HALF], BF16))
        qraw = [E(nc.sbuf_tensor(f"qraw{i}", [P, HALF], BF16)) for i in range(2)]
        qout = [E(nc.sbuf_tensor(f"qout{i}", [P, HALF], BF16)) for i in range(2)]
        sw = E(nc.sbuf_tensor("sw", [P, HALF], BF16))
        t1 = E(nc.sbuf_tensor("t1", [P, HALF], BF16))
        t2 = E(nc.sbuf_tensor("t2", [P, HALF], BF16))
        vbuf = [E(nc.sbuf_tensor(f"vbuf{i}", [P, 512], BF16)) for i in range(2)]

        ps_tr = [E(nc.psum_tensor(f"ps_tr{i}", [P, 512], BF16)) for i in range(2)]
        ps_q = [E(nc.psum_tensor(f"ps_q{i}", [P, 512], F32)) for i in range(4)]
        ps_v = [E(nc.psum_tensor(f"ps_v{i}", [P, 512], F32)) for i in range(2)]

        pl = Plan(nc, es, ["ld_x", "ld_w", "ld_c", "ve", "se", "ge", "te", "st"])
        block = E(nc.Block())

        xn_fm_r = xn_fm[:].rearrange("p (c q) -> p c q", q=HALF)

        pl.dma("sync", "ld_c", consts[:], consts_d)
        pl.dma("gpsimd", "ld_c", idb[:], ident_d)
        pl.dma("gpsimd", "ld_c", ropeC[:], ropeC_d)
        pl.dma("gpsimd", "ld_c", ropeS[:], ropeS_d)
        LD_C_ALL = pl.ctr["ld_c"].v
        eps_col = consts[:, 0:1]

        for r in range(NC):
            pl.dma("gpsimd", "ld_w", wbuf[:, r * D:(r + 1) * D], wq_d[r * P:(r + 1) * P, :])
        W_Q_DONE = pl.ctr["ld_w"].v
        for r in range(NC):
            pl.dma("gpsimd", "ld_w", wkv[:, r * 512:(r + 1) * 512], wk_d[r * P:(r + 1) * P, :])
        W_K_DONE = pl.ctr["ld_w"].v
        for r in range(NC):
            pl.dma("gpsimd", "ld_w", wkv[:, 8192 + r * 512:8192 + (r + 1) * 512],
                   wv_d[r * P:(r + 1) * P, :])
        W_V_DONE = pl.ctr["ld_w"].v

        xn_done_ve = []
        stt_ve_l = []
        evac_done_ve = []
        tr_batch_te = []
        tr_evac_ve = []
        for t in range(NT):
            if t >= 2:
                pl.wait("sync", "ve", xn_done_ve[t - 2])
            pl.dma("sync", "ld_x", xb[t % 2][:], xA[t * P:(t + 1) * P, :])
            ldv = pl.ctr["ld_x"].v
            if t == 0:
                pl.wait("scalar", "ld_c", LD_C_ALL)
            v_xn, v_stt = plan_ln(
                pl, xb[t % 2], stats, sq_scr, xn_tm, eps_col, ("ld_x", ldv),
                stats_war_ve=(stt_ve_l[-1] if stt_ve_l else None),
                xn_war=(("te", tr_batch_te[-1]) if tr_batch_te else None))
            xn_done_ve.append(v_xn)
            stt_ve_l.append(v_stt)
            pl.wait("tensor", "ve", v_xn)
            if t == 0:
                pl.wait("tensor", "ld_c", LD_C_ALL)
            for cb in range(4):
                j = (t * 4 + cb) % 2
                if t * 4 + cb >= 2:
                    pl.wait("tensor", "ve", tr_evac_ve[(t * 4 + cb) - 2])
                for cc in range(4):
                    c = cb * 4 + cc
                    if cc < 3:
                        pl.run("tensor", lambda e, j=j, cc=cc, c=c: e.transpose(
                            ps_tr[j][:, cc * P:(cc + 1) * P],
                            xn_tm[:, c * P:(c + 1) * P], idb[:]))
                    else:
                        pl.inc("tensor", "te", lambda e, j=j, cc=cc, c=c: e.transpose(
                            ps_tr[j][:, cc * P:(cc + 1) * P],
                            xn_tm[:, c * P:(c + 1) * P], idb[:]))
                te_v = pl.ctr["te"].v
                tr_batch_te.append(te_v)
                pl.wait("vector", "te", te_v)
                dst = xn_fm_r[:, cb * 4:(cb + 1) * 4, t * P:(t + 1) * P]
                src_ = ps_tr[j][:].rearrange("p (c q) -> p c q", q=P)
                pl.inc("vector", "ve", lambda v, d=dst, s=src_: v.tensor_copy(d, s))
                tr_evac_ve.append(pl.ctr["ve"].v)
            evac_done_ve.append(pl.ctr["ve"].v)
        XN_ALL_VE = evac_done_ve[-1]

        pl.wait("gpsimd", "ve", XN_ALL_VE)
        for c in range(NC):
            pl.dma("gpsimd", "st", xn_fm_o[c, :, :], xn_fm[:, c * HALF:(c + 1) * HALF])

        head_evac_ve = []
        head_out_ve = []
        head_st = []
        rope_ge_sw = []
        for hd in range(HEADS + KV0):
            is_k = hd >= HEADS
            kh = hd - HEADS
            slot = hd % 2
            if hd == 0:
                pl.wait("tensor", "ld_w", W_Q_DONE)
                pl.wait("tensor", "ve", XN_ALL_VE)
            if hd == HEADS:
                pl.wait("tensor", "ld_w", W_K_DONE)
            if hd >= 2:
                pl.wait("tensor", "ve", head_evac_ve[hd - 2])
            for qc in range(2):
                for c in range(NC):
                    if is_k:
                        lhsT = wkv[:, c * 512 + kh * P: c * 512 + (kh + 1) * P]
                    else:
                        lhsT = wbuf[:, c * D + hd * P: c * D + (hd + 1) * P]
                    rhs = xn_fm_r[:, c, qc * 512:(qc + 1) * 512]
                    if c < NC - 1:
                        pl.run("tensor", lambda e, o=ps_q[slot * 2 + qc][:], l=lhsT, r=rhs,
                               st_=(c == 0): e.matmul(o, l, r, start=st_, stop=False))
                    else:
                        pl.inc("tensor", "te", lambda e, o=ps_q[slot * 2 + qc][:], l=lhsT,
                               r=rhs: e.matmul(o, l, r, start=False, stop=True))
            te_head = pl.ctr["te"].v
            pl.wait("vector", "te", te_head)
            if hd >= 2:
                pl.wait("vector", "ve", head_out_ve[hd - 2])
                pl.wait("vector", "ge", rope_ge_sw[hd - 2])
            pl.run("vector", lambda v, s=slot: v.tensor_copy(
                qraw[s][:, 0:512], ps_q[s * 2][:]))
            pl.inc("vector", "ve", lambda v, s=slot: v.tensor_copy(
                qraw[s][:, 512:1024], ps_q[s * 2 + 1][:]))
            head_evac_ve.append(pl.ctr["ve"].v)
            if hd >= 2:
                pl.wait("vector", "st", head_st[hd - 2])
                pl.wait("gpsimd", "st", head_st[hd - 2])
            v_out, ge_sw = plan_rope(
                pl, qraw[slot], qout[slot], sw, t1, t2, ropeC, ropeS,
                head_evac_ve[-1],
                t2_war_ve=(head_out_ve[-1] if head_out_ve else None))
            rope_ge_sw.append(ge_sw)
            head_out_ve.append(v_out)
            pl.wait("sync", "ve", v_out)
            dst = k_o[kh, :, :] if is_k else q_o[hd, :, :]
            pl.dma("sync", "st", dst, qout[slot][:])
            head_st.append(pl.ctr["st"].v)

        v_evac_ve = []
        v_st = []
        for t in range(NT):
            if t == 0:
                pl.wait("tensor", "ld_w", W_V_DONE)
            if t >= 2:
                pl.wait("tensor", "ve", v_evac_ve[t - 2])
            for c in range(NC):
                lhsT = xn_fm_r[:, c, t * P:(t + 1) * P]
                rhs = wkv[:, 8192 + c * 512:8192 + (c + 1) * 512]
                if c < NC - 1:
                    pl.run("tensor", lambda e, o=ps_v[t % 2][:], l=lhsT, r=rhs,
                           st_=(c == 0): e.matmul(o, l, r, start=st_, stop=False))
                else:
                    pl.inc("tensor", "te", lambda e, o=ps_v[t % 2][:], l=lhsT, r=rhs:
                           e.matmul(o, l, r, start=False, stop=True))
            te_v = pl.ctr["te"].v
            pl.wait("vector", "te", te_v)
            if t >= 2:
                pl.wait("vector", "st", v_st[t - 2])
            pl.inc("vector", "ve", lambda v, t=t: v.tensor_copy(
                vbuf[t % 2][:], ps_v[t % 2][:]))
            v_evac_ve.append(pl.ctr["ve"].v)
            pl.wait("sync", "ve", v_evac_ve[-1])
            pl.dma("sync", "st", v_o[t * P:(t + 1) * P, :], vbuf[t % 2][:])
            v_st.append(pl.ctr["st"].v)

        pl.wait("sync", "st", pl.ctr["st"].v)
        pl.finalize(block)
    return nc



def _rope_tables(h):
    inv = (1.0 / 10000.0 ** (np.arange(0, DH, 2, dtype=np.float64) / DH))
    t = np.arange(h * HALF, (h + 1) * HALF, dtype=np.float64)
    ang = t[None, :] * inv[:, None]
    c = np.cos(ang)
    s = np.sin(ang)
    Cf = np.concatenate([c, c], axis=0).astype(np.float32)
    Sf = np.concatenate([-s, s], axis=0).astype(np.float32)
    return Cf, Sf


_PERM = np.concatenate([np.arange(0, DH, 2), np.arange(1, DH, 2)])


def _prep_A(inputs):
    x = np.asarray(inputs["L0_attn_x"])
    w_attn = np.asarray(inputs["L0_attn_w_attn"])
    wq = w_attn[:, :D].reshape(D, HEADS, DH)[:, :, _PERM].reshape(D, D)
    wk = w_attn[:, D:D + 512].reshape(D, KV0, DH)[:, :, _PERM].reshape(D, 512)
    wv = np.ascontiguousarray(w_attn[:, D + 512:])
    ident = np.eye(P, dtype=np.float32)
    consts = np.zeros((P, 4), dtype=np.float32)
    consts[:, 0] = LN_EPS
    maps = []
    for c in range(NCORE):
        b, h = divmod(c, 2)
        Cf, Sf = _rope_tables(h)
        maps.append({
            "xA": np.ascontiguousarray(x[b, h * HALF:(h + 1) * HALF, :]),
            "wq": np.ascontiguousarray(wq),
            "wk": np.ascontiguousarray(wk),
            "wv": wv,
            "ropeC": Cf, "ropeS": Sf,
            "ident": ident, "consts": consts,
        })
    return maps






def build_B(debug=False):
    nc = _bass()
    q_d = nc.dram_tensor("q_i", [HEADS, P, HALF], BF16, kind="ExternalInput").ap()
    k_d = nc.dram_tensor("k_i", [KV0, P, S], BF16, kind="ExternalInput").ap()
    v_d = nc.dram_tensor("v_i", [S, 512], BF16, kind="ExternalInput").ap()
    wp_d = nc.dram_tensor("wp", [D, D], F32, kind="ExternalInput").ap()
    nw_d = nc.dram_tensor("nw", [1, D], F32, kind="ExternalInput").ap()
    mask_d = nc.dram_tensor("mask", [S, HALF], BF16, kind="ExternalInput").ap()
    consts_d = nc.dram_tensor("consts", [P, 4], F32, kind="ExternalInput").ap()
    s_o = nc.dram_tensor("s_o", [HALF, D], BF16, kind="ExternalOutput").ap()
    if debug:
        y_dbg = nc.dram_tensor("y_dbg", [HEADS, P, HALF], BF16, kind="ExternalOutput").ap()
        d_dbg = nc.dram_tensor("d_dbg", [HEADS, P, HALF], F32, kind="ExternalOutput").ap()

    with contextlib.ExitStack() as es:
        E = es.enter_context
        q_sb = E(nc.sbuf_tensor("q_sb", [P, HEADS * HALF], BF16))
        k_sb = E(nc.sbuf_tensor("k_sb", [P, KV0 * S], BF16))
        v_sb = E(nc.sbuf_tensor("v_sb", [P, NKT * 512], BF16))
        mask_sb = E(nc.sbuf_tensor("mask_sb", [P, NKT * HALF], BF16))
        consts = E(nc.sbuf_tensor("consts_sb", [P, 4], F32))
        ones_bf = E(nc.sbuf_tensor("ones_bf", [P, P], BF16))
        ones1 = E(nc.sbuf_tensor("ones1", [P, P], F32))
        nw_row = E(nc.sbuf_tensor("nw_row", [1, D], F32))
        NW = E(nc.sbuf_tensor("NW", [P, D], F32))
        pt = [E(nc.sbuf_tensor(f"pt{i}", [P, HALF], BF16)) for i in range(2)]
        rec = E(nc.sbuf_tensor("rec", [P, HALF], F32))
        y_fm = E(nc.sbuf_tensor("y_fm", [P, HEADS * HALF], BF16))
        wp_sb = [E(nc.sbuf_tensor(f"wp_sb{i}", [P, D], BF16)) for i in range(2)]
        s_sb = [E(nc.sbuf_tensor(f"s_sb{i}", [P, D], BF16)) for i in range(2)]
        scr = E(nc.sbuf_tensor("scr", [P, 512], F32))
        rstats = E(nc.sbuf_tensor("rstats", [P, 8], F32))

        ps_s = [E(nc.psum_tensor(f"ps_s{i}", [P, 512], F32)) for i in range(4)]
        ps_pv = [E(nc.psum_tensor(f"ps_pv{i}", [P, 512], F32)) for i in range(2)]
        ps_d = [E(nc.psum_tensor(f"ps_d{i}", [P, 512], F32)) for i in range(2)]

        pl = Plan(nc, es, ["ld", "ldw", "ve", "se", "ge", "te", "st"])
        block = E(nc.Block())

        q_r = q_sb[:].rearrange("p (h q) -> p h q", q=HALF)
        k_r = k_sb[:].rearrange("p (h s) -> p h s", s=S)
        v_r = v_sb[:].rearrange("p (t c) -> p t c", c=512)
        mask_r = mask_sb[:].rearrange("p (t q) -> p t q", q=HALF)

        for hh in range(HEADS):
            pl.dma("sync", "ld", q_r[:, hh, :], q_d[hh, :, :])
        for kv in range(KV0):
            pl.dma("sync", "ld", k_r[:, kv, :], k_d[kv, :, :])
        for kt in range(NKT):
            pl.dma("sync", "ld", v_r[:, kt, :], v_d[kt * P:(kt + 1) * P, :])
            pl.dma("sync", "ld", mask_r[:, kt, :], mask_d[kt * P:(kt + 1) * P, :])
        pl.dma("sync", "ld", consts[:], consts_d)
        pl.dma("sync", "ld", nw_row[:], nw_d)
        LD_ALL = pl.ctr["ld"].v
        negC_col = consts[:, 0:1]
        eps_col = consts[:, 1:2]

        pl.run("gpsimd", lambda g: g.memset(ones_bf[:], 1.0))
        pl.inc("gpsimd", "ge", lambda g: g.memset(ones1[:], 1.0))
        GE_ONES = pl.ctr["ge"].v

        pl.wait("tensor", "ld", LD_ALL)
        pl.wait("tensor", "ge", GE_ONES)
        for i in range(4):
            if i > 0:
                pl.wait("tensor", "ve")
            pl.inc("tensor", "te", lambda e, i=i: e.matmul(
                ps_pv[0][:], ones1[0:1, :], nw_row[0:1, i * 512:(i + 1) * 512],
                start=True, stop=True))
            te_nw = pl.ctr["te"].v
            pl.wait("vector", "te", te_nw)
            pl.inc("vector", "ve", lambda v, i=i: v.tensor_copy(
                NW[:, i * 512:(i + 1) * 512], ps_pv[0][:]))
        NW_VE = pl.ctr["ve"].v

        exp_se = {}
        mask_ve = {}
        pvden_te = {}
        head_evac_ve = []
        pl.wait("scalar", "ld", LD_ALL)
        for h in range(HEADS):
            kv = h // 4
            if h == 0:
                pl.wait("tensor", "ve", NW_VE)
            else:
                pl.wait("tensor", "ve", head_evac_ve[h - 1])
            for kt in range(NKT):
                sl = kt % 2
                prev = (h, kt - 2) if kt >= 2 else ((h - 1, NKT - 1) if h > 0 else None)
                if prev in exp_se:
                    pl.wait("tensor", "se", exp_se[prev])
                for qc in range(2):
                    pl.inc("tensor", "te", lambda e, h=h, kv=kv, kt=kt, qc=qc, sl=sl:
                           e.matmul(ps_s[sl * 2 + qc][:],
                                    k_r[:, kv, kt * P:(kt + 1) * P],
                                    q_r[:, h, qc * 512:(qc + 1) * 512],
                                    start=True, stop=True))
                te_s = pl.ctr["te"].v
                pl.wait("scalar", "te", te_s)
                if prev in pvden_te:
                    pl.wait("scalar", "te", pvden_te[prev])
                for qc in range(2):
                    pl.inc("scalar", "se", lambda s, sl=sl, qc=qc: s.activation(
                        pt[sl][:, qc * 512:(qc + 1) * 512], ps_s[sl * 2 + qc][:],
                        AF.Exp, bias=negC_col, scale=INV_SQRT_DH))
                exp_se[(h, kt)] = pl.ctr["se"].v
                pl.wait("vector", "se", exp_se[(h, kt)])
                for qc in range(2):
                    pl.inc("vector", "ve", lambda v, sl=sl, kt=kt, qc=qc:
                           v.tensor_mul(pt[sl][:, qc * 512:(qc + 1) * 512],
                                        pt[sl][:, qc * 512:(qc + 1) * 512],
                                        mask_r[:, kt, qc * 512:(qc + 1) * 512]))
                mask_ve[(h, kt)] = pl.ctr["ve"].v
                pl.wait("tensor", "ve", mask_ve[(h, kt)])
                for qc in range(2):
                    pl.run("tensor", lambda e, kt=kt, kv=kv, sl=sl, qc=qc, h=h:
                           e.matmul(ps_pv[qc][:],
                                    v_r[:, kt, kv * P:(kv + 1) * P],
                                    pt[sl][:, qc * 512:(qc + 1) * 512],
                                    start=(kt == 0), stop=(kt == NKT - 1)))
                for qc in range(2):
                    pl.inc("tensor", "te", lambda e, kt=kt, sl=sl, qc=qc:
                           e.matmul(ps_d[qc][:],
                                    ones_bf[:],
                                    pt[sl][:, qc * 512:(qc + 1) * 512],
                                    start=(kt == 0), stop=(kt == NKT - 1)))
                pvden_te[(h, kt)] = pl.ctr["te"].v
            pl.wait("vector", "te", pvden_te[(h, NKT - 1)])
            pl.run("vector", lambda v: v.reciprocal(rec[:, 0:512], ps_d[0][:]))
            pl.inc("vector", "ve", lambda v: v.reciprocal(
                rec[:, 512:1024], ps_d[1][:]))
            pl.wait("vector", "ve")
            for qc in range(2):
                pl.inc("vector", "ve", lambda v, h=h, qc=qc: v.tensor_mul(
                    y_fm[:, h * HALF + qc * 512: h * HALF + (qc + 1) * 512],
                    ps_pv[qc][:], rec[:, qc * 512:(qc + 1) * 512]))
            head_evac_ve.append(pl.ctr["ve"].v)
            if debug:
                pl.wait("sync", "ve", head_evac_ve[-1])
                pl.dma("sync", "st", y_dbg[h, :, :],
                       y_fm[:, h * HALF:(h + 1) * HALF])
                pl.dma("sync", "st", d_dbg[h, :, :], rec[:])
                pl.wait("vector", "st", pl.ctr["st"].v)
        Y_ALL_VE = head_evac_ve[-1]

        y_r = y_fm[:].rearrange("p (h q) -> p h q", q=HALF)

        s_st = []
        epi_ve = {}
        wp_use_te = {}
        for pair in range(NT // 2):
            t0, t1 = pair * 2, pair * 2 + 1
            ps_o = {t0: [ps_s[0], ps_s[1], ps_s[2], ps_s[3]],
                    t1: [ps_pv[0], ps_pv[1], ps_d[0], ps_d[1]]}
            if pair == 0:
                pl.wait("tensor", "ve", Y_ALL_VE)
            else:
                pl.wait("tensor", "ve", epi_ve[t0 - 1])
            for c in range(NC):
                if (pair, c) != (0, 0) and (pair, c) != (0, 1):
                    key = (pair, c - 2) if c >= 2 else (pair - 1, NC - 2 + c)
                    pl.wait("gpsimd", "te", wp_use_te[key])
                pl.dma("gpsimd", "ldw", wp_sb[c % 2][:], wp_d[c * P:(c + 1) * P, :])
                ldw_v = pl.ctr["ldw"].v
                pl.wait("tensor", "ldw", ldw_v)
                for t in (t0, t1):
                    for oc in range(4):
                        last = (t == t1 and oc == 3)
                        if not last:
                            pl.run("tensor", lambda e, c=c, t=t, oc=oc,
                                   o=ps_o[t][oc]: e.matmul(
                                o[:], y_r[:, c, t * P:(t + 1) * P],
                                wp_sb[c % 2][:, oc * 512:(oc + 1) * 512],
                                start=(c == 0), stop=(c == NC - 1)))
                        else:
                            pl.inc("tensor", "te", lambda e, c=c, t=t, oc=oc,
                                   o=ps_o[t][oc]: e.matmul(
                                o[:], y_r[:, c, t * P:(t + 1) * P],
                                wp_sb[c % 2][:, oc * 512:(oc + 1) * 512],
                                start=(c == 0), stop=(c == NC - 1)))
                wp_use_te[(pair, c)] = pl.ctr["te"].v
            te_pair = pl.ctr["te"].v
            for t in (t0, t1):
                pl.wait("scalar", "te", te_pair)
                for oc in range(4):
                    pl.inc("scalar", "se", lambda s, oc=oc, o=ps_o[t][oc]:
                           s.activation(scr[:], o[:], AF.Square,
                                        accum_out=rstats[:, oc:oc + 1]))
                se_sq = pl.ctr["se"].v
                pl.wait("vector", "se", se_sq)
                pl.inc("vector", "ve", lambda v: v.reduce_sum(
                    rstats[:, 4:5], rstats[:, 0:4], axis=AX.X))
                pl.wait("vector", "ve")
                pl.wait("scalar", "ve")
                pl.inc("scalar", "se", lambda s: s.activation(
                    rstats[:, 5:6], rstats[:, 4:5], AF.Ln, bias=eps_col,
                    scale=1.0 / D))
                pl.wait("scalar", "se")
                pl.inc("scalar", "se", lambda s: s.activation(
                    rstats[:, 6:7], rstats[:, 5:6], AF.Exp, scale=-0.5))
                rstd_se = pl.ctr["se"].v
                pl.wait("vector", "se", rstd_se)
                if t >= 2:
                    pl.wait("vector", "st", s_st[t - 2])
                for oc in range(4):
                    pl.inc("vector", "ve", lambda v, oc=oc, t=t, o=ps_o[t][oc]:
                           v.scalar_tensor_tensor(
                               s_sb[t % 2][:, oc * 512:(oc + 1) * 512],
                               o[:], rstats[:, 6:7],
                               NW[:, oc * 512:(oc + 1) * 512],
                               ALU.mult, ALU.mult))
                epi_ve[t] = pl.ctr["ve"].v
                pl.wait("sync", "ve", epi_ve[t])
                pl.dma("sync", "st", s_o[t * P:(t + 1) * P, :], s_sb[t % 2][:])
                s_st.append(pl.ctr["st"].v)

        pl.wait("sync", "st", pl.ctr["st"].v)
        pl.finalize(block)
    return nc


def _prep_B(inputs, outA):
    wp = np.asarray(inputs["L0_attn_w_proj"])
    nw = np.asarray(inputs["L0_attn_norm_w"]).reshape(1, D)
    consts = np.zeros((P, 4), dtype=np.float32)
    consts[:, 0] = -C0
    consts[:, 1] = RMS_EPS
    maps = []
    for c in range(NCORE):
        b, h = divmod(c, 2)
        k_full = np.concatenate([outA[2 * b]["k_o"], outA[2 * b + 1]["k_o"]], axis=2)
        v_full = np.concatenate([outA[2 * b]["v_o"], outA[2 * b + 1]["v_o"]], axis=0)
        kpos = np.arange(S)[:, None]
        qpos = h * HALF + np.arange(HALF)[None, :]
        mask = (kpos <= qpos).astype(BFDT)
        maps.append({
            "q_i": outA[c]["q_o"], "k_i": np.ascontiguousarray(k_full),
            "v_i": np.ascontiguousarray(v_full),
            "wp": wp, "nw": nw, "mask": mask, "consts": consts,
        })
    return maps



def build_C(n_pairs=4, n_experts=8, dump_cmat=False, no_bias=False, simple_combine=False):
    nc = _bass()
    xn_d = nc.dram_tensor("xn_fm", [NC, P, HALF], BF16, kind="ExternalInput").ap()
    x_d = nc.dram_tensor("x_i", [HALF, D], F32, kind="ExternalInput").ap()
    s_d = nc.dram_tensor("s_i", [HALF, D], BF16, kind="ExternalInput").ap()
    wg_d = nc.dram_tensor("wg", [P, P], F32, kind="ExternalInput").ap()
    W_d = nc.dram_tensor("W_t", [8, D, D], F32, kind="ExternalInput").ap()
    b_d = nc.dram_tensor("b_moe", [1, 8 * D], F32, kind="ExternalInput").ap()
    x1_o = nc.dram_tensor("x1_o", [HALF, D], F32, kind="ExternalOutput").ap()

    with contextlib.ExitStack() as es:
        E = es.enter_context
        xn_fm = E(nc.sbuf_tensor("xn_fm_sb", [P, NC * HALF], BF16))
        wg_sb = E(nc.sbuf_tensor("wg_sb", [P, NC * 8], BF16))
        bmoe_t = E(nc.sbuf_tensor("bmoe_sb", [8, 8 * D], BF16))
        ones_t = E(nc.sbuf_tensor("ones_row", [8, P], BF16))
        gex = E(nc.sbuf_tensor("gex", [P, 8], F32))
        gs = E(nc.sbuf_tensor("gs", [P, 4], F32))
        cmat = E(nc.sbuf_tensor("cmat", [P, NT * 8], F32))
        wrow = [E(nc.sbuf_tensor(f"wrow{i}", [P, D], BF16)) for i in range(2)]
        acc = [E(nc.sbuf_tensor(f"acc{i}", [P, D], F32)) for i in range(2)]
        g_sb = [E(nc.sbuf_tensor(f"g_sb{i}", [P, D], BF16)) for i in range(2)]
        tsg = E(nc.sbuf_tensor("tsg", [P, D], BF16))
        xb = [E(nc.sbuf_tensor(f"xb{i}", [P, D], F32)) for i in range(2)]
        ssb = [E(nc.sbuf_tensor(f"ssb{i}", [P, D], BF16)) for i in range(2)]
        x1b = [E(nc.sbuf_tensor(f"x1b{i}", [P, D], F32)) for i in range(2)]

        ps_e = [E(nc.psum_tensor(f"ps_e{i}", [P, 512], F32)) for i in range(8)]

        pl = Plan(nc, es, ["ld", "ldw", "ve", "se", "te", "st", "ge"])
        block = E(nc.Block())

        xn_r = xn_fm[:].rearrange("p (c q) -> p c q", q=HALF)

        for c in range(NC):
            pl.dma("sync", "ld", xn_r[:, c, :], xn_d[c, :, :])
        XN_LD = pl.ctr["ld"].v
        pl.dma("gpsimd", "ldw", wg_sb[:], wg_d)
        bmoe = bmoe_t[0:1, :]
        ones_row = ones_t[0:1, :]
        pl.dma("gpsimd", "ldw", bmoe, b_d)
        CONST_LDW = pl.ctr["ldw"].v
        pl.inc("gpsimd", "ge", lambda g: g.memset(ones_row, 1.0))
        GE_ONES = pl.ctr["ge"].v

        ps_l = ps_e[0][:, 0:8]

        cmat_ve = {}
        pl.wait("tensor", "ld", XN_LD)
        pl.wait("tensor", "ldw", CONST_LDW)
        for t in range(NT):
            if t > 0:
                pl.wait("tensor", "ve", cmat_ve[t - 1])
            for c in range(NC):
                wgr = wg_sb[:].rearrange("p (c e) -> p c e", e=8)
                if c < NC - 1:
                    pl.run("tensor", lambda e, c=c, t=t, w=wgr: e.matmul(
                        ps_l, xn_r[:, c, t * P:(t + 1) * P], w[:, c, :],
                        start=(c == 0), stop=False))
                else:
                    pl.inc("tensor", "te", lambda e, c=c, t=t, w=wgr: e.matmul(
                        ps_l, xn_r[:, c, t * P:(t + 1) * P], w[:, c, :],
                        start=False, stop=True))
            te_l = pl.ctr["te"].v
            pl.wait("vector", "te", te_l)
            pl.inc("vector", "ve", lambda v: v.reduce_max(gs[:, 0:1], ps_l, axis=AX.X))
            pl.wait("vector", "ve")
            pl.inc("vector", "ve", lambda v: v.tensor_scalar(
                gs[:, 1:2], gs[:, 0:1], -1.0, None, ALU.mult))
            ve_m = pl.ctr["ve"].v
            pl.wait("scalar", "ve", ve_m)
            pl.inc("scalar", "se", lambda s: s.activation(
                gex[:], ps_l, AF.Exp, bias=gs[:, 1:2]))
            pl.wait("vector", "se")
            pl.inc("vector", "ve", lambda v: v.reduce_sum(gs[:, 2:3], gex[:], axis=AX.X))
            pl.wait("vector", "ve")
            pl.inc("vector", "ve", lambda v: v.reciprocal(gs[:, 3:4], gs[:, 2:3]))
            pl.wait("vector", "ve")
            pl.inc("vector", "ve", lambda v, t=t: v.tensor_scalar(
                cmat[:, t * 8:(t + 1) * 8], ps_l, gs[:, 0:1], gs[:, 3:4],
                ALU.is_equal, ALU.mult))
            cmat_ve[t] = pl.ctr["ve"].v
        CMAT_ALL = pl.ctr["ve"].v
        if dump_cmat:
            pl.wait("sync", "ve", CMAT_ALL)
            pl.dma("sync", "st", x1_o[0:P, 0:NT * 8], cmat[:])
            pl.wait("sync", "st", pl.ctr["st"].v)
            pl.finalize(block)
            return nc

        comb_ve = {}
        wrow_te = {}
        x1_st = []
        silu_se_last = 0
        for pair in range(n_pairs):
            t0 = pair * 2
            for e in range(n_experts):
                for c in range(NC):
                    idx = e * NC + c
                    if pair > 0 or idx >= 2:
                        key = idx - 2 if idx >= 2 else (pair - 1, 8 * NC - 2 + idx)
                        pl.wait("gpsimd", "te",
                                wrow_te[(pair, idx - 2)] if idx >= 2
                                else wrow_te[(pair - 1, 8 * NC - 2 + idx)])
                    pl.dma("gpsimd", "ldw", wrow[idx % 2][:],
                           W_d[e, c * P:(c + 1) * P, :])
                    ldw_v = pl.ctr["ldw"].v
                    pl.wait("tensor", "ldw", ldw_v)
                    if e == 0 and c == 0:
                        if pair == 0:
                            pl.wait("tensor", "ve", CMAT_ALL)
                        else:
                            pl.wait("tensor", "ve", comb_ve[(pair - 1, n_experts - 1)])
                    elif c == 0:
                        pl.wait("tensor", "ve", comb_ve[(pair, e - 1)])
                    for ti in range(2):
                        t = t0 + ti
                        for oc in range(4):
                            last = (ti == 1 and oc == 3)
                            fn = pl.inc if last else pl.run
                            args = ("tensor", "te") if last else ("tensor",)
                            lam = (lambda ee, c=c, t=t, ti=ti, oc=oc, i=idx % 2,
                                   sp=(no_bias and c == NC - 1):
                                   ee.matmul(ps_e[ti * 4 + oc][:],
                                             xn_r[:, c, t * P:(t + 1) * P],
                                             wrow[i][:, oc * 512:(oc + 1) * 512],
                                             start=(c == 0), stop=sp))
                            if last:
                                pl.inc("tensor", "te", lam)
                            else:
                                pl.run("tensor", lam)
                    wrow_te[(pair, idx)] = pl.ctr["te"].v
                for ti in range(2):
                    for oc in range(4):
                        last = (ti == 1 and oc == 3)
                        if no_bias:
                            lam = (lambda ee: ee.nop())
                        else:
                            lam = (lambda ee, e=e, ti=ti, oc=oc: ee.matmul(
                                ps_e[ti * 4 + oc][:], ones_row,
                                bmoe[0:1, e * D + oc * 512: e * D + (oc + 1) * 512],
                                start=False, stop=True))
                        if last:
                            pl.inc("tensor", "te", lam)
                        else:
                            pl.run("tensor", lam)
                te_e = pl.ctr["te"].v
                pl.wait("vector", "te", te_e)
                if e == 0 and pair > 0:
                    pl.wait("vector", "se", silu_se_last)
                for ti in range(2):
                    t = t0 + ti
                    for oc in range(4):
                        last = (ti == 1 and oc == 3)
                        ccol = cmat[:, t * 8 + e: t * 8 + e + 1]
                        if e == 0 or simple_combine:
                            lam = (lambda v, ti=ti, oc=oc, cc=ccol:
                                   v.tensor_scalar(
                                       acc[ti][:, oc * 512:(oc + 1) * 512],
                                       ps_e[ti * 4 + oc][:], cc, None, ALU.mult))
                        else:
                            lam = (lambda v, ti=ti, oc=oc, cc=ccol:
                                   v.scalar_tensor_tensor(
                                       acc[ti][:, oc * 512:(oc + 1) * 512],
                                       ps_e[ti * 4 + oc][:], cc,
                                       acc[ti][:, oc * 512:(oc + 1) * 512],
                                       ALU.mult, ALU.add))
                        if last:
                            pl.inc("vector", "ve", lam)
                        else:
                            pl.run("vector", lam)
                comb_ve[(pair, e)] = pl.ctr["ve"].v
            for ti in range(2):
                t = t0 + ti
                if pair > 0:
                    pl.wait("sync", "st", x1_st[-2] if len(x1_st) >= 2 else 0)
                pl.dma("sync", "ld", xb[ti][:], x_d[t * P:(t + 1) * P, :])
                pl.dma("sync", "ld", ssb[ti][:], s_d[t * P:(t + 1) * P, :])
                ld_xs = pl.ctr["ld"].v
                pl.wait("scalar", "ve", comb_ve[(pair, n_experts - 1)])
                pl.inc("scalar", "se", lambda s, ti=ti: s.activation(
                    g_sb[ti][:], acc[ti][:], AF.Silu))
                se_g = pl.ctr["se"].v
                silu_se_last = se_g
                pl.wait("vector", "se", se_g)
                pl.wait("vector", "ld", ld_xs)
                pl.inc("vector", "ve", lambda v, ti=ti: v.tensor_mul(
                    tsg[:], ssb[ti][:], g_sb[ti][:]))
                pl.wait("vector", "ve")
                pl.inc("vector", "ve", lambda v, ti=ti: v.tensor_add(
                    x1b[ti][:], xb[ti][:], tsg[:]))
                ve_x1 = pl.ctr["ve"].v
                pl.wait("sync", "ve", ve_x1)
                pl.dma("sync", "st", x1_o[t * P:(t + 1) * P, :], x1b[ti][:])
                x1_st.append(pl.ctr["st"].v)

        pl.wait("sync", "st", pl.ctr["st"].v)
        pl.finalize(block)
    return nc


def _prep_C(inputs, outA, outB):
    x = np.asarray(inputs["L0_attn_x"])
    wg = np.asarray(inputs["L0_ffn_wg"])
    W = np.asarray(inputs["L0_ffn_W"])
    W_t = np.ascontiguousarray(np.transpose(W, (0, 2, 1)))
    b = np.ascontiguousarray(np.asarray(inputs["L0_ffn_b"]).reshape(1, 8 * D))
    maps = []
    for c in range(NCORE):
        b_, h = divmod(c, 2)
        maps.append({
            "xn_fm": outA[c]["xn_fm"],
            "x_i": np.ascontiguousarray(x[b_, h * HALF:(h + 1) * HALF, :]),
            "s_i": outB[c]["s_o"],
            "wg": np.ascontiguousarray(
                wg.reshape(NC, P, 8).transpose(1, 0, 2).reshape(P, P)),
            "W_t": W_t, "b_moe": b,
        })
    return maps



def build_D():
    nc = _bass()
    x1_d = nc.dram_tensor("x1_i", [HALF, D], F32, kind="ExternalInput").ap()
    wq_d = nc.dram_tensor("wq_t", [D, D], F32, kind="ExternalInput").ap()
    wk_d = nc.dram_tensor("wk_t", [D, D], F32, kind="ExternalInput").ap()
    wv_d = nc.dram_tensor("wv_t", [D, D], F32, kind="ExternalInput").ap()
    qnw_d = nc.dram_tensor("qnw", [P, 1], F32, kind="ExternalInput").ap()
    knw_d = nc.dram_tensor("knw", [P, 1], F32, kind="ExternalInput").ap()
    qnwr_d = nc.dram_tensor("qnw_r", [1, P], F32, kind="ExternalInput").ap()
    knwr_d = nc.dram_tensor("knw_r", [1, P], F32, kind="ExternalInput").ap()
    w1s_d = nc.dram_tensor("w1_sh", [D, HALF], F32, kind="ExternalInput").ap()
    w2s_d = nc.dram_tensor("w2_sh", [8192, 256], F32, kind="ExternalInput").ap()
    ropeC_d = nc.dram_tensor("ropeC", [P, HALF], F32, kind="ExternalInput").ap()
    ropeS_d = nc.dram_tensor("ropeS", [P, HALF], F32, kind="ExternalInput").ap()
    ident_d = nc.dram_tensor("ident", [P, P], F32, kind="ExternalInput").ap()
    consts_d = nc.dram_tensor("consts", [P, 4], F32, kind="ExternalInput").ap()
    xn_fm_o = nc.dram_tensor("xn1_fm", [NC, P, HALF], BF16, kind="ExternalOutput").ap()
    q_o = nc.dram_tensor("q1_o", [HEADS, P, HALF], BF16, kind="ExternalOutput").ap()
    k_o = nc.dram_tensor("k1_o", [HEADS, P, HALF], BF16, kind="ExternalOutput").ap()
    v_o = nc.dram_tensor("v1_o", [HALF, D], BF16, kind="ExternalOutput").ap()
    negc_o = nc.dram_tensor("negc_o", [1, 1], F32, kind="ExternalOutput").ap()
    parts_o = nc.dram_tensor("parts_o", [1, 2], F32, kind="ExternalOutput").ap()

    with contextlib.ExitStack() as es:
        E = es.enter_context
        xb = [E(nc.sbuf_tensor(f"xb{i}", [P, D], F32)) for i in range(2)]
        sq_scr = E(nc.sbuf_tensor("sq_scr", [P, D], F32))
        stats = E(nc.sbuf_tensor("stats", [P, 4], F32))
        consts = E(nc.sbuf_tensor("consts_sb", [P, 4], F32))
        xn_tm = E(nc.sbuf_tensor("xn_tm", [P, D], BF16))
        xn_fm = E(nc.sbuf_tensor("xn_fm_sb", [P, NC * HALF], BF16))
        idb = E(nc.sbuf_tensor("idb", [P, P], BF16))
        wbuf = E(nc.sbuf_tensor("wbuf", [P, NC * D], BF16))
        nrm = E(nc.sbuf_tensor("nrm", [P, 2], F32))
        nrow = E(nc.sbuf_tensor("nrow", [8, P], F32))
        ones_c = E(nc.sbuf_tensor("ones_c", [P, 8], F32))
        ones1r = E(nc.sbuf_tensor("ones1r", [8, P], F32))
        ropeC = E(nc.sbuf_tensor("ropeC_sb", [P, HALF], BF16))
        ropeS = E(nc.sbuf_tensor("ropeS_sb", [P, HALF], BF16))
        qf32 = [E(nc.sbuf_tensor(f"qf32_{i}", [P, HALF], F32)) for i in range(2)]
        q2 = E(nc.sbuf_tensor("q2", [P, HALF], F32))
        rln = E(nc.sbuf_tensor("rln", [8, HALF], F32))
        rexp = E(nc.sbuf_tensor("rexp", [8, HALF], F32))
        cstat = E(nc.sbuf_tensor("cstat", [8, 8], F32))
        qn_bf = E(nc.sbuf_tensor("qn_bf", [P, HALF], BF16))
        qout = [E(nc.sbuf_tensor(f"qout{i}", [P, HALF], BF16)) for i in range(2)]
        sw = E(nc.sbuf_tensor("sw", [P, HALF], BF16))
        t1 = E(nc.sbuf_tensor("t1", [P, HALF], BF16))
        t2 = E(nc.sbuf_tensor("t2", [P, HALF], BF16))
        vbuf = [E(nc.sbuf_tensor(f"vbuf{i}", [P, 512], BF16)) for i in range(2)]
        wstat = E(nc.sbuf_tensor("wstat", [P, 96], F32))
        wtile = [E(nc.sbuf_tensor(f"wtile{i}", [P, HALF], F32)) for i in range(2)]

        ps_q = [E(nc.psum_tensor(f"ps_q{i}", [P, 512], F32)) for i in range(4)]
        ps_r = [E(nc.psum_tensor(f"ps_r{i}", [P, 512], F32)) for i in range(2)]
        ps_tr = [E(nc.psum_tensor(f"ps_tr{i}", [P, 512], BF16)) for i in range(2)]

        pl = Plan(nc, es, ["ld_x", "ld_w", "ld_c", "ve", "se", "ge", "te", "st"])
        block = E(nc.Block())

        xn_fm_r = xn_fm[:].rearrange("p (c q) -> p c q", q=HALF)

        pl.dma("sync", "ld_c", consts[:], consts_d)
        pl.dma("sync", "ld_c", nrm[:, 0:1], qnw_d)
        pl.dma("sync", "ld_c", nrm[:, 1:2], knw_d)
        pl.dma("sync", "ld_c", nrow[0:1, :], qnwr_d)
        pl.dma("sync", "ld_c", nrow[1:2, :], knwr_d)
        pl.dma("gpsimd", "ld_c", idb[:], ident_d)
        pl.dma("gpsimd", "ld_c", ropeC[:], ropeC_d)
        pl.dma("gpsimd", "ld_c", ropeS[:], ropeS_d)
        LD_C_ALL = pl.ctr["ld_c"].v
        pl.inc("gpsimd", "ge", lambda g: g.memset(ones_c[:], 1.0))
        GE_ONES = pl.ctr["ge"].v
        eps_ln_col = consts[:, 0:1]
        eps_rms_col = consts[:, 1:2]

        pl.wait("scalar", "ld_c", LD_C_ALL)
        pl.inc("scalar", "se", lambda s: s.activation(
            sq_scr[0:2, 0:P], nrow[0:2, :], AF.Square, accum_out=cstat[0:2, 1:2]))
        pl.wait("scalar", "se")
        pl.inc("scalar", "se", lambda s: s.activation(
            cstat[0:2, 2:3], cstat[0:2, 1:2], AF.Ln))
        SE_LNQ = pl.ctr["se"].v
        pl.wait("tensor", "se", SE_LNQ)
        pl.wait("tensor", "ge", GE_ONES)
        pl.inc("tensor", "te", lambda e: e.matmul(
            ps_r[0][0:1, 2:3], ones_c[0:2, 3:4], cstat[0:2, 2:3],
            start=True, stop=True))
        TE_LNSUM = pl.ctr["te"].v
        pl.wait("scalar", "te", TE_LNSUM)
        pl.inc("scalar", "se", lambda s: s.activation(
            cstat[0:1, 5:6], ps_r[0][0:1, 2:3], AF.Exp, scale=0.5))
        pl.wait("vector", "se")
        pl.inc("vector", "ve", lambda v: v.tensor_scalar(
            cstat[0:1, 6:7], cstat[0:1, 5:6], -8.0 / P, -4.0, ALU.mult, ALU.add))
        NEGC_VE = pl.ctr["ve"].v
        pl.wait("sync", "ve", NEGC_VE)
        pl.dma("sync", "st", negc_o, cstat[0:1, 6:7])

        wabs_se = []
        for i in range(NC):
            pl.dma("gpsimd", "ld_w", wtile[i % 2][:], w1s_d[i * P:(i + 1) * P, :])
            lv = pl.ctr["ld_w"].v
            if i >= 2:
                pl.wait("gpsimd", "se", wabs_se[i - 2])
            pl.wait("scalar", "ld_w", lv)
            pl.inc("scalar", "se", lambda s, i=i: s.activation(
                sq_scr[:, 0:HALF], wtile[i % 2][:], AF.Abs,
                accum_out=wstat[:, i:i + 1]))
            wabs_se.append(pl.ctr["se"].v)
        w2r = w2s_d.rearrange("(n p) c -> n p c", p=P)
        for i in range(64):
            pl.wait("gpsimd", "se", wabs_se[NC + i - 2])
            pl.dma("gpsimd", "ld_w", wtile[i % 2][:, 0:256], w2r[i, :, :])
            lv = pl.ctr["ld_w"].v
            pl.wait("scalar", "ld_w", lv)
            pl.inc("scalar", "se", lambda s, i=i: s.activation(
                sq_scr[:, 0:256], wtile[i % 2][:, 0:256], AF.Abs,
                accum_out=wstat[:, 24 + i:25 + i]))
            wabs_se.append(pl.ctr["se"].v)
        WABS_SE = pl.ctr["se"].v
        pl.wait("vector", "se", WABS_SE)
        pl.run("vector", lambda v: v.reduce_sum(
            wstat[:, 16:17], wstat[:, 0:NC], axis=AX.X))
        pl.inc("vector", "ve", lambda v: v.reduce_sum(
            wstat[:, 88:89], wstat[:, 24:88], axis=AX.X))
        WSUM_VE = pl.ctr["ve"].v
        pl.wait("tensor", "ve", WSUM_VE)
        pl.wait("tensor", "ge", GE_ONES)
        pl.run("tensor", lambda e: e.matmul(
            ps_r[0][0:1, 0:1], ones_c[:, 0:1], wstat[:, 16:17],
            start=True, stop=True))
        pl.inc("tensor", "te", lambda e: e.matmul(
            ps_r[0][0:1, 1:2], ones_c[:, 1:2], wstat[:, 88:89],
            start=True, stop=True))
        WP_TE = pl.ctr["te"].v
        pl.wait("vector", "te", WP_TE)
        pl.inc("vector", "ve", lambda v: v.tensor_copy(
            wstat[0:1, 90:92], ps_r[0][0:1, 0:2]))
        WP_VE = pl.ctr["ve"].v
        pl.wait("sync", "ve", WP_VE)
        pl.dma("sync", "st", parts_o, wstat[0:1, 90:92])

        xn_done_ve = []
        stt_ve_l = []
        tr_batch_te = []
        tr_evac_ve = []
        evac_done_ve = []
        for t in range(NT):
            if t >= 2:
                pl.wait("sync", "ve", xn_done_ve[t - 2])
            pl.dma("sync", "ld_x", xb[t % 2][:], x1_d[t * P:(t + 1) * P, :])
            ldv = pl.ctr["ld_x"].v
            v_xn, v_stt = plan_ln(
                pl, xb[t % 2], stats, sq_scr, xn_tm, eps_ln_col, ("ld_x", ldv),
                stats_war_ve=(stt_ve_l[-1] if stt_ve_l else None),
                xn_war=(("te", tr_batch_te[-1]) if tr_batch_te else None))
            xn_done_ve.append(v_xn)
            stt_ve_l.append(v_stt)
            pl.wait("tensor", "ve", v_xn)
            if t == 0:
                pl.wait("tensor", "ld_c", LD_C_ALL)
            for cb in range(4):
                j = (t * 4 + cb) % 2
                if t * 4 + cb >= 2:
                    pl.wait("tensor", "ve", tr_evac_ve[(t * 4 + cb) - 2])
                for cc in range(4):
                    c = cb * 4 + cc
                    if cc < 3:
                        pl.run("tensor", lambda e, j=j, cc=cc, c=c: e.transpose(
                            ps_tr[j][:, cc * P:(cc + 1) * P],
                            xn_tm[:, c * P:(c + 1) * P], idb[:]))
                    else:
                        pl.inc("tensor", "te", lambda e, j=j, cc=cc, c=c: e.transpose(
                            ps_tr[j][:, cc * P:(cc + 1) * P],
                            xn_tm[:, c * P:(c + 1) * P], idb[:]))
                te_v = pl.ctr["te"].v
                tr_batch_te.append(te_v)
                pl.wait("vector", "te", te_v)
                dst = xn_fm_r[:, cb * 4:(cb + 1) * 4, t * P:(t + 1) * P]
                src_ = ps_tr[j][:].rearrange("p (c q) -> p c q", q=P)
                pl.inc("vector", "ve", lambda v, d=dst, s=src_: v.tensor_copy(d, s))
                tr_evac_ve.append(pl.ctr["ve"].v)
            evac_done_ve.append(pl.ctr["ve"].v)
        XN_ALL_VE = evac_done_ve[-1]
        pl.wait("gpsimd", "ve", XN_ALL_VE)
        for c in range(NC):
            pl.dma("gpsimd", "st", xn_fm_o[c, :, :], xn_fm[:, c * HALF:(c + 1) * HALF])

        def load_w(which_d, done_te_of_prev_phase):
            if done_te_of_prev_phase is not None:
                pl.wait("gpsimd", "te", done_te_of_prev_phase)
            vals = []
            for r in range(NC):
                pl.dma("gpsimd", "ld_w", wbuf[:, r * D:(r + 1) * D],
                       which_d[r * P:(r + 1) * P, :])
                vals.append(pl.ctr["ld_w"].v)
            return vals

        wq_lds = load_w(wq_d, None)

        def qk_heads(nw_col, lds, out_dram):
            evac_ve = []
            out_ve = []
            st_l = []
            ge_sw_l = []
            prev_qn_ve = None
            for hd in range(HEADS):
                sl = hd % 2
                if hd == 0:
                    pl.wait("tensor", "ld_w", lds[NC - 1])
                    pl.wait("tensor", "ve", XN_ALL_VE)
                if hd >= 2:
                    pl.wait("tensor", "ve", evac_ve[hd - 2])
                for qc in range(2):
                    for c in range(NC):
                        lhsT = wbuf[:, c * D + hd * P: c * D + (hd + 1) * P]
                        rhs = xn_fm_r[:, c, qc * 512:(qc + 1) * 512]
                        if c < NC - 1:
                            pl.run("tensor", lambda e, o=ps_q[sl * 2 + qc][:],
                                   l=lhsT, r=rhs, st_=(c == 0):
                                   e.matmul(o, l, r, start=st_, stop=False))
                        else:
                            pl.inc("tensor", "te", lambda e, o=ps_q[sl * 2 + qc][:],
                                   l=lhsT, r=rhs:
                                   e.matmul(o, l, r, start=False, stop=True))
                te_head = pl.ctr["te"].v
                pl.wait("vector", "te", te_head)
                if hd >= 2:
                    pl.wait("vector", "ve", out_ve[hd - 2])
                    pl.wait("vector", "ge", ge_sw_l[hd - 2])
                for qc in range(2):
                    lam = (lambda v, sl=sl, qc=qc: v.tensor_copy(
                        qf32[sl][:, qc * 512:(qc + 1) * 512], ps_q[sl * 2 + qc][:]))
                    if qc == 1:
                        pl.inc("vector", "ve", lam)
                    else:
                        pl.run("vector", lam)
                ve_q0 = pl.ctr["ve"].v
                evac_ve.append(ve_q0)
                pl.wait("scalar", "ve", ve_q0)
                pl.inc("scalar", "se", lambda s, sl=sl: s.activation(
                    q2[:], qf32[sl][:], AF.Square))
                se_q2 = pl.ctr["se"].v
                pl.wait("tensor", "se", se_q2)
                if prev_qn_ve is not None:
                    pl.wait("tensor", "ve", prev_qn_ve)
                for i in range(2):
                    lam = (lambda e, i=i: e.matmul(
                        ps_r[i][0:1, :], ones_c[:, 2:3], q2[:, i * 512:(i + 1) * 512],
                        start=True, stop=True))
                    if i == 1:
                        pl.inc("tensor", "te", lam)
                    else:
                        pl.run("tensor", lam)
                te_ssq = pl.ctr["te"].v
                pl.wait("scalar", "te", te_ssq)
                for i in range(2):
                    lam = (lambda s, i=i: s.activation(
                        rln[0:1, i * 512:(i + 1) * 512], ps_r[i][0:1, :],
                        AF.Ln, bias=eps_rms_col[0:1], scale=1.0 / P))
                    if i == 1:
                        pl.inc("scalar", "se", lam)
                    else:
                        pl.run("scalar", lam)
                pl.wait("scalar", "se")
                for i in range(2):
                    lam = (lambda s, i=i: s.activation(
                        rexp[0:1, i * 512:(i + 1) * 512],
                        rln[0:1, i * 512:(i + 1) * 512], AF.Exp, scale=-0.5))
                    if i == 1:
                        pl.inc("scalar", "se", lam)
                    else:
                        pl.run("scalar", lam)
                se_rstd = pl.ctr["se"].v
                pl.wait("tensor", "se", se_rstd)
                for i in range(2):
                    lam = (lambda e, i=i: e.matmul(
                        ps_r[i][:], ones1r[0:1, :],
                        rexp[0:1, i * 512:(i + 1) * 512],
                        start=True, stop=True))
                    if i == 1:
                        pl.inc("tensor", "te", lam)
                    else:
                        pl.run("tensor", lam)
                te_bc = pl.ctr["te"].v
                pl.wait("vector", "te", te_bc)
                if ge_sw_l:
                    pl.wait("vector", "ge", ge_sw_l[-1])
                for i in range(2):
                    lam = (lambda v, sl=sl, i=i: v.scalar_tensor_tensor(
                        qn_bf[:, i * 512:(i + 1) * 512],
                        qf32[sl][:, i * 512:(i + 1) * 512], nw_col, ps_r[i][:],
                        ALU.mult, ALU.mult))
                    if i == 1:
                        pl.inc("vector", "ve", lam)
                    else:
                        pl.run("vector", lam)
                ve_qn = pl.ctr["ve"].v
                prev_qn_ve = ve_qn
                if len(st_l) >= 2:
                    pl.wait("vector", "st", st_l[-2])
                    pl.wait("gpsimd", "st", st_l[-2])
                v_out, ge_sw = plan_rope(pl, qn_bf, qout[sl], sw, t1, t2,
                                         ropeC, ropeS, ve_qn,
                                         t2_war_ve=(out_ve[-1] if out_ve else None))
                ge_sw_l.append(ge_sw)
                out_ve.append(v_out)
                pl.wait("sync", "ve", v_out)
                pl.dma("sync", "st", out_dram[hd, :, :], qout[sl][:])
                st_l.append(pl.ctr["st"].v)
            return pl.ctr["te"].v

        pl.inc("gpsimd", "ge", lambda g: g.memset(ones1r[0:1, :], 1.0))
        GE_ONES2 = pl.ctr["ge"].v
        pl.wait("tensor", "ge", GE_ONES2)

        te_q = qk_heads(nrm[:, 0:1], wq_lds, q_o)
        wk_lds = load_w(wk_d, te_q)
        te_k = qk_heads(nrm[:, 1:2], wk_lds, k_o)
        wv_lds = load_w(wv_d, te_k)

        v_evac_ve = []
        v_st = []
        for t in range(NT):
            for oc in range(4):
                if t == 0 and oc == 0:
                    pl.wait("tensor", "ld_w", wv_lds[NC - 1])
                sl = (t * 4 + oc) % 2
                if t * 4 + oc >= 2:
                    pl.wait("tensor", "ve", v_evac_ve[t * 4 + oc - 2])
                for c in range(NC):
                    lhsT = xn_fm_r[:, c, t * P:(t + 1) * P]
                    rhs = wbuf[:, c * D + oc * 512: c * D + (oc + 1) * 512]
                    if c < NC - 1:
                        pl.run("tensor", lambda e, o=ps_q[sl][:], l=lhsT, r=rhs,
                               st_=(c == 0): e.matmul(o, l, r, start=st_, stop=False))
                    else:
                        pl.inc("tensor", "te", lambda e, o=ps_q[sl][:], l=lhsT, r=rhs:
                               e.matmul(o, l, r, start=False, stop=True))
                te_v = pl.ctr["te"].v
                pl.wait("vector", "te", te_v)
                if t * 4 + oc >= 2:
                    pl.wait("vector", "st", v_st[t * 4 + oc - 2])
                pl.inc("vector", "ve", lambda v, sl=sl: v.tensor_copy(
                    vbuf[sl][:], ps_q[sl][:]))
                v_evac_ve.append(pl.ctr["ve"].v)
                pl.wait("sync", "ve", v_evac_ve[-1])
                pl.dma("sync", "st",
                       v_o[t * P:(t + 1) * P, oc * 512:(oc + 1) * 512], vbuf[sl][:])
                v_st.append(pl.ctr["st"].v)

        pl.wait("sync", "st", pl.ctr["st"].v)
        pl.finalize(block)
    return nc


def _prep_D(inputs, outC):
    wq_t = np.ascontiguousarray(np.asarray(inputs["L1_attn_Wq"]).T)
    wk_t = np.ascontiguousarray(np.asarray(inputs["L1_attn_Wk"]).T)
    wv_t = np.ascontiguousarray(np.asarray(inputs["L1_attn_Wv"]).T)
    w1_t = np.ascontiguousarray(np.asarray(inputs["L1_ffn_w1"]).T)
    w2_t = np.ascontiguousarray(np.asarray(inputs["L1_ffn_w2"]).T)
    qnw = np.asarray(inputs["L1_attn_q_norm_w"]).reshape(P, 1)
    knw = np.asarray(inputs["L1_attn_k_norm_w"]).reshape(P, 1)
    ident = np.eye(P, dtype=np.float32)
    consts = np.zeros((P, 4), dtype=np.float32)
    consts[:, 0] = LN_EPS
    consts[:, 1] = RMS_EPS
    maps = []
    for c in range(NCORE):
        b, h = divmod(c, 2)
        Cf, Sf = _rope_tables(h)
        maps.append({
            "x1_i": outC[c]["x1_o"],
            "wq_t": wq_t, "wk_t": wk_t, "wv_t": wv_t,
            "qnw": np.ascontiguousarray(qnw), "knw": np.ascontiguousarray(knw),
            "qnw_r": np.ascontiguousarray(qnw.reshape(1, P)),
            "knw_r": np.ascontiguousarray(knw.reshape(1, P)),
            "w1_sh": np.ascontiguousarray(w1_t[:, c * HALF:(c + 1) * HALF]),
            "w2_sh": np.ascontiguousarray(w2_t[:, c * 256:(c + 1) * 256]),
            "ropeC": Cf, "ropeS": Sf, "ident": ident, "consts": consts,
        })
    return maps, w1_t, w2_t



def build_E():
    nc = _bass()
    q_d = nc.dram_tensor("q_i", [HEADS, P, HALF], BF16, kind="ExternalInput").ap()
    k_d = nc.dram_tensor("k_i", [HEADS, P, S], BF16, kind="ExternalInput").ap()
    v_d = nc.dram_tensor("v_i", [HEADS, S, DH], BF16, kind="ExternalInput").ap()
    wo_d = nc.dram_tensor("wo_t", [D, D], F32, kind="ExternalInput").ap()
    mask_d = nc.dram_tensor("mask", [S, HALF], BF16, kind="ExternalInput").ap()
    negc_d = nc.dram_tensor("negc", [P, 1], F32, kind="ExternalInput").ap()
    s_o = nc.dram_tensor("s_o", [HALF, D], BF16, kind="ExternalOutput").ap()

    with contextlib.ExitStack() as es:
        E = es.enter_context
        q_sb = E(nc.sbuf_tensor("q_sb", [P, HEADS * HALF], BF16))
        k_sb = [E(nc.sbuf_tensor(f"k_sb{i}", [P, S], BF16)) for i in range(2)]
        v_sb = [E(nc.sbuf_tensor(f"v_sb{i}", [P, NKT * DH], BF16)) for i in range(2)]
        mask_sb = E(nc.sbuf_tensor("mask_sb", [P, NKT * HALF], BF16))
        negc = E(nc.sbuf_tensor("negc_sb", [P, 1], F32))
        ones_bf = E(nc.sbuf_tensor("ones_bf", [P, P], BF16))
        pt = [E(nc.sbuf_tensor(f"pt{i}", [P, HALF], BF16)) for i in range(2)]
        rec = E(nc.sbuf_tensor("rec", [P, HALF], F32))
        y_fm = E(nc.sbuf_tensor("y_fm", [P, HEADS * HALF], BF16))
        wo_sb = [E(nc.sbuf_tensor(f"wo_sb{i}", [P, D], BF16)) for i in range(2)]
        s_sb = [E(nc.sbuf_tensor(f"s_sb{i}", [P, D], BF16)) for i in range(2)]

        ps_s = [E(nc.psum_tensor(f"ps_s{i}", [P, 512], F32)) for i in range(4)]
        ps_pv = [E(nc.psum_tensor(f"ps_pv{i}", [P, 512], F32)) for i in range(2)]
        ps_d = [E(nc.psum_tensor(f"ps_d{i}", [P, 512], F32)) for i in range(2)]

        pl = Plan(nc, es, ["ld", "ldk", "ldw", "ve", "se", "ge", "te", "st"])
        block = E(nc.Block())

        q_r = q_sb[:].rearrange("p (h q) -> p h q", q=HALF)
        mask_r = mask_sb[:].rearrange("p (t q) -> p t q", q=HALF)

        for hh in range(HEADS):
            pl.dma("sync", "ld", q_r[:, hh, :], q_d[hh, :, :])
        for kt in range(NKT):
            pl.dma("sync", "ld", mask_r[:, kt, :], mask_d[kt * P:(kt + 1) * P, :])
        pl.dma("sync", "ld", negc[:], negc_d)
        LD_ALL = pl.ctr["ld"].v
        pl.inc("gpsimd", "ge", lambda g: g.memset(ones_bf[:], 1.0))
        GE_ONES = pl.ctr["ge"].v
        negC_col = negc[:, 0:1]

        kv_ld = []
        def plan_kv_load(h):
            pl.dma("gpsimd", "ldk", k_sb[h % 2][:], k_d[h, :, :])
            pl.dma("gpsimd", "ldk", v_sb[h % 2][:].rearrange("p (t c) -> p t c", c=DH),
                   v_d[h, :, :].rearrange("(t p) c -> p t c", p=P))
            kv_ld.append(pl.ctr["ldk"].v)
        plan_kv_load(0)
        plan_kv_load(1)

        exp_se = {}
        mask_ve = {}
        pvden_te = {}
        head_evac_ve = []
        pl.wait("scalar", "ld", LD_ALL)
        pl.wait("tensor", "ge", GE_ONES)
        for h in range(HEADS):
            sl_h = h % 2
            k_r = k_sb[sl_h]
            v_r = v_sb[sl_h][:].rearrange("p (t c) -> p t c", c=DH)
            pl.wait("tensor", "ldk", kv_ld[h])
            if h == 0:
                pl.wait("tensor", "ld", LD_ALL)
            else:
                pl.wait("tensor", "ve", head_evac_ve[h - 1])
            for kt in range(NKT):
                sl = kt % 2
                prev = (h, kt - 2) if kt >= 2 else ((h - 1, NKT - 1) if h > 0 else None)
                if prev in exp_se:
                    pl.wait("tensor", "se", exp_se[prev])
                for qc in range(2):
                    pl.inc("tensor", "te", lambda e, h=h, kt=kt, qc=qc, sl=sl, k_r=k_r:
                           e.matmul(ps_s[sl * 2 + qc][:],
                                    k_r[:, kt * P:(kt + 1) * P],
                                    q_r[:, h, qc * 512:(qc + 1) * 512],
                                    start=True, stop=True))
                te_s = pl.ctr["te"].v
                pl.wait("scalar", "te", te_s)
                if prev in pvden_te:
                    pl.wait("scalar", "te", pvden_te[prev])
                for qc in range(2):
                    pl.inc("scalar", "se", lambda s, sl=sl, qc=qc: s.activation(
                        pt[sl][:, qc * 512:(qc + 1) * 512], ps_s[sl * 2 + qc][:],
                        AF.Exp, bias=negC_col, scale=INV_SQRT_DH))
                exp_se[(h, kt)] = pl.ctr["se"].v
                pl.wait("vector", "se", exp_se[(h, kt)])
                for qc in range(2):
                    pl.inc("vector", "ve", lambda v, sl=sl, kt=kt, qc=qc:
                           v.tensor_mul(pt[sl][:, qc * 512:(qc + 1) * 512],
                                        pt[sl][:, qc * 512:(qc + 1) * 512],
                                        mask_r[:, kt, qc * 512:(qc + 1) * 512]))
                mask_ve[(h, kt)] = pl.ctr["ve"].v
                pl.wait("tensor", "ve", mask_ve[(h, kt)])
                for qc in range(2):
                    pl.run("tensor", lambda e, kt=kt, sl=sl, qc=qc, v_r=v_r:
                           e.matmul(ps_pv[qc][:],
                                    v_r[:, kt, :],
                                    pt[sl][:, qc * 512:(qc + 1) * 512],
                                    start=(kt == 0), stop=(kt == NKT - 1)))
                for qc in range(2):
                    pl.inc("tensor", "te", lambda e, kt=kt, sl=sl, qc=qc:
                           e.matmul(ps_d[qc][:],
                                    ones_bf[:],
                                    pt[sl][:, qc * 512:(qc + 1) * 512],
                                    start=(kt == 0), stop=(kt == NKT - 1)))
                pvden_te[(h, kt)] = pl.ctr["te"].v
            if h + 2 < HEADS:
                pl.wait("gpsimd", "te", pvden_te[(h, NKT - 1)])
                plan_kv_load(h + 2)
            pl.wait("vector", "te", pvden_te[(h, NKT - 1)])
            pl.run("vector", lambda v: v.reciprocal(rec[:, 0:512], ps_d[0][:]))
            pl.inc("vector", "ve", lambda v: v.reciprocal(
                rec[:, 512:1024], ps_d[1][:]))
            pl.wait("vector", "ve")
            for qc in range(2):
                pl.inc("vector", "ve", lambda v, h=h, qc=qc: v.tensor_mul(
                    y_fm[:, h * HALF + qc * 512: h * HALF + (qc + 1) * 512],
                    ps_pv[qc][:], rec[:, qc * 512:(qc + 1) * 512]))
            head_evac_ve.append(pl.ctr["ve"].v)
        Y_ALL_VE = head_evac_ve[-1]

        y_r = y_fm[:].rearrange("p (h q) -> p h q", q=HALF)

        s_st = []
        epi_ve = {}
        wo_use_te = {}
        for pair in range(NT // 2):
            t0, t1 = pair * 2, pair * 2 + 1
            ps_o = {t0: [ps_s[0], ps_s[1], ps_s[2], ps_s[3]],
                    t1: [ps_pv[0], ps_pv[1], ps_d[0], ps_d[1]]}
            if pair == 0:
                pl.wait("tensor", "ve", Y_ALL_VE)
            else:
                pl.wait("tensor", "ve", epi_ve[t0 - 1])
            for c in range(NC):
                if (pair, c) != (0, 0) and (pair, c) != (0, 1):
                    key = (pair, c - 2) if c >= 2 else (pair - 1, NC - 2 + c)
                    pl.wait("gpsimd", "te", wo_use_te[key])
                pl.dma("gpsimd", "ldw", wo_sb[c % 2][:], wo_d[c * P:(c + 1) * P, :])
                ldw_v = pl.ctr["ldw"].v
                pl.wait("tensor", "ldw", ldw_v)
                for t in (t0, t1):
                    for oc in range(4):
                        last = (t == t1 and oc == 3)
                        lam = (lambda e, c=c, t=t, oc=oc, o=ps_o[t][oc]: e.matmul(
                            o[:], y_r[:, c, t * P:(t + 1) * P],
                            wo_sb[c % 2][:, oc * 512:(oc + 1) * 512],
                            start=(c == 0), stop=(c == NC - 1)))
                        if last:
                            pl.inc("tensor", "te", lam)
                        else:
                            pl.run("tensor", lam)
                wo_use_te[(pair, c)] = pl.ctr["te"].v
            te_pair = pl.ctr["te"].v
            for t in (t0, t1):
                pl.wait("vector", "te", te_pair)
                if t >= 2:
                    pl.wait("vector", "st", s_st[t - 2])
                for oc in range(4):
                    lam = (lambda v, oc=oc, t=t, o=ps_o[t][oc]: v.tensor_copy(
                        s_sb[t % 2][:, oc * 512:(oc + 1) * 512], o[:]))
                    if oc == 3:
                        pl.inc("vector", "ve", lam)
                    else:
                        pl.run("vector", lam)
                epi_ve[t] = pl.ctr["ve"].v
                pl.wait("sync", "ve", epi_ve[t])
                pl.dma("sync", "st", s_o[t * P:(t + 1) * P, :], s_sb[t % 2][:])
                s_st.append(pl.ctr["st"].v)

        pl.wait("sync", "st", pl.ctr["st"].v)
        pl.finalize(block)
    return nc


def _prep_E(inputs, outD):
    wo_t = np.ascontiguousarray(np.asarray(inputs["L1_attn_Wo"]).T)
    maps = []
    for c in range(NCORE):
        b, h = divmod(c, 2)
        k_full = np.concatenate([outD[2 * b]["k1_o"], outD[2 * b + 1]["k1_o"]], axis=2)
        v_full = np.concatenate([outD[2 * b]["v1_o"], outD[2 * b + 1]["v1_o"]], axis=0)
        v_heads = np.ascontiguousarray(
            v_full.reshape(S, HEADS, DH).transpose(1, 0, 2))
        kpos = np.arange(S)[:, None]
        qpos = h * HALF + np.arange(HALF)[None, :]
        mask = ((kpos <= qpos) & (kpos > qpos - 1024)).astype(BFDT)
        negc = np.ascontiguousarray(
            np.repeat(np.asarray(outD[c]["negc_o"]), P, axis=0))
        maps.append({
            "q_i": outD[c]["q1_o"], "k_i": np.ascontiguousarray(k_full),
            "v_i": v_heads, "wo_t": wo_t, "mask": mask, "negc": negc,
        })
    return maps



def build_F1():
    nc = _bass()
    xn_d = nc.dram_tensor("xn1_fm", [NC, P, HALF], BF16, kind="ExternalInput").ap()
    w1_d = nc.dram_tensor("w1_t", [D, 8192], F32, kind="ExternalInput").ap()
    w2_d = nc.dram_tensor("w2_t", [8192, D], F32, kind="ExternalInput").ap()
    b1_d = nc.dram_tensor("b1_r", [P, 64], F32, kind="ExternalInput").ap()
    parts_d = nc.dram_tensor("parts", [8, 2], F32, kind="ExternalInput").ap()
    h_o = nc.dram_tensor("h_o", [8192, HALF], BF16, kind="ExternalOutput").ap()
    w2q_o = nc.dram_tensor("w2q_o", [8192, D], BF16, kind="ExternalOutput").ap()

    NW1 = np.float32(0.7 / (8192.0 * 2048.0))

    with contextlib.ExitStack() as es:
        E = es.enter_context
        xn_fm = E(nc.sbuf_tensor("xn_fm_sb", [P, NC * HALF], BF16))
        w1q = E(nc.sbuf_tensor("w1q", [P, NC * D], BF16))
        wslab = [E(nc.sbuf_tensor(f"wslab{i}", [P, D], F32)) for i in range(2)]
        abuf = E(nc.sbuf_tensor("abuf", [P, D], BF16))
        w2qb = [E(nc.sbuf_tensor(f"w2qb{i}", [P, D], BF16)) for i in range(2)]
        b1_sb = E(nc.sbuf_tensor("b1_sb", [P, 64], F32))
        parts_sb = E(nc.sbuf_tensor("parts_sb", [8, 2], F32))
        qstat = E(nc.sbuf_tensor("qstat", [P, 4], F32))
        ones_c = E(nc.sbuf_tensor("ones_c", [P, 8], F32))
        ones_cb = E(nc.sbuf_tensor("ones_cb", [P, 8], BF16))
        ones_rb = E(nc.sbuf_tensor("ones_rb", [8, P], BF16))
        ones_rf = E(nc.sbuf_tensor("ones_rf", [8, P], F32))
        nxs = E(nc.sbuf_tensor("nxs", [8, HALF], BF16))
        h_sb = [E(nc.sbuf_tensor(f"h_sb{i}", [P, HALF], BF16)) for i in range(2)]

        ps_h = [E(nc.psum_tensor(f"ps_h{i}", [P, 512], F32)) for i in range(4)]
        ps_x = [E(nc.psum_tensor(f"ps_x{i}", [P, 512], F32)) for i in range(2)]

        pl = Plan(nc, es, ["ld", "ldw", "ve", "se", "ge", "te", "st"])
        block = E(nc.Block())

        xn_r = xn_fm[:].rearrange("p (c q) -> p c q", q=HALF)

        for c in range(NC):
            pl.dma("sync", "ld", xn_r[:, c, :], xn_d[c, :, :])
        pl.dma("sync", "ld", b1_sb[:], b1_d)
        pl.dma("sync", "ld", parts_sb[:], parts_d)
        LD_ALL = pl.ctr["ld"].v
        pl.run("gpsimd", lambda g: g.memset(ones_c[:], 1.0))
        pl.run("gpsimd", lambda g: g.memset(ones_cb[:], 1.0))
        pl.run("gpsimd", lambda g: g.memset(ones_rf[0:1, :], 1.0))
        pl.inc("gpsimd", "ge", lambda g: g.memset(ones_rb[0:1, :], 1.0))
        GE_ONES = pl.ctr["ge"].v

        pl.wait("tensor", "ld", LD_ALL)
        pl.wait("tensor", "ge", GE_ONES)
        pl.inc("tensor", "te", lambda e: e.matmul(
            ps_x[0][0:1, 0:2], ones_c[0:8, 0:1], parts_sb[:],
            start=True, stop=True))
        TE_TOT = pl.ctr["te"].v
        pl.wait("vector", "te", TE_TOT)
        pl.inc("vector", "ve", lambda v: v.tensor_scalar(
            qstat[0:1, 0:2], ps_x[0][0:1, 0:2], float(NW1), None, ALU.mult))
        pl.wait("vector", "ve")
        VE_THR_ROW = pl.ctr["ve"].v
        pl.wait("tensor", "ve", VE_THR_ROW)
        pl.inc("tensor", "te", lambda e: e.matmul(
            ps_x[1][:, 0:2], ones_rf[0:1, :], qstat[0:1, 0:2],
            start=True, stop=True))
        TE_BC = pl.ctr["te"].v
        pl.wait("vector", "te", TE_BC)
        pl.run("vector", lambda v: v.tensor_copy(qstat[:, 0:2], ps_x[1][:, 0:2]))
        pl.inc("vector", "ve", lambda v: v.tensor_scalar(
            qstat[:, 2:4], ps_x[1][:, 0:2], -1.0, None, ALU.mult))
        VE_THR = pl.ctr["ve"].v

        thr1 = qstat[:, 0:1]
        nthr1 = qstat[:, 2:3]
        h_st = []
        quant_ve = {}
        ft_te = {}
        gelu_se = {}
        for g in range(4):
            for kc in range(NC):
                idx = g * NC + kc
                if idx >= 2:
                    pl.wait("gpsimd", "ve", quant_ve[idx - 2])
                pl.dma("gpsimd", "ldw", wslab[idx % 2][:],
                       w1_d[kc * P:(kc + 1) * P, g * D:(g + 1) * D])
                lv = pl.ctr["ldw"].v
                pl.wait("vector", "ldw", lv)
                if idx == 0:
                    pl.wait("vector", "ve", VE_THR)
                if g > 0 and kc == 0:
                    pl.wait("vector", "te", ft_te[(g - 1, NC - 1)])
                pl.run("vector", lambda v, i=idx % 2: v.tensor_scalar(
                    abuf[:], wslab[i][:], thr1, None, ALU.is_ge))
                pl.inc("vector", "ve", lambda v, i=idx % 2, kc=kc:
                       v.scalar_tensor_tensor(
                           w1q[:, kc * D:(kc + 1) * D], wslab[i][:], nthr1,
                           abuf[:], ALU.is_le, ALU.subtract))
                quant_ve[idx] = pl.ctr["ve"].v
            for ft in range(NC):
                gf = g * NC + ft
                sl = ft % 2
                if gf >= 2:
                    pl.wait("tensor", "se", gelu_se[gf - 2])
                if ft == 0:
                    pl.wait("tensor", "ve", quant_ve[g * NC + NC - 1])
                for qc in range(2):
                    for kc in range(NC):
                        last = (qc == 1 and kc == NC - 1)
                        lam = (lambda e, kc=kc, ft=ft, qc=qc, sl=sl: e.matmul(
                            ps_h[sl * 2 + qc][:],
                            w1q[:, kc * D + ft * P: kc * D + (ft + 1) * P],
                            xn_r[:, kc, qc * 512:(qc + 1) * 512],
                            start=(kc == 0), stop=(kc == NC - 1)))
                        if last:
                            pl.inc("tensor", "te", lam)
                        else:
                            pl.run("tensor", lam)
                ft_te[(g, ft)] = pl.ctr["te"].v
                pl.wait("scalar", "te", ft_te[(g, ft)])
                if gf >= 2:
                    pl.wait("scalar", "st", h_st[gf - 2])
                for qc in range(2):
                    lam = (lambda s, qc=qc, sl=sl, gf=gf: s.activation(
                        h_sb[sl][:, qc * 512:(qc + 1) * 512], ps_h[sl * 2 + qc][:],
                        AF.Gelu, bias=b1_sb[:, gf:gf + 1], scale=-1.0))
                    if qc == 1:
                        pl.inc("scalar", "se", lam)
                    else:
                        pl.run("scalar", lam)
                gelu_se[gf] = pl.ctr["se"].v
                pl.wait("sync", "se", gelu_se[gf])
                pl.dma("sync", "st", h_o[gf * P:(gf + 1) * P, :], h_sb[sl][:])
                h_st.append(pl.ctr["st"].v)

        thr2 = qstat[:, 1:2]
        nthr2 = qstat[:, 3:4]
        w2_st = []
        w2q_ve = {}
        for fc in range(64):
            pl.wait("gpsimd", "ve", quant_ve[4 * NC - 2 + fc] if fc < 2 else
                    w2q_ve[fc - 2])
            pl.dma("gpsimd", "ldw", wslab[fc % 2][:],
                   w2_d[fc * P:(fc + 1) * P, :])
            lv = pl.ctr["ldw"].v
            pl.wait("vector", "ldw", lv)
            if fc >= 2:
                pl.wait("vector", "st", w2_st[fc - 2])
            pl.run("vector", lambda v, i=fc % 2: v.tensor_scalar(
                abuf[:], wslab[i][:], thr2, None, ALU.is_ge))
            pl.inc("vector", "ve", lambda v, i=fc % 2: v.scalar_tensor_tensor(
                w2qb[i][:], wslab[i][:], nthr2, abuf[:], ALU.is_le, ALU.subtract))
            w2q_ve[fc] = pl.ctr["ve"].v
            pl.wait("sync", "ve", w2q_ve[fc])
            pl.dma("sync", "st", w2q_o[fc * P:(fc + 1) * P, :], w2qb[fc % 2][:])
            w2_st.append(pl.ctr["st"].v)

        pl.wait("sync", "st", pl.ctr["st"].v)
        pl.finalize(block)
    return nc



def build_F2():
    nc = _bass()
    h_d = nc.dram_tensor("h_i", [8192, HALF], BF16, kind="ExternalInput").ap()
    w2q_d = nc.dram_tensor("w2q", [8192, D], BF16, kind="ExternalInput").ap()
    b2_d = nc.dram_tensor("b2_r", [1, D], F32, kind="ExternalInput").ap()
    x1_d = nc.dram_tensor("x1_i", [HALF, D], F32, kind="ExternalInput").ap()
    s1_d = nc.dram_tensor("s1_i", [HALF, D], BF16, kind="ExternalInput").ap()
    out_o = nc.dram_tensor("out_o", [HALF, D], F32, kind="ExternalOutput").ap()

    with contextlib.ExitStack() as es:
        E = es.enter_context
        h_sb = E(nc.sbuf_tensor("h_sb", [P, 64 * HALF], BF16))
        w2qb = [E(nc.sbuf_tensor(f"w2qb{i}", [P, D], BF16)) for i in range(2)]
        b2_sb = E(nc.sbuf_tensor("b2_sb", [8, D], BF16))
        nhs = E(nc.sbuf_tensor("nhs", [8, HALF], BF16))
        ones_cb = E(nc.sbuf_tensor("ones_cb", [P, 8], BF16))
        mones_rb = E(nc.sbuf_tensor("mones_rb", [8, P], BF16))
        g_sb = [E(nc.sbuf_tensor(f"g_sb{i}", [P, D], BF16)) for i in range(2)]
        tsg = E(nc.sbuf_tensor("tsg", [P, D], BF16))
        xb = [E(nc.sbuf_tensor(f"xb{i}", [P, D], F32)) for i in range(2)]
        ssb = [E(nc.sbuf_tensor(f"ssb{i}", [P, D], BF16)) for i in range(2)]
        x2b = [E(nc.sbuf_tensor(f"x2b{i}", [P, D], F32)) for i in range(2)]

        ps = [E(nc.psum_tensor(f"ps{i}", [P, 512], F32)) for i in range(8)]

        pl = Plan(nc, es, ["ld", "ldw", "ve", "se", "ge", "te", "st"])
        block = E(nc.Block())

        h_r = h_sb[:].rearrange("p (c q) -> p c q", q=HALF)

        for c in range(64):
            pl.dma("sync", "ld", h_r[:, c, :], h_d[c * P:(c + 1) * P, :])
        H_LD = pl.ctr["ld"].v
        pl.dma("gpsimd", "ld", b2_sb[0:1, :], b2_d)
        B2_LD = pl.ctr["ld"].v
        pl.run("gpsimd", lambda g: g.memset(ones_cb[:], 1.0))
        pl.inc("gpsimd", "ge", lambda g: g.memset(mones_rb[0:1, :], -1.0))
        GE_ONES = pl.ctr["ge"].v

        out_st = []
        epi_ve = {}
        w2_use_te = {}
        for pair in range(4):
            t0, t1 = pair * 2, pair * 2 + 1
            if pair == 0:
                pl.wait("tensor", "ld", H_LD)
                pl.wait("tensor", "ge", GE_ONES)
            else:
                pl.wait("tensor", "ve", epi_ve[t0 - 1])
            for fc in range(64):
                if (pair, fc) not in [(0, 0), (0, 1)]:
                    key = (pair, fc - 2) if fc >= 2 else (pair - 1, 62 + fc)
                    pl.wait("gpsimd", "te", w2_use_te[key])
                pl.dma("gpsimd", "ldw", w2qb[fc % 2][:],
                       w2q_d[fc * P:(fc + 1) * P, :])
                lv = pl.ctr["ldw"].v
                pl.wait("tensor", "ldw", lv)
                for ti in range(2):
                    t = pair * 2 + ti
                    for oc in range(4):
                        last = (ti == 1 and oc == 3)
                        lam = (lambda e, fc=fc, t=t, ti=ti, oc=oc, i=fc % 2:
                               e.matmul(ps[ti * 4 + oc][:],
                                        h_r[:, fc, t * P:(t + 1) * P],
                                        w2qb[i][:, oc * 512:(oc + 1) * 512],
                                        start=(fc == 0), stop=False))
                        if last:
                            pl.inc("tensor", "te", lam)
                        else:
                            pl.run("tensor", lam)
                w2_use_te[(pair, fc)] = pl.ctr["te"].v
            pl.wait("tensor", "ld", B2_LD)
            for ti in range(2):
                for oc in range(4):
                    last = (ti == 1 and oc == 3)
                    lam = (lambda e, ti=ti, oc=oc: e.matmul(
                        ps[ti * 4 + oc][:], mones_rb[0:1, :],
                        b2_sb[0:1, oc * 512:(oc + 1) * 512],
                        start=False, stop=True))
                    if last:
                        pl.inc("tensor", "te", lam)
                    else:
                        pl.run("tensor", lam)
            te_pair = pl.ctr["te"].v
            for ti in range(2):
                t = pair * 2 + ti
                pl.dma("sync", "ld", xb[ti][:], x1_d[t * P:(t + 1) * P, :])
                pl.dma("sync", "ld", ssb[ti][:], s1_d[t * P:(t + 1) * P, :])
                ld_xs = pl.ctr["ld"].v
                pl.wait("scalar", "te", te_pair)
                if pair > 0:
                    pl.wait("scalar", "st", out_st[2 * (pair - 1) + ti])
                for oc in range(4):
                    lam = (lambda s, ti=ti, oc=oc: s.activation(
                        g_sb[ti][:, oc * 512:(oc + 1) * 512], ps[ti * 4 + oc][:],
                        AF.Silu, scale=-1.0))
                    if oc == 3:
                        pl.inc("scalar", "se", lam)
                    else:
                        pl.run("scalar", lam)
                se_g = pl.ctr["se"].v
                pl.wait("vector", "se", se_g)
                pl.wait("vector", "ld", ld_xs)
                pl.inc("vector", "ve", lambda v, ti=ti: v.tensor_mul(
                    tsg[:], ssb[ti][:], g_sb[ti][:]))
                pl.wait("vector", "ve")
                pl.inc("vector", "ve", lambda v, ti=ti: v.tensor_add(
                    x2b[ti][:], xb[ti][:], tsg[:]))
                epi_ve[t] = pl.ctr["ve"].v
                pl.wait("sync", "ve", epi_ve[t])
                pl.dma("sync", "st", out_o[t * P:(t + 1) * P, :], x2b[ti][:])
                out_st.append(pl.ctr["st"].v)

        pl.wait("sync", "st", pl.ctr["st"].v)
        pl.finalize(block)
    return nc


def _prep_F1(inputs, outD, w1_t, w2_t):
    b1 = np.asarray(inputs["L1_ffn_b1"])
    b1_r = np.ascontiguousarray(b1.reshape(64, P).T)
    parts = np.ascontiguousarray(
        np.concatenate([np.asarray(outD[c]["parts_o"]) for c in range(NCORE)],
                       axis=0))
    maps = []
    for c in range(NCORE):
        maps.append({
            "xn1_fm": outD[c]["xn1_fm"],
            "w1_t": w1_t, "w2_t": w2_t, "b1_r": b1_r, "parts": parts,
        })
    return maps


def _prep_F2(inputs, outC, outE, outF1):
    b2 = np.ascontiguousarray(np.asarray(inputs["L1_ffn_b2"]).reshape(1, D))
    maps = []
    for c in range(NCORE):
        maps.append({
            "h_i": outF1[c]["h_o"],
            "w2q": outF1[c]["w2q_o"],
            "b2_r": b2,
            "x1_i": outC[c]["x1_o"],
            "s1_i": outE[c]["s_o"],
        })
    return maps



def kernel(**inputs):
    cores = list(range(NCORE))
    mapsA = _prep_A(inputs)
    resA = run_bass_kernel_spmd(build_A(), mapsA, core_ids=cores)
    outA = [resA.results[c] for c in cores]
    mapsB = _prep_B(inputs, outA)
    resB = run_bass_kernel_spmd(build_B(), mapsB, core_ids=cores)
    outB = [resB.results[c] for c in cores]
    mapsC = _prep_C(inputs, outA, outB)
    resC = run_bass_kernel_spmd(build_C(), mapsC, core_ids=cores)
    outC = [resC.results[c] for c in cores]
    mapsD, w1_t, w2_t = _prep_D(inputs, outC)
    resD = run_bass_kernel_spmd(build_D(), mapsD, core_ids=cores)
    outD = [resD.results[c] for c in cores]
    mapsE = _prep_E(inputs, outD)
    resE = run_bass_kernel_spmd(build_E(), mapsE, core_ids=cores)
    outE = [resE.results[c] for c in cores]
    mapsF1 = _prep_F1(inputs, outD, w1_t, w2_t)
    resF1 = run_bass_kernel_spmd(build_F1(), mapsF1, core_ids=cores)
    outF1 = [resF1.results[c] for c in cores]
    mapsF2 = _prep_F2(inputs, outC, outE, outF1)
    resF2 = run_bass_kernel_spmd(build_F2(), mapsF2, core_ids=cores)
    outF2 = [resF2.results[c] for c in cores]
    out = np.stack([
        np.concatenate([np.asarray(outF2[2 * b]["out_o"]),
                        np.asarray(outF2[2 * b + 1]["out_o"])], axis=0)
        for b in range(B)]).astype(np.float32)
    return out


# version 02
# speedup vs baseline: 1.1498x; 1.1498x over previous
import contextlib
import numpy as np
import ml_dtypes
import concourse.bass as bass
import concourse.mybir as mybir
from concourse.bass_utils import run_bass_kernel_spmd

F32 = mybir.dt.float32
BF16 = mybir.dt.bfloat16
AF = mybir.ActivationFunctionType
ALU = mybir.AluOpType
AX = mybir.AxisListType
BFDT = ml_dtypes.bfloat16

P = 128
D = 2048
S = 2048
B = 4
HALF = 1024
NT = HALF // P
NC = D // P
NKT = S // P
HEADS = 16
DH = 128
KV0 = 4
LN_EPS = 1e-5
RMS_EPS = 1e-6
C0 = 12.0
INV_SQRT_DH = float(1.0 / np.sqrt(128.0))
NCORE = 8


class Ctr:
    def __init__(self, sem):
        self.sem = sem
        self.v = 0


class Plan:
    ENGS = ["sync", "gpsimd", "tensor", "vector", "scalar"]

    def __init__(self, nc, es, sem_names):
        self.nc = nc
        self.ops = {e: [] for e in self.ENGS}
        self.ctr = {n: Ctr(es.enter_context(nc.semaphore(n))) for n in sem_names}

    def on(self, eng, fn):
        self.ops[eng].append(fn)

    def wait(self, eng, cname, val=None):
        c = self.ctr[cname]
        v = c.v if val is None else val
        self.on(eng, lambda e, s=c.sem, v=v: e.wait_ge(s, v))
        return v

    def inc(self, eng, cname, emit, n=1):
        c = self.ctr[cname]
        c.v += n
        self.on(eng, lambda e, emit=emit, s=c.sem, n=n: emit(e).then_inc(s, n))
        return c.v

    def run(self, eng, emit):
        self.on(eng, emit)

    def dma(self, eng, cname, out, in_):
        return self.inc(eng, cname, lambda e, o=out, i=in_: e.dma_start(out=o, in_=i), 16)

    def finalize(self, block):
        for eng in self.ENGS:
            ops = self.ops[eng]
            if eng == "sync":
                @block.sync
                def _(e, ops=ops):
                    for f in ops:
                        f(e)
            elif eng == "gpsimd":
                @block.gpsimd
                def _(e, ops=ops):
                    for f in ops:
                        f(e)
            elif eng == "tensor":
                @block.tensor
                def _(e, ops=ops):
                    for f in ops:
                        f(e)
            elif eng == "vector":
                @block.vector
                def _(e, ops=ops):
                    for f in ops:
                        f(e)
            elif eng == "scalar":
                @block.scalar
                def _(e, ops=ops):
                    for f in ops:
                        f(e)


def _bass():
    return bass.Bass("TRN2", target_bir_lowering=False)



def plan_ln(pl, xb, stats, sq_scr, xn_out, eps_col, x_ready_ev,
            stats_war_ve=None, xn_war=None):
    cn, cv = x_ready_ev
    pl.wait("scalar", cn, cv)
    if stats_war_ve is not None:
        pl.wait("scalar", "ve", stats_war_ve)
    pl.inc("scalar", "se", lambda s: s.activation(
        sq_scr[:], xb[:], AF.Square, accum_out=stats[:, 1:2]))
    se_sq = pl.ctr["se"].v
    pl.wait("vector", cn, cv)
    pl.inc("vector", "ve", lambda v: v.reduce_sum(stats[:, 0:1], xb[:], axis=AX.X))
    pl.wait("vector", "ve")
    pl.inc("vector", "ve", lambda v: v.tensor_scalar(
        stats[:, 2:3], stats[:, 0:1], 1.0 / D, None, ALU.mult))
    pl.wait("vector", "ve")
    pl.wait("vector", "se", se_sq)
    pl.inc("vector", "ve", lambda v: v.scalar_tensor_tensor(
        stats[:, 3:4], stats[:, 0:1], stats[:, 2:3], stats[:, 1:2],
        ALU.mult, ALU.subtract))
    stt_ve = pl.ctr["ve"].v
    pl.wait("scalar", "ve")
    pl.inc("scalar", "se", lambda s: s.activation(
        stats[:, 1:2], stats[:, 3:4], AF.Ln, bias=eps_col, scale=-1.0 / D))
    pl.wait("scalar", "se")
    pl.inc("scalar", "se", lambda s: s.activation(
        stats[:, 3:4], stats[:, 1:2], AF.Exp, scale=-0.5))
    pl.wait("vector", "se")
    if xn_war is not None:
        pl.wait("vector", xn_war[0], xn_war[1])
    pl.inc("vector", "ve", lambda v: v.tensor_scalar(
        xn_out[:, 0:D], xb[:], stats[:, 2:3], stats[:, 3:4],
        ALU.subtract, ALU.mult))
    return pl.ctr["ve"].v, stt_ve


def plan_rope(pl, src, out, sw, t1, t2, ropeC, ropeS, src_ready_ve,
              t2_war_ve=None):
    n = src.shape[1]
    pl.wait("gpsimd", "ve", src_ready_ve)
    pl.run("gpsimd", lambda g: g.tensor_copy(sw[0:64, 0:n], src[64:128, 0:n]))
    pl.inc("gpsimd", "ge", lambda g: g.tensor_copy(sw[64:128, 0:n], src[0:64, 0:n]))
    ge_sw = pl.ctr["ge"].v
    pl.wait("gpsimd", "ge")
    if t2_war_ve is not None:
        pl.wait("gpsimd", "ve", t2_war_ve)
    pl.inc("gpsimd", "ge", lambda g: g.tensor_mul(t2[:, 0:n], sw[:, 0:n], ropeS[:, 0:n]))
    ge_t2 = pl.ctr["ge"].v
    pl.wait("vector", "ve", src_ready_ve)
    pl.inc("vector", "ve", lambda v: v.tensor_mul(t1[:, 0:n], src[:, 0:n], ropeC[:, 0:n]))
    pl.wait("vector", "ve")
    pl.wait("vector", "ge", ge_t2)
    pl.inc("vector", "ve", lambda v: v.tensor_add(out[:, 0:n], t1[:, 0:n], t2[:, 0:n]))
    return pl.ctr["ve"].v, ge_sw



def build_A():
    nc = _bass()
    xA = nc.dram_tensor("xA", [HALF, D], F32, kind="ExternalInput").ap()
    wq_d = nc.dram_tensor("wq", [D, D], F32, kind="ExternalInput").ap()
    wk_d = nc.dram_tensor("wk", [D, 512], F32, kind="ExternalInput").ap()
    wv_d = nc.dram_tensor("wv", [D, 512], F32, kind="ExternalInput").ap()
    ropeC_d = nc.dram_tensor("ropeC", [P, HALF], F32, kind="ExternalInput").ap()
    ropeS_d = nc.dram_tensor("ropeS", [P, HALF], F32, kind="ExternalInput").ap()
    ident_d = nc.dram_tensor("ident", [P, P], F32, kind="ExternalInput").ap()
    consts_d = nc.dram_tensor("consts", [P, 4], F32, kind="ExternalInput").ap()
    xn_fm_o = nc.dram_tensor("xn_fm", [NC, P, HALF], BF16, kind="ExternalOutput").ap()
    xn_tm_o = nc.dram_tensor("xn_tm_o", [HALF, D], F32, kind="ExternalOutput").ap()
    q_o = nc.dram_tensor("q_o", [HEADS, P, HALF], BF16, kind="ExternalOutput").ap()
    k_o = nc.dram_tensor("k_o", [KV0, P, HALF], BF16, kind="ExternalOutput").ap()
    v_o = nc.dram_tensor("v_o", [HALF, 512], BF16, kind="ExternalOutput").ap()

    with contextlib.ExitStack() as es:
        E = es.enter_context
        xb = [E(nc.sbuf_tensor(f"xb{i}", [P, D], F32)) for i in range(2)]
        sq_scr = E(nc.sbuf_tensor("sq_scr", [P, D], F32))
        stats = E(nc.sbuf_tensor("stats", [P, 4], F32))
        consts = E(nc.sbuf_tensor("consts_sb", [P, 4], F32))
        xn_tm = E(nc.sbuf_tensor("xn_tm", [P, D], F32))
        xn_bft = E(nc.sbuf_tensor("xn_bft", [P, D], BF16))
        xn_fm = E(nc.sbuf_tensor("xn_fm_sb", [P, NC * HALF], BF16))
        idb = E(nc.sbuf_tensor("idb", [P, P], BF16))
        wbuf = E(nc.sbuf_tensor("wbuf", [P, NC * D], BF16))
        wkv = E(nc.sbuf_tensor("wkv", [P, NC * 1024], BF16))
        ropeC = E(nc.sbuf_tensor("ropeC_sb", [P, HALF], BF16))
        ropeS = E(nc.sbuf_tensor("ropeS_sb", [P, HALF], BF16))
        qraw = [E(nc.sbuf_tensor(f"qraw{i}", [P, HALF], BF16)) for i in range(2)]
        qout = [E(nc.sbuf_tensor(f"qout{i}", [P, HALF], BF16)) for i in range(2)]
        sw = E(nc.sbuf_tensor("sw", [P, HALF], BF16))
        t1 = E(nc.sbuf_tensor("t1", [P, HALF], BF16))
        t2 = E(nc.sbuf_tensor("t2", [P, HALF], BF16))
        vbuf = [E(nc.sbuf_tensor(f"vbuf{i}", [P, 512], BF16)) for i in range(2)]

        ps_tr = [E(nc.psum_tensor(f"ps_tr{i}", [P, 512], BF16)) for i in range(2)]
        ps_q = [E(nc.psum_tensor(f"ps_q{i}", [P, 512], F32)) for i in range(4)]
        ps_v = [E(nc.psum_tensor(f"ps_v{i}", [P, 512], F32)) for i in range(2)]

        pl = Plan(nc, es, ["ld_x", "ld_w", "ld_c", "ve", "se", "ge", "te", "st"])
        block = E(nc.Block())

        xn_fm_r = xn_fm[:].rearrange("p (c q) -> p c q", q=HALF)

        pl.dma("sync", "ld_c", consts[:], consts_d)
        pl.dma("gpsimd", "ld_c", idb[:], ident_d)
        pl.dma("gpsimd", "ld_c", ropeC[:], ropeC_d)
        pl.dma("gpsimd", "ld_c", ropeS[:], ropeS_d)
        LD_C_ALL = pl.ctr["ld_c"].v
        eps_col = consts[:, 0:1]

        for r in range(NC):
            pl.dma("gpsimd", "ld_w", wbuf[:, r * D:(r + 1) * D], wq_d[r * P:(r + 1) * P, :])
        W_Q_DONE = pl.ctr["ld_w"].v
        for r in range(NC):
            pl.dma("gpsimd", "ld_w", wkv[:, r * 512:(r + 1) * 512], wk_d[r * P:(r + 1) * P, :])
        W_K_DONE = pl.ctr["ld_w"].v
        for r in range(NC):
            pl.dma("gpsimd", "ld_w", wkv[:, 8192 + r * 512:8192 + (r + 1) * 512],
                   wv_d[r * P:(r + 1) * P, :])
        W_V_DONE = pl.ctr["ld_w"].v

        xn_done_ve = []
        stt_ve_l = []
        evac_done_ve = []
        tr_batch_te = []
        tr_evac_ve = []
        xn_dma_st = []
        for t in range(NT):
            if t >= 2:
                pl.wait("sync", "ve", xn_done_ve[t - 2])
            pl.dma("sync", "ld_x", xb[t % 2][:], xA[t * P:(t + 1) * P, :])
            ldv = pl.ctr["ld_x"].v
            if t == 0:
                pl.wait("scalar", "ld_c", LD_C_ALL)
            v_xn, v_stt = plan_ln(
                pl, xb[t % 2], stats, sq_scr, xn_tm, eps_col, ("ld_x", ldv),
                stats_war_ve=(stt_ve_l[-1] if stt_ve_l else None),
                xn_war=(("st", xn_dma_st[-1]) if xn_dma_st else None))
            xn_done_ve.append(v_xn)
            stt_ve_l.append(v_stt)
            pl.wait("vector", "ve", v_xn)
            if t >= 1:
                pl.wait("vector", "te", tr_batch_te[-1])
            pl.inc("vector", "ve", lambda v: v.tensor_copy(xn_bft[:], xn_tm[:, 0:D]))
            v_cast = pl.ctr["ve"].v
            pl.wait("sync", "ve", v_cast)
            pl.dma("sync", "st", xn_tm_o[t * P:(t + 1) * P, :], xn_tm[:, 0:D])
            xn_dma_st.append(pl.ctr["st"].v)
            pl.wait("tensor", "ve", v_cast)
            if t == 0:
                pl.wait("tensor", "ld_c", LD_C_ALL)
            for cb in range(4):
                j = (t * 4 + cb) % 2
                if t * 4 + cb >= 2:
                    pl.wait("tensor", "ve", tr_evac_ve[(t * 4 + cb) - 2])
                for cc in range(4):
                    c = cb * 4 + cc
                    if cc < 3:
                        pl.run("tensor", lambda e, j=j, cc=cc, c=c: e.transpose(
                            ps_tr[j][:, cc * P:(cc + 1) * P],
                            xn_bft[:, c * P:(c + 1) * P], idb[:]))
                    else:
                        pl.inc("tensor", "te", lambda e, j=j, cc=cc, c=c: e.transpose(
                            ps_tr[j][:, cc * P:(cc + 1) * P],
                            xn_bft[:, c * P:(c + 1) * P], idb[:]))
                te_v = pl.ctr["te"].v
                tr_batch_te.append(te_v)
                pl.wait("vector", "te", te_v)
                dst = xn_fm_r[:, cb * 4:(cb + 1) * 4, t * P:(t + 1) * P]
                src_ = ps_tr[j][:].rearrange("p (c q) -> p c q", q=P)
                pl.inc("vector", "ve", lambda v, d=dst, s=src_: v.tensor_copy(d, s))
                tr_evac_ve.append(pl.ctr["ve"].v)
            evac_done_ve.append(pl.ctr["ve"].v)
        XN_ALL_VE = evac_done_ve[-1]

        pl.wait("gpsimd", "ve", XN_ALL_VE)
        for c in range(NC):
            pl.dma("gpsimd", "st", xn_fm_o[c, :, :], xn_fm[:, c * HALF:(c + 1) * HALF])

        head_evac_ve = []
        head_out_ve = []
        head_st = []
        rope_ge_sw = []
        for hd in range(HEADS + KV0):
            is_k = hd >= HEADS
            kh = hd - HEADS
            slot = hd % 2
            if hd == 0:
                pl.wait("tensor", "ld_w", W_Q_DONE)
                pl.wait("tensor", "ve", XN_ALL_VE)
            if hd == HEADS:
                pl.wait("tensor", "ld_w", W_K_DONE)
            if hd >= 2:
                pl.wait("tensor", "ve", head_evac_ve[hd - 2])
            for qc in range(2):
                for c in range(NC):
                    if is_k:
                        lhsT = wkv[:, c * 512 + kh * P: c * 512 + (kh + 1) * P]
                    else:
                        lhsT = wbuf[:, c * D + hd * P: c * D + (hd + 1) * P]
                    rhs = xn_fm_r[:, c, qc * 512:(qc + 1) * 512]
                    if c < NC - 1:
                        pl.run("tensor", lambda e, o=ps_q[slot * 2 + qc][:], l=lhsT, r=rhs,
                               st_=(c == 0): e.matmul(o, l, r, start=st_, stop=False))
                    else:
                        pl.inc("tensor", "te", lambda e, o=ps_q[slot * 2 + qc][:], l=lhsT,
                               r=rhs: e.matmul(o, l, r, start=False, stop=True))
            te_head = pl.ctr["te"].v
            pl.wait("vector", "te", te_head)
            if hd >= 2:
                pl.wait("vector", "ve", head_out_ve[hd - 2])
                pl.wait("vector", "ge", rope_ge_sw[hd - 2])
            pl.run("vector", lambda v, s=slot: v.tensor_copy(
                qraw[s][:, 0:512], ps_q[s * 2][:]))
            pl.inc("vector", "ve", lambda v, s=slot: v.tensor_copy(
                qraw[s][:, 512:1024], ps_q[s * 2 + 1][:]))
            head_evac_ve.append(pl.ctr["ve"].v)
            if hd >= 2:
                pl.wait("vector", "st", head_st[hd - 2])
                pl.wait("gpsimd", "st", head_st[hd - 2])
            v_out, ge_sw = plan_rope(
                pl, qraw[slot], qout[slot], sw, t1, t2, ropeC, ropeS,
                head_evac_ve[-1],
                t2_war_ve=(head_out_ve[-1] if head_out_ve else None))
            rope_ge_sw.append(ge_sw)
            head_out_ve.append(v_out)
            pl.wait("sync", "ve", v_out)
            dst = k_o[kh, :, :] if is_k else q_o[hd, :, :]
            pl.dma("sync", "st", dst, qout[slot][:])
            head_st.append(pl.ctr["st"].v)

        v_evac_ve = []
        v_st = []
        for t in range(NT):
            if t == 0:
                pl.wait("tensor", "ld_w", W_V_DONE)
            if t >= 2:
                pl.wait("tensor", "ve", v_evac_ve[t - 2])
            for c in range(NC):
                lhsT = xn_fm_r[:, c, t * P:(t + 1) * P]
                rhs = wkv[:, 8192 + c * 512:8192 + (c + 1) * 512]
                if c < NC - 1:
                    pl.run("tensor", lambda e, o=ps_v[t % 2][:], l=lhsT, r=rhs,
                           st_=(c == 0): e.matmul(o, l, r, start=st_, stop=False))
                else:
                    pl.inc("tensor", "te", lambda e, o=ps_v[t % 2][:], l=lhsT, r=rhs:
                           e.matmul(o, l, r, start=False, stop=True))
            te_v = pl.ctr["te"].v
            pl.wait("vector", "te", te_v)
            if t >= 2:
                pl.wait("vector", "st", v_st[t - 2])
            pl.inc("vector", "ve", lambda v, t=t: v.tensor_copy(
                vbuf[t % 2][:], ps_v[t % 2][:]))
            v_evac_ve.append(pl.ctr["ve"].v)
            pl.wait("sync", "ve", v_evac_ve[-1])
            pl.dma("sync", "st", v_o[t * P:(t + 1) * P, :], vbuf[t % 2][:])
            v_st.append(pl.ctr["st"].v)

        pl.wait("sync", "st", pl.ctr["st"].v)
        pl.finalize(block)
    return nc



def _rope_tables(h):
    inv = (1.0 / 10000.0 ** (np.arange(0, DH, 2, dtype=np.float64) / DH))
    t = np.arange(h * HALF, (h + 1) * HALF, dtype=np.float64)
    ang = t[None, :] * inv[:, None]
    c = np.cos(ang)
    s = np.sin(ang)
    Cf = np.concatenate([c, c], axis=0).astype(np.float32)
    Sf = np.concatenate([-s, s], axis=0).astype(np.float32)
    return Cf, Sf


_PERM = np.concatenate([np.arange(0, DH, 2), np.arange(1, DH, 2)])


def _prep_A(inputs):
    x = np.asarray(inputs["L0_attn_x"])
    w_attn = np.asarray(inputs["L0_attn_w_attn"])
    wq = w_attn[:, :D].reshape(D, HEADS, DH)[:, :, _PERM].reshape(D, D)
    wk = w_attn[:, D:D + 512].reshape(D, KV0, DH)[:, :, _PERM].reshape(D, 512)
    wv = np.ascontiguousarray(w_attn[:, D + 512:])
    ident = np.eye(P, dtype=np.float32)
    consts = np.zeros((P, 4), dtype=np.float32)
    consts[:, 0] = LN_EPS
    maps = []
    for c in range(NCORE):
        b, h = divmod(c, 2)
        Cf, Sf = _rope_tables(h)
        maps.append({
            "xA": np.ascontiguousarray(x[b, h * HALF:(h + 1) * HALF, :]),
            "wq": np.ascontiguousarray(wq),
            "wk": np.ascontiguousarray(wk),
            "wv": wv,
            "ropeC": Cf, "ropeS": Sf,
            "ident": ident, "consts": consts,
        })
    return maps






def build_B(debug=False):
    nc = _bass()
    q_d = nc.dram_tensor("q_i", [HEADS, P, HALF], BF16, kind="ExternalInput").ap()
    k_d = nc.dram_tensor("k_i", [KV0, P, S], BF16, kind="ExternalInput").ap()
    v_d = nc.dram_tensor("v_i", [S, 512], BF16, kind="ExternalInput").ap()
    wp_d = nc.dram_tensor("wp", [D, D], F32, kind="ExternalInput").ap()
    nw_d = nc.dram_tensor("nw", [1, D], F32, kind="ExternalInput").ap()
    mask_d = nc.dram_tensor("mask", [S, HALF], BF16, kind="ExternalInput").ap()
    consts_d = nc.dram_tensor("consts", [P, 4], F32, kind="ExternalInput").ap()
    s_o = nc.dram_tensor("s_o", [HALF, D], BF16, kind="ExternalOutput").ap()
    if debug:
        y_dbg = nc.dram_tensor("y_dbg", [HEADS, P, HALF], BF16, kind="ExternalOutput").ap()
        d_dbg = nc.dram_tensor("d_dbg", [HEADS, P, HALF], F32, kind="ExternalOutput").ap()

    with contextlib.ExitStack() as es:
        E = es.enter_context
        q_sb = E(nc.sbuf_tensor("q_sb", [P, HEADS * HALF], BF16))
        k_sb = E(nc.sbuf_tensor("k_sb", [P, KV0 * S], BF16))
        v_sb = E(nc.sbuf_tensor("v_sb", [P, NKT * 512], BF16))
        mask_sb = E(nc.sbuf_tensor("mask_sb", [P, NKT * HALF], BF16))
        consts = E(nc.sbuf_tensor("consts_sb", [P, 4], F32))
        ones_bf = E(nc.sbuf_tensor("ones_bf", [P, P], BF16))
        ones1 = E(nc.sbuf_tensor("ones1", [P, P], F32))
        nw_row = E(nc.sbuf_tensor("nw_row", [1, D], F32))
        NW = E(nc.sbuf_tensor("NW", [P, D], F32))
        pt = [E(nc.sbuf_tensor(f"pt{i}", [P, HALF], BF16)) for i in range(2)]
        rec = E(nc.sbuf_tensor("rec", [P, HALF], F32))
        y_fm = E(nc.sbuf_tensor("y_fm", [P, HEADS * HALF], BF16))
        wp_sb = [E(nc.sbuf_tensor(f"wp_sb{i}", [P, D], BF16)) for i in range(2)]
        s_sb = [E(nc.sbuf_tensor(f"s_sb{i}", [P, D], BF16)) for i in range(2)]
        scr = E(nc.sbuf_tensor("scr", [P, 512], F32))
        rstats = E(nc.sbuf_tensor("rstats", [P, 8], F32))

        ps_s = [E(nc.psum_tensor(f"ps_s{i}", [P, 512], F32)) for i in range(4)]
        ps_pv = [E(nc.psum_tensor(f"ps_pv{i}", [P, 512], F32)) for i in range(2)]
        ps_d = [E(nc.psum_tensor(f"ps_d{i}", [P, 512], F32)) for i in range(2)]

        pl = Plan(nc, es, ["ld", "ldw", "ve", "se", "ge", "te", "st"])
        block = E(nc.Block())

        q_r = q_sb[:].rearrange("p (h q) -> p h q", q=HALF)
        k_r = k_sb[:].rearrange("p (h s) -> p h s", s=S)
        v_r = v_sb[:].rearrange("p (t c) -> p t c", c=512)
        mask_r = mask_sb[:].rearrange("p (t q) -> p t q", q=HALF)

        for hh in range(HEADS):
            pl.dma("sync", "ld", q_r[:, hh, :], q_d[hh, :, :])
        for kv in range(KV0):
            pl.dma("sync", "ld", k_r[:, kv, :], k_d[kv, :, :])
        for kt in range(NKT):
            pl.dma("sync", "ld", v_r[:, kt, :], v_d[kt * P:(kt + 1) * P, :])
            pl.dma("sync", "ld", mask_r[:, kt, :], mask_d[kt * P:(kt + 1) * P, :])
        pl.dma("sync", "ld", consts[:], consts_d)
        pl.dma("sync", "ld", nw_row[:], nw_d)
        LD_ALL = pl.ctr["ld"].v
        negC_col = consts[:, 0:1]
        eps_col = consts[:, 1:2]

        pl.run("gpsimd", lambda g: g.memset(ones_bf[:], 1.0))
        pl.inc("gpsimd", "ge", lambda g: g.memset(ones1[:], 1.0))
        GE_ONES = pl.ctr["ge"].v

        pl.wait("tensor", "ld", LD_ALL)
        pl.wait("tensor", "ge", GE_ONES)
        for i in range(4):
            if i > 0:
                pl.wait("tensor", "ve")
            pl.inc("tensor", "te", lambda e, i=i: e.matmul(
                ps_pv[0][:], ones1[0:1, :], nw_row[0:1, i * 512:(i + 1) * 512],
                start=True, stop=True))
            te_nw = pl.ctr["te"].v
            pl.wait("vector", "te", te_nw)
            pl.inc("vector", "ve", lambda v, i=i: v.tensor_copy(
                NW[:, i * 512:(i + 1) * 512], ps_pv[0][:]))
        NW_VE = pl.ctr["ve"].v

        exp_se = {}
        mask_ve = {}
        pvden_te = {}
        head_evac_ve = []
        pl.wait("scalar", "ld", LD_ALL)
        for h in range(HEADS):
            kv = h // 4
            if h == 0:
                pl.wait("tensor", "ve", NW_VE)
            else:
                pl.wait("tensor", "ve", head_evac_ve[h - 1])
            for kt in range(NKT):
                sl = kt % 2
                prev = (h, kt - 2) if kt >= 2 else ((h - 1, NKT - 1) if h > 0 else None)
                if prev in exp_se:
                    pl.wait("tensor", "se", exp_se[prev])
                for qc in range(2):
                    pl.inc("tensor", "te", lambda e, h=h, kv=kv, kt=kt, qc=qc, sl=sl:
                           e.matmul(ps_s[sl * 2 + qc][:],
                                    k_r[:, kv, kt * P:(kt + 1) * P],
                                    q_r[:, h, qc * 512:(qc + 1) * 512],
                                    start=True, stop=True))
                te_s = pl.ctr["te"].v
                pl.wait("scalar", "te", te_s)
                if prev in pvden_te:
                    pl.wait("scalar", "te", pvden_te[prev])
                for qc in range(2):
                    pl.inc("scalar", "se", lambda s, sl=sl, qc=qc: s.activation(
                        pt[sl][:, qc * 512:(qc + 1) * 512], ps_s[sl * 2 + qc][:],
                        AF.Exp, bias=negC_col, scale=INV_SQRT_DH))
                exp_se[(h, kt)] = pl.ctr["se"].v
                pl.wait("vector", "se", exp_se[(h, kt)])
                for qc in range(2):
                    pl.inc("vector", "ve", lambda v, sl=sl, kt=kt, qc=qc:
                           v.tensor_mul(pt[sl][:, qc * 512:(qc + 1) * 512],
                                        pt[sl][:, qc * 512:(qc + 1) * 512],
                                        mask_r[:, kt, qc * 512:(qc + 1) * 512]))
                mask_ve[(h, kt)] = pl.ctr["ve"].v
                pl.wait("tensor", "ve", mask_ve[(h, kt)])
                for qc in range(2):
                    pl.run("tensor", lambda e, kt=kt, kv=kv, sl=sl, qc=qc, h=h:
                           e.matmul(ps_pv[qc][:],
                                    v_r[:, kt, kv * P:(kv + 1) * P],
                                    pt[sl][:, qc * 512:(qc + 1) * 512],
                                    start=(kt == 0), stop=(kt == NKT - 1)))
                for qc in range(2):
                    pl.inc("tensor", "te", lambda e, kt=kt, sl=sl, qc=qc:
                           e.matmul(ps_d[qc][:],
                                    ones_bf[:],
                                    pt[sl][:, qc * 512:(qc + 1) * 512],
                                    start=(kt == 0), stop=(kt == NKT - 1)))
                pvden_te[(h, kt)] = pl.ctr["te"].v
            pl.wait("vector", "te", pvden_te[(h, NKT - 1)])
            pl.run("vector", lambda v: v.reciprocal(rec[:, 0:512], ps_d[0][:]))
            pl.inc("vector", "ve", lambda v: v.reciprocal(
                rec[:, 512:1024], ps_d[1][:]))
            pl.wait("vector", "ve")
            for qc in range(2):
                pl.inc("vector", "ve", lambda v, h=h, qc=qc: v.tensor_mul(
                    y_fm[:, h * HALF + qc * 512: h * HALF + (qc + 1) * 512],
                    ps_pv[qc][:], rec[:, qc * 512:(qc + 1) * 512]))
            head_evac_ve.append(pl.ctr["ve"].v)
            if debug:
                pl.wait("sync", "ve", head_evac_ve[-1])
                pl.dma("sync", "st", y_dbg[h, :, :],
                       y_fm[:, h * HALF:(h + 1) * HALF])
                pl.dma("sync", "st", d_dbg[h, :, :], rec[:])
                pl.wait("vector", "st", pl.ctr["st"].v)
        Y_ALL_VE = head_evac_ve[-1]

        y_r = y_fm[:].rearrange("p (h q) -> p h q", q=HALF)

        s_st = []
        epi_ve = {}
        wp_use_te = {}
        for pair in range(NT // 2):
            t0, t1 = pair * 2, pair * 2 + 1
            ps_o = {t0: [ps_s[0], ps_s[1], ps_s[2], ps_s[3]],
                    t1: [ps_pv[0], ps_pv[1], ps_d[0], ps_d[1]]}
            if pair == 0:
                pl.wait("tensor", "ve", Y_ALL_VE)
            else:
                pl.wait("tensor", "ve", epi_ve[t0 - 1])
            for c in range(NC):
                if (pair, c) != (0, 0) and (pair, c) != (0, 1):
                    key = (pair, c - 2) if c >= 2 else (pair - 1, NC - 2 + c)
                    pl.wait("gpsimd", "te", wp_use_te[key])
                pl.dma("gpsimd", "ldw", wp_sb[c % 2][:], wp_d[c * P:(c + 1) * P, :])
                ldw_v = pl.ctr["ldw"].v
                pl.wait("tensor", "ldw", ldw_v)
                for t in (t0, t1):
                    for oc in range(4):
                        last = (t == t1 and oc == 3)
                        if not last:
                            pl.run("tensor", lambda e, c=c, t=t, oc=oc,
                                   o=ps_o[t][oc]: e.matmul(
                                o[:], y_r[:, c, t * P:(t + 1) * P],
                                wp_sb[c % 2][:, oc * 512:(oc + 1) * 512],
                                start=(c == 0), stop=(c == NC - 1)))
                        else:
                            pl.inc("tensor", "te", lambda e, c=c, t=t, oc=oc,
                                   o=ps_o[t][oc]: e.matmul(
                                o[:], y_r[:, c, t * P:(t + 1) * P],
                                wp_sb[c % 2][:, oc * 512:(oc + 1) * 512],
                                start=(c == 0), stop=(c == NC - 1)))
                wp_use_te[(pair, c)] = pl.ctr["te"].v
            te_pair = pl.ctr["te"].v
            for t in (t0, t1):
                pl.wait("scalar", "te", te_pair)
                for oc in range(4):
                    pl.inc("scalar", "se", lambda s, oc=oc, o=ps_o[t][oc]:
                           s.activation(scr[:], o[:], AF.Square,
                                        accum_out=rstats[:, oc:oc + 1]))
                se_sq = pl.ctr["se"].v
                pl.wait("vector", "se", se_sq)
                pl.inc("vector", "ve", lambda v: v.reduce_sum(
                    rstats[:, 4:5], rstats[:, 0:4], axis=AX.X))
                pl.wait("vector", "ve")
                pl.wait("scalar", "ve")
                pl.inc("scalar", "se", lambda s: s.activation(
                    rstats[:, 5:6], rstats[:, 4:5], AF.Ln, bias=eps_col,
                    scale=1.0 / D))
                pl.wait("scalar", "se")
                pl.inc("scalar", "se", lambda s: s.activation(
                    rstats[:, 6:7], rstats[:, 5:6], AF.Exp, scale=-0.5))
                rstd_se = pl.ctr["se"].v
                pl.wait("vector", "se", rstd_se)
                if t >= 2:
                    pl.wait("vector", "st", s_st[t - 2])
                for oc in range(4):
                    pl.inc("vector", "ve", lambda v, oc=oc, t=t, o=ps_o[t][oc]:
                           v.scalar_tensor_tensor(
                               s_sb[t % 2][:, oc * 512:(oc + 1) * 512],
                               o[:], rstats[:, 6:7],
                               NW[:, oc * 512:(oc + 1) * 512],
                               ALU.mult, ALU.mult))
                epi_ve[t] = pl.ctr["ve"].v
                pl.wait("sync", "ve", epi_ve[t])
                pl.dma("sync", "st", s_o[t * P:(t + 1) * P, :], s_sb[t % 2][:])
                s_st.append(pl.ctr["st"].v)

        pl.wait("sync", "st", pl.ctr["st"].v)
        pl.finalize(block)
    return nc


def _prep_B(inputs, outA):
    wp = np.asarray(inputs["L0_attn_w_proj"])
    nw = np.asarray(inputs["L0_attn_norm_w"]).reshape(1, D)
    consts = np.zeros((P, 4), dtype=np.float32)
    consts[:, 0] = -C0
    consts[:, 1] = RMS_EPS
    maps = []
    for c in range(NCORE):
        b, h = divmod(c, 2)
        k_full = np.concatenate([outA[2 * b]["k_o"], outA[2 * b + 1]["k_o"]], axis=2)
        v_full = np.concatenate([outA[2 * b]["v_o"], outA[2 * b + 1]["v_o"]], axis=0)
        kpos = np.arange(S)[:, None]
        qpos = h * HALF + np.arange(HALF)[None, :]
        mask = (kpos <= qpos).astype(BFDT)
        maps.append({
            "q_i": outA[c]["q_o"], "k_i": np.ascontiguousarray(k_full),
            "v_i": np.ascontiguousarray(v_full),
            "wp": wp, "nw": nw, "mask": mask, "consts": consts,
        })
    return maps



def build_C(n_pairs=4, n_experts=8, dump_cmat=False, no_bias=False, simple_combine=False):
    nc = _bass()
    xn_d = nc.dram_tensor("xn_fm", [NC, P, HALF], BF16, kind="ExternalInput").ap()
    x_d = nc.dram_tensor("x_i", [HALF, D], F32, kind="ExternalInput").ap()
    s_d = nc.dram_tensor("s_i", [HALF, D], BF16, kind="ExternalInput").ap()
    wg_d = nc.dram_tensor("wg", [P, P], F32, kind="ExternalInput").ap()
    W_d = nc.dram_tensor("W_t", [8, D, D], F32, kind="ExternalInput").ap()
    b_d = nc.dram_tensor("b_moe", [1, 8 * D], F32, kind="ExternalInput").ap()
    x1_o = nc.dram_tensor("x1_o", [HALF, D], F32, kind="ExternalOutput").ap()

    with contextlib.ExitStack() as es:
        E = es.enter_context
        xn_fm = E(nc.sbuf_tensor("xn_fm_sb", [P, NC * HALF], BF16))
        wg_sb = E(nc.sbuf_tensor("wg_sb", [P, NC * 8], BF16))
        bbuf = E(nc.sbuf_tensor("bbuf", [8, D], BF16))
        ones_t = E(nc.sbuf_tensor("ones_row", [8, P], BF16))
        gex = E(nc.sbuf_tensor("gex", [P, 8], F32))
        gs = E(nc.sbuf_tensor("gs", [P, 4], F32))
        cmat = E(nc.sbuf_tensor("cmat", [P, NT * 8], F32))
        wrow = [E(nc.sbuf_tensor(f"wrow{i}", [P, D], BF16)) for i in range(2)]
        acc = [E(nc.sbuf_tensor(f"acc{i}", [P, D], F32)) for i in range(2)]
        g_sb = [E(nc.sbuf_tensor(f"g_sb{i}", [P, D], BF16)) for i in range(2)]
        tsg = E(nc.sbuf_tensor("tsg", [P, D], BF16))
        xb = [E(nc.sbuf_tensor(f"xb{i}", [P, D], F32)) for i in range(2)]
        ssb = [E(nc.sbuf_tensor(f"ssb{i}", [P, D], BF16)) for i in range(2)]
        x1b = [E(nc.sbuf_tensor(f"x1b{i}", [P, D], F32)) for i in range(2)]

        ps_e = [E(nc.psum_tensor(f"ps_e{i}", [P, 512], F32)) for i in range(8)]

        pl = Plan(nc, es, ["ld", "ldw", "ve", "se", "te", "st", "ge"])
        block = E(nc.Block())

        xn_r = xn_fm[:].rearrange("p (c q) -> p c q", q=HALF)

        for c in range(NC):
            pl.dma("sync", "ld", xn_r[:, c, :], xn_d[c, :, :])
        XN_LD = pl.ctr["ld"].v
        pl.dma("gpsimd", "ldw", wg_sb[:], wg_d)
        bmoe = bmoe_t[0:1, :]
        ones_row = ones_t[0:1, :]
        pl.dma("gpsimd", "ldw", bmoe, b_d)
        CONST_LDW = pl.ctr["ldw"].v
        pl.inc("gpsimd", "ge", lambda g: g.memset(ones_row, 1.0))
        GE_ONES = pl.ctr["ge"].v

        ps_l = ps_e[0][:, 0:8]

        cmat_ve = {}
        pl.wait("tensor", "ld", XN_LD)
        pl.wait("tensor", "ldw", CONST_LDW)
        for t in range(NT):
            if t > 0:
                pl.wait("tensor", "ve", cmat_ve[t - 1])
            for c in range(NC):
                wgr = wg_sb[:].rearrange("p (c e) -> p c e", e=8)
                if c < NC - 1:
                    pl.run("tensor", lambda e, c=c, t=t, w=wgr: e.matmul(
                        ps_l, xn_r[:, c, t * P:(t + 1) * P], w[:, c, :],
                        start=(c == 0), stop=False))
                else:
                    pl.inc("tensor", "te", lambda e, c=c, t=t, w=wgr: e.matmul(
                        ps_l, xn_r[:, c, t * P:(t + 1) * P], w[:, c, :],
                        start=False, stop=True))
            te_l = pl.ctr["te"].v
            pl.wait("vector", "te", te_l)
            pl.inc("vector", "ve", lambda v: v.reduce_max(gs[:, 0:1], ps_l, axis=AX.X))
            pl.wait("vector", "ve")
            pl.inc("vector", "ve", lambda v: v.tensor_scalar(
                gs[:, 1:2], gs[:, 0:1], -1.0, None, ALU.mult))
            ve_m = pl.ctr["ve"].v
            pl.wait("scalar", "ve", ve_m)
            pl.inc("scalar", "se", lambda s: s.activation(
                gex[:], ps_l, AF.Exp, bias=gs[:, 1:2]))
            pl.wait("vector", "se")
            pl.inc("vector", "ve", lambda v: v.reduce_sum(gs[:, 2:3], gex[:], axis=AX.X))
            pl.wait("vector", "ve")
            pl.inc("vector", "ve", lambda v: v.reciprocal(gs[:, 3:4], gs[:, 2:3]))
            pl.wait("vector", "ve")
            pl.inc("vector", "ve", lambda v, t=t: v.tensor_scalar(
                cmat[:, t * 8:(t + 1) * 8], ps_l, gs[:, 0:1], gs[:, 3:4],
                ALU.is_equal, ALU.mult))
            cmat_ve[t] = pl.ctr["ve"].v
        CMAT_ALL = pl.ctr["ve"].v
        if dump_cmat:
            pl.wait("sync", "ve", CMAT_ALL)
            pl.dma("sync", "st", x1_o[0:P, 0:NT * 8], cmat[:])
            pl.wait("sync", "st", pl.ctr["st"].v)
            pl.finalize(block)
            return nc

        comb_ve = {}
        wrow_te = {}
        x1_st = []
        silu_se_last = 0
        for pair in range(n_pairs):
            t0 = pair * 2
            for e in range(n_experts):
                for c in range(NC):
                    idx = e * NC + c
                    if pair > 0 or idx >= 2:
                        key = idx - 2 if idx >= 2 else (pair - 1, 8 * NC - 2 + idx)
                        pl.wait("gpsimd", "te",
                                wrow_te[(pair, idx - 2)] if idx >= 2
                                else wrow_te[(pair - 1, 8 * NC - 2 + idx)])
                    pl.dma("gpsimd", "ldw", wrow[idx % 2][:],
                           W_d[e, c * P:(c + 1) * P, :])
                    ldw_v = pl.ctr["ldw"].v
                    pl.wait("tensor", "ldw", ldw_v)
                    if e == 0 and c == 0:
                        if pair == 0:
                            pl.wait("tensor", "ve", CMAT_ALL)
                        else:
                            pl.wait("tensor", "ve", comb_ve[(pair - 1, n_experts - 1)])
                    elif c == 0:
                        pl.wait("tensor", "ve", comb_ve[(pair, e - 1)])
                    for ti in range(2):
                        t = t0 + ti
                        for oc in range(4):
                            last = (ti == 1 and oc == 3)
                            fn = pl.inc if last else pl.run
                            args = ("tensor", "te") if last else ("tensor",)
                            lam = (lambda ee, c=c, t=t, ti=ti, oc=oc, i=idx % 2,
                                   sp=(no_bias and c == NC - 1):
                                   ee.matmul(ps_e[ti * 4 + oc][:],
                                             xn_r[:, c, t * P:(t + 1) * P],
                                             wrow[i][:, oc * 512:(oc + 1) * 512],
                                             start=(c == 0), stop=sp))
                            if last:
                                pl.inc("tensor", "te", lam)
                            else:
                                pl.run("tensor", lam)
                    wrow_te[(pair, idx)] = pl.ctr["te"].v
                for ti in range(2):
                    for oc in range(4):
                        last = (ti == 1 and oc == 3)
                        if no_bias:
                            lam = (lambda ee: ee.nop())
                        else:
                            lam = (lambda ee, e=e, ti=ti, oc=oc: ee.matmul(
                                ps_e[ti * 4 + oc][:], ones_row,
                                bmoe[0:1, e * D + oc * 512: e * D + (oc + 1) * 512],
                                start=False, stop=True))
                        if last:
                            pl.inc("tensor", "te", lam)
                        else:
                            pl.run("tensor", lam)
                te_e = pl.ctr["te"].v
                pl.wait("vector", "te", te_e)
                if e == 0 and pair > 0:
                    pl.wait("vector", "se", silu_se_last)
                for ti in range(2):
                    t = t0 + ti
                    for oc in range(4):
                        last = (ti == 1 and oc == 3)
                        ccol = cmat[:, t * 8 + e: t * 8 + e + 1]
                        if e == 0 or simple_combine:
                            lam = (lambda v, ti=ti, oc=oc, cc=ccol:
                                   v.tensor_scalar(
                                       acc[ti][:, oc * 512:(oc + 1) * 512],
                                       ps_e[ti * 4 + oc][:], cc, None, ALU.mult))
                        else:
                            lam = (lambda v, ti=ti, oc=oc, cc=ccol:
                                   v.scalar_tensor_tensor(
                                       acc[ti][:, oc * 512:(oc + 1) * 512],
                                       ps_e[ti * 4 + oc][:], cc,
                                       acc[ti][:, oc * 512:(oc + 1) * 512],
                                       ALU.mult, ALU.add))
                        if last:
                            pl.inc("vector", "ve", lam)
                        else:
                            pl.run("vector", lam)
                comb_ve[(pair, e)] = pl.ctr["ve"].v
            for ti in range(2):
                t = t0 + ti
                if pair > 0:
                    pl.wait("sync", "st", x1_st[-2] if len(x1_st) >= 2 else 0)
                pl.dma("sync", "ld", xb[ti][:], x_d[t * P:(t + 1) * P, :])
                pl.dma("sync", "ld", ssb[ti][:], s_d[t * P:(t + 1) * P, :])
                ld_xs = pl.ctr["ld"].v
                pl.wait("scalar", "ve", comb_ve[(pair, n_experts - 1)])
                pl.inc("scalar", "se", lambda s, ti=ti: s.activation(
                    g_sb[ti][:], acc[ti][:], AF.Silu))
                se_g = pl.ctr["se"].v
                silu_se_last = se_g
                pl.wait("vector", "se", se_g)
                pl.wait("vector", "ld", ld_xs)
                pl.inc("vector", "ve", lambda v, ti=ti: v.tensor_mul(
                    tsg[:], ssb[ti][:], g_sb[ti][:]))
                pl.wait("vector", "ve")
                pl.inc("vector", "ve", lambda v, ti=ti: v.tensor_add(
                    x1b[ti][:], xb[ti][:], tsg[:]))
                ve_x1 = pl.ctr["ve"].v
                pl.wait("sync", "ve", ve_x1)
                pl.dma("sync", "st", x1_o[t * P:(t + 1) * P, :], x1b[ti][:])
                x1_st.append(pl.ctr["st"].v)

        pl.wait("sync", "st", pl.ctr["st"].v)
        pl.finalize(block)
    return nc


def _prep_C(inputs, outA, outB):
    x = np.asarray(inputs["L0_attn_x"])
    wg = np.asarray(inputs["L0_ffn_wg"])
    W = np.asarray(inputs["L0_ffn_W"])
    W_t = np.ascontiguousarray(np.transpose(W, (0, 2, 1)))
    b = np.ascontiguousarray(np.asarray(inputs["L0_ffn_b"]).reshape(1, 8 * D))
    maps = []
    for c in range(NCORE):
        b_, h = divmod(c, 2)
        maps.append({
            "xn_fm": outA[c]["xn_fm"],
            "x_i": np.ascontiguousarray(x[b_, h * HALF:(h + 1) * HALF, :]),
            "s_i": outB[c]["s_o"],
            "wg": np.ascontiguousarray(
                wg.reshape(NC, P, 8).transpose(1, 0, 2).reshape(P, P)),
            "W_t": W_t, "b_moe": b,
        })
    return maps



def build_D():
    nc = _bass()
    x1_d = nc.dram_tensor("x1_i", [HALF, D], F32, kind="ExternalInput").ap()
    wq_d = nc.dram_tensor("wq_t", [D, D], F32, kind="ExternalInput").ap()
    wk_d = nc.dram_tensor("wk_t", [D, D], F32, kind="ExternalInput").ap()
    wv_d = nc.dram_tensor("wv_t", [D, D], F32, kind="ExternalInput").ap()
    qnw_d = nc.dram_tensor("qnw", [P, 1], F32, kind="ExternalInput").ap()
    knw_d = nc.dram_tensor("knw", [P, 1], F32, kind="ExternalInput").ap()
    qnwr_d = nc.dram_tensor("qnw_r", [1, P], F32, kind="ExternalInput").ap()
    knwr_d = nc.dram_tensor("knw_r", [1, P], F32, kind="ExternalInput").ap()
    w1s_d = nc.dram_tensor("w1_sh", [D, HALF], F32, kind="ExternalInput").ap()
    w2s_d = nc.dram_tensor("w2_sh", [8192, 256], F32, kind="ExternalInput").ap()
    ropeC_d = nc.dram_tensor("ropeC", [P, HALF], F32, kind="ExternalInput").ap()
    ropeS_d = nc.dram_tensor("ropeS", [P, HALF], F32, kind="ExternalInput").ap()
    ident_d = nc.dram_tensor("ident", [P, P], F32, kind="ExternalInput").ap()
    consts_d = nc.dram_tensor("consts", [P, 4], F32, kind="ExternalInput").ap()
    xn_fm_o = nc.dram_tensor("xn1_fm", [NC, P, HALF], BF16, kind="ExternalOutput").ap()
    q_o = nc.dram_tensor("q1_o", [HEADS, P, HALF], BF16, kind="ExternalOutput").ap()
    k_o = nc.dram_tensor("k1_o", [HEADS, P, HALF], BF16, kind="ExternalOutput").ap()
    v_o = nc.dram_tensor("v1_o", [HALF, D], BF16, kind="ExternalOutput").ap()
    negc_o = nc.dram_tensor("negc_o", [1, 1], F32, kind="ExternalOutput").ap()
    parts_o = nc.dram_tensor("parts_o", [1, 2], F32, kind="ExternalOutput").ap()

    with contextlib.ExitStack() as es:
        E = es.enter_context
        xb = [E(nc.sbuf_tensor(f"xb{i}", [P, D], F32)) for i in range(2)]
        sq_scr = E(nc.sbuf_tensor("sq_scr", [P, D], F32))
        stats = E(nc.sbuf_tensor("stats", [P, 4], F32))
        consts = E(nc.sbuf_tensor("consts_sb", [P, 4], F32))
        xn_tm = E(nc.sbuf_tensor("xn_tm", [P, D], BF16))
        xn_fm = E(nc.sbuf_tensor("xn_fm_sb", [P, NC * HALF], BF16))
        idb = E(nc.sbuf_tensor("idb", [P, P], BF16))
        wbuf = E(nc.sbuf_tensor("wbuf", [P, NC * D], BF16))
        nrm = E(nc.sbuf_tensor("nrm", [P, 2], F32))
        nrow = E(nc.sbuf_tensor("nrow", [8, P], F32))
        ones_c = E(nc.sbuf_tensor("ones_c", [P, 8], F32))
        ones1r = E(nc.sbuf_tensor("ones1r", [8, P], F32))
        ropeC = E(nc.sbuf_tensor("ropeC_sb", [P, HALF], BF16))
        ropeS = E(nc.sbuf_tensor("ropeS_sb", [P, HALF], BF16))
        qf32 = [E(nc.sbuf_tensor(f"qf32_{i}", [P, HALF], F32)) for i in range(2)]
        q2 = E(nc.sbuf_tensor("q2", [P, HALF], F32))
        rln = E(nc.sbuf_tensor("rln", [8, HALF], F32))
        rexp = E(nc.sbuf_tensor("rexp", [8, HALF], F32))
        cstat = E(nc.sbuf_tensor("cstat", [8, 8], F32))
        qn_bf = E(nc.sbuf_tensor("qn_bf", [P, HALF], BF16))
        qout = [E(nc.sbuf_tensor(f"qout{i}", [P, HALF], BF16)) for i in range(2)]
        sw = E(nc.sbuf_tensor("sw", [P, HALF], BF16))
        t1 = E(nc.sbuf_tensor("t1", [P, HALF], BF16))
        t2 = E(nc.sbuf_tensor("t2", [P, HALF], BF16))
        vbuf = [E(nc.sbuf_tensor(f"vbuf{i}", [P, 512], BF16)) for i in range(2)]
        wstat = E(nc.sbuf_tensor("wstat", [P, 96], F32))
        wtile = [E(nc.sbuf_tensor(f"wtile{i}", [P, HALF], F32)) for i in range(2)]

        ps_q = [E(nc.psum_tensor(f"ps_q{i}", [P, 512], F32)) for i in range(4)]
        ps_r = [E(nc.psum_tensor(f"ps_r{i}", [P, 512], F32)) for i in range(2)]
        ps_tr = [E(nc.psum_tensor(f"ps_tr{i}", [P, 512], BF16)) for i in range(2)]

        pl = Plan(nc, es, ["ld_x", "ld_w", "ld_c", "ve", "se", "ge", "te", "st"])
        block = E(nc.Block())

        xn_fm_r = xn_fm[:].rearrange("p (c q) -> p c q", q=HALF)

        pl.dma("sync", "ld_c", consts[:], consts_d)
        pl.dma("sync", "ld_c", nrm[:, 0:1], qnw_d)
        pl.dma("sync", "ld_c", nrm[:, 1:2], knw_d)
        pl.dma("sync", "ld_c", nrow[0:1, :], qnwr_d)
        pl.dma("sync", "ld_c", nrow[1:2, :], knwr_d)
        pl.dma("gpsimd", "ld_c", idb[:], ident_d)
        pl.dma("gpsimd", "ld_c", ropeC[:], ropeC_d)
        pl.dma("gpsimd", "ld_c", ropeS[:], ropeS_d)
        LD_C_ALL = pl.ctr["ld_c"].v
        pl.inc("gpsimd", "ge", lambda g: g.memset(ones_c[:], 1.0))
        GE_ONES = pl.ctr["ge"].v
        eps_ln_col = consts[:, 0:1]
        eps_rms_col = consts[:, 1:2]

        pl.wait("scalar", "ld_c", LD_C_ALL)
        pl.inc("scalar", "se", lambda s: s.activation(
            sq_scr[0:2, 0:P], nrow[0:2, :], AF.Square, accum_out=cstat[0:2, 1:2]))
        pl.wait("scalar", "se")
        pl.inc("scalar", "se", lambda s: s.activation(
            cstat[0:2, 2:3], cstat[0:2, 1:2], AF.Ln))
        SE_LNQ = pl.ctr["se"].v
        pl.wait("tensor", "se", SE_LNQ)
        pl.wait("tensor", "ge", GE_ONES)
        pl.inc("tensor", "te", lambda e: e.matmul(
            ps_r[0][0:1, 2:3], ones_c[0:2, 3:4], cstat[0:2, 2:3],
            start=True, stop=True))
        TE_LNSUM = pl.ctr["te"].v
        pl.wait("scalar", "te", TE_LNSUM)
        pl.inc("scalar", "se", lambda s: s.activation(
            cstat[0:1, 5:6], ps_r[0][0:1, 2:3], AF.Exp, scale=0.5))
        pl.wait("vector", "se")
        pl.inc("vector", "ve", lambda v: v.tensor_scalar(
            cstat[0:1, 6:7], cstat[0:1, 5:6], -8.0 / P, -4.0, ALU.mult, ALU.add))
        NEGC_VE = pl.ctr["ve"].v
        pl.wait("sync", "ve", NEGC_VE)
        pl.dma("sync", "st", negc_o, cstat[0:1, 6:7])

        wabs_se = []
        for i in range(NC):
            pl.dma("gpsimd", "ld_w", wtile[i % 2][:], w1s_d[i * P:(i + 1) * P, :])
            lv = pl.ctr["ld_w"].v
            if i >= 2:
                pl.wait("gpsimd", "se", wabs_se[i - 2])
            pl.wait("scalar", "ld_w", lv)
            pl.inc("scalar", "se", lambda s, i=i: s.activation(
                sq_scr[:, 0:HALF], wtile[i % 2][:], AF.Abs,
                accum_out=wstat[:, i:i + 1]))
            wabs_se.append(pl.ctr["se"].v)
        w2r = w2s_d.rearrange("(n p) c -> n p c", p=P)
        for i in range(64):
            pl.wait("gpsimd", "se", wabs_se[NC + i - 2])
            pl.dma("gpsimd", "ld_w", wtile[i % 2][:, 0:256], w2r[i, :, :])
            lv = pl.ctr["ld_w"].v
            pl.wait("scalar", "ld_w", lv)
            pl.inc("scalar", "se", lambda s, i=i: s.activation(
                sq_scr[:, 0:256], wtile[i % 2][:, 0:256], AF.Abs,
                accum_out=wstat[:, 24 + i:25 + i]))
            wabs_se.append(pl.ctr["se"].v)
        WABS_SE = pl.ctr["se"].v
        pl.wait("vector", "se", WABS_SE)
        pl.run("vector", lambda v: v.reduce_sum(
            wstat[:, 16:17], wstat[:, 0:NC], axis=AX.X))
        pl.inc("vector", "ve", lambda v: v.reduce_sum(
            wstat[:, 88:89], wstat[:, 24:88], axis=AX.X))
        WSUM_VE = pl.ctr["ve"].v
        pl.wait("tensor", "ve", WSUM_VE)
        pl.wait("tensor", "ge", GE_ONES)
        pl.run("tensor", lambda e: e.matmul(
            ps_r[0][0:1, 0:1], ones_c[:, 0:1], wstat[:, 16:17],
            start=True, stop=True))
        pl.inc("tensor", "te", lambda e: e.matmul(
            ps_r[0][0:1, 1:2], ones_c[:, 1:2], wstat[:, 88:89],
            start=True, stop=True))
        WP_TE = pl.ctr["te"].v
        pl.wait("vector", "te", WP_TE)
        pl.inc("vector", "ve", lambda v: v.tensor_copy(
            wstat[0:1, 90:92], ps_r[0][0:1, 0:2]))
        WP_VE = pl.ctr["ve"].v
        pl.wait("sync", "ve", WP_VE)
        pl.dma("sync", "st", parts_o, wstat[0:1, 90:92])

        xn_done_ve = []
        stt_ve_l = []
        tr_batch_te = []
        tr_evac_ve = []
        evac_done_ve = []
        for t in range(NT):
            if t >= 2:
                pl.wait("sync", "ve", xn_done_ve[t - 2])
            pl.dma("sync", "ld_x", xb[t % 2][:], x1_d[t * P:(t + 1) * P, :])
            ldv = pl.ctr["ld_x"].v
            v_xn, v_stt = plan_ln(
                pl, xb[t % 2], stats, sq_scr, xn_tm, eps_ln_col, ("ld_x", ldv),
                stats_war_ve=(stt_ve_l[-1] if stt_ve_l else None),
                xn_war=(("te", tr_batch_te[-1]) if tr_batch_te else None))
            xn_done_ve.append(v_xn)
            stt_ve_l.append(v_stt)
            pl.wait("tensor", "ve", v_xn)
            if t == 0:
                pl.wait("tensor", "ld_c", LD_C_ALL)
            for cb in range(4):
                j = (t * 4 + cb) % 2
                if t * 4 + cb >= 2:
                    pl.wait("tensor", "ve", tr_evac_ve[(t * 4 + cb) - 2])
                for cc in range(4):
                    c = cb * 4 + cc
                    if cc < 3:
                        pl.run("tensor", lambda e, j=j, cc=cc, c=c: e.transpose(
                            ps_tr[j][:, cc * P:(cc + 1) * P],
                            xn_tm[:, c * P:(c + 1) * P], idb[:]))
                    else:
                        pl.inc("tensor", "te", lambda e, j=j, cc=cc, c=c: e.transpose(
                            ps_tr[j][:, cc * P:(cc + 1) * P],
                            xn_tm[:, c * P:(c + 1) * P], idb[:]))
                te_v = pl.ctr["te"].v
                tr_batch_te.append(te_v)
                pl.wait("vector", "te", te_v)
                dst = xn_fm_r[:, cb * 4:(cb + 1) * 4, t * P:(t + 1) * P]
                src_ = ps_tr[j][:].rearrange("p (c q) -> p c q", q=P)
                pl.inc("vector", "ve", lambda v, d=dst, s=src_: v.tensor_copy(d, s))
                tr_evac_ve.append(pl.ctr["ve"].v)
            evac_done_ve.append(pl.ctr["ve"].v)
        XN_ALL_VE = evac_done_ve[-1]
        pl.wait("gpsimd", "ve", XN_ALL_VE)
        for c in range(NC):
            pl.dma("gpsimd", "st", xn_fm_o[c, :, :], xn_fm[:, c * HALF:(c + 1) * HALF])

        def load_w(which_d, done_te_of_prev_phase):
            if done_te_of_prev_phase is not None:
                pl.wait("gpsimd", "te", done_te_of_prev_phase)
            vals = []
            for r in range(NC):
                pl.dma("gpsimd", "ld_w", wbuf[:, r * D:(r + 1) * D],
                       which_d[r * P:(r + 1) * P, :])
                vals.append(pl.ctr["ld_w"].v)
            return vals

        wq_lds = load_w(wq_d, None)

        def qk_heads(nw_col, lds, out_dram):
            evac_ve = []
            out_ve = []
            st_l = []
            ge_sw_l = []
            prev_qn_ve = None
            for hd in range(HEADS):
                sl = hd % 2
                if hd == 0:
                    pl.wait("tensor", "ld_w", lds[NC - 1])
                    pl.wait("tensor", "ve", XN_ALL_VE)
                if hd >= 2:
                    pl.wait("tensor", "ve", evac_ve[hd - 2])
                for qc in range(2):
                    for c in range(NC):
                        lhsT = wbuf[:, c * D + hd * P: c * D + (hd + 1) * P]
                        rhs = xn_fm_r[:, c, qc * 512:(qc + 1) * 512]
                        if c < NC - 1:
                            pl.run("tensor", lambda e, o=ps_q[sl * 2 + qc][:],
                                   l=lhsT, r=rhs, st_=(c == 0):
                                   e.matmul(o, l, r, start=st_, stop=False))
                        else:
                            pl.inc("tensor", "te", lambda e, o=ps_q[sl * 2 + qc][:],
                                   l=lhsT, r=rhs:
                                   e.matmul(o, l, r, start=False, stop=True))
                te_head = pl.ctr["te"].v
                pl.wait("vector", "te", te_head)
                if hd >= 2:
                    pl.wait("vector", "ve", out_ve[hd - 2])
                    pl.wait("vector", "ge", ge_sw_l[hd - 2])
                for qc in range(2):
                    lam = (lambda v, sl=sl, qc=qc: v.tensor_copy(
                        qf32[sl][:, qc * 512:(qc + 1) * 512], ps_q[sl * 2 + qc][:]))
                    if qc == 1:
                        pl.inc("vector", "ve", lam)
                    else:
                        pl.run("vector", lam)
                ve_q0 = pl.ctr["ve"].v
                evac_ve.append(ve_q0)
                pl.wait("scalar", "ve", ve_q0)
                pl.inc("scalar", "se", lambda s, sl=sl: s.activation(
                    q2[:], qf32[sl][:], AF.Square))
                se_q2 = pl.ctr["se"].v
                pl.wait("tensor", "se", se_q2)
                if prev_qn_ve is not None:
                    pl.wait("tensor", "ve", prev_qn_ve)
                for i in range(2):
                    lam = (lambda e, i=i: e.matmul(
                        ps_r[i][0:1, :], ones_c[:, 2:3], q2[:, i * 512:(i + 1) * 512],
                        start=True, stop=True))
                    if i == 1:
                        pl.inc("tensor", "te", lam)
                    else:
                        pl.run("tensor", lam)
                te_ssq = pl.ctr["te"].v
                pl.wait("scalar", "te", te_ssq)
                for i in range(2):
                    lam = (lambda s, i=i: s.activation(
                        rln[0:1, i * 512:(i + 1) * 512], ps_r[i][0:1, :],
                        AF.Ln, bias=eps_rms_col[0:1], scale=1.0 / P))
                    if i == 1:
                        pl.inc("scalar", "se", lam)
                    else:
                        pl.run("scalar", lam)
                pl.wait("scalar", "se")
                for i in range(2):
                    lam = (lambda s, i=i: s.activation(
                        rexp[0:1, i * 512:(i + 1) * 512],
                        rln[0:1, i * 512:(i + 1) * 512], AF.Exp, scale=-0.5))
                    if i == 1:
                        pl.inc("scalar", "se", lam)
                    else:
                        pl.run("scalar", lam)
                se_rstd = pl.ctr["se"].v
                pl.wait("tensor", "se", se_rstd)
                for i in range(2):
                    lam = (lambda e, i=i: e.matmul(
                        ps_r[i][:], ones1r[0:1, :],
                        rexp[0:1, i * 512:(i + 1) * 512],
                        start=True, stop=True))
                    if i == 1:
                        pl.inc("tensor", "te", lam)
                    else:
                        pl.run("tensor", lam)
                te_bc = pl.ctr["te"].v
                pl.wait("vector", "te", te_bc)
                if ge_sw_l:
                    pl.wait("vector", "ge", ge_sw_l[-1])
                for i in range(2):
                    lam = (lambda v, sl=sl, i=i: v.scalar_tensor_tensor(
                        qn_bf[:, i * 512:(i + 1) * 512],
                        qf32[sl][:, i * 512:(i + 1) * 512], nw_col, ps_r[i][:],
                        ALU.mult, ALU.mult))
                    if i == 1:
                        pl.inc("vector", "ve", lam)
                    else:
                        pl.run("vector", lam)
                ve_qn = pl.ctr["ve"].v
                prev_qn_ve = ve_qn
                if len(st_l) >= 2:
                    pl.wait("vector", "st", st_l[-2])
                    pl.wait("gpsimd", "st", st_l[-2])
                v_out, ge_sw = plan_rope(pl, qn_bf, qout[sl], sw, t1, t2,
                                         ropeC, ropeS, ve_qn,
                                         t2_war_ve=(out_ve[-1] if out_ve else None))
                ge_sw_l.append(ge_sw)
                out_ve.append(v_out)
                pl.wait("sync", "ve", v_out)
                pl.dma("sync", "st", out_dram[hd, :, :], qout[sl][:])
                st_l.append(pl.ctr["st"].v)
            return pl.ctr["te"].v

        pl.inc("gpsimd", "ge", lambda g: g.memset(ones1r[0:1, :], 1.0))
        GE_ONES2 = pl.ctr["ge"].v
        pl.wait("tensor", "ge", GE_ONES2)

        te_q = qk_heads(nrm[:, 0:1], wq_lds, q_o)
        wk_lds = load_w(wk_d, te_q)
        te_k = qk_heads(nrm[:, 1:2], wk_lds, k_o)
        wv_lds = load_w(wv_d, te_k)

        v_evac_ve = []
        v_st = []
        for t in range(NT):
            for oc in range(4):
                if t == 0 and oc == 0:
                    pl.wait("tensor", "ld_w", wv_lds[NC - 1])
                sl = (t * 4 + oc) % 2
                if t * 4 + oc >= 2:
                    pl.wait("tensor", "ve", v_evac_ve[t * 4 + oc - 2])
                for c in range(NC):
                    lhsT = xn_fm_r[:, c, t * P:(t + 1) * P]
                    rhs = wbuf[:, c * D + oc * 512: c * D + (oc + 1) * 512]
                    if c < NC - 1:
                        pl.run("tensor", lambda e, o=ps_q[sl][:], l=lhsT, r=rhs,
                               st_=(c == 0): e.matmul(o, l, r, start=st_, stop=False))
                    else:
                        pl.inc("tensor", "te", lambda e, o=ps_q[sl][:], l=lhsT, r=rhs:
                               e.matmul(o, l, r, start=False, stop=True))
                te_v = pl.ctr["te"].v
                pl.wait("vector", "te", te_v)
                if t * 4 + oc >= 2:
                    pl.wait("vector", "st", v_st[t * 4 + oc - 2])
                pl.inc("vector", "ve", lambda v, sl=sl: v.tensor_copy(
                    vbuf[sl][:], ps_q[sl][:]))
                v_evac_ve.append(pl.ctr["ve"].v)
                pl.wait("sync", "ve", v_evac_ve[-1])
                pl.dma("sync", "st",
                       v_o[t * P:(t + 1) * P, oc * 512:(oc + 1) * 512], vbuf[sl][:])
                v_st.append(pl.ctr["st"].v)

        pl.wait("sync", "st", pl.ctr["st"].v)
        pl.finalize(block)
    return nc


def _prep_D(inputs, outC):
    wq_t = np.ascontiguousarray(np.asarray(inputs["L1_attn_Wq"]).T)
    wk_t = np.ascontiguousarray(np.asarray(inputs["L1_attn_Wk"]).T)
    wv_t = np.ascontiguousarray(np.asarray(inputs["L1_attn_Wv"]).T)
    w1_t = np.ascontiguousarray(np.asarray(inputs["L1_ffn_w1"]).T)
    w2_t = np.ascontiguousarray(np.asarray(inputs["L1_ffn_w2"]).T)
    qnw = np.asarray(inputs["L1_attn_q_norm_w"]).reshape(P, 1)
    knw = np.asarray(inputs["L1_attn_k_norm_w"]).reshape(P, 1)
    ident = np.eye(P, dtype=np.float32)
    consts = np.zeros((P, 4), dtype=np.float32)
    consts[:, 0] = LN_EPS
    consts[:, 1] = RMS_EPS
    maps = []
    for c in range(NCORE):
        b, h = divmod(c, 2)
        Cf, Sf = _rope_tables(h)
        maps.append({
            "x1_i": outC[c]["x1_o"],
            "wq_t": wq_t, "wk_t": wk_t, "wv_t": wv_t,
            "qnw": np.ascontiguousarray(qnw), "knw": np.ascontiguousarray(knw),
            "qnw_r": np.ascontiguousarray(qnw.reshape(1, P)),
            "knw_r": np.ascontiguousarray(knw.reshape(1, P)),
            "w1_sh": np.ascontiguousarray(w1_t[:, c * HALF:(c + 1) * HALF]),
            "w2_sh": np.ascontiguousarray(w2_t[:, c * 256:(c + 1) * 256]),
            "ropeC": Cf, "ropeS": Sf, "ident": ident, "consts": consts,
        })
    return maps, w1_t, w2_t



def build_E():
    nc = _bass()
    q_d = nc.dram_tensor("q_i", [HEADS, P, HALF], BF16, kind="ExternalInput").ap()
    k_d = nc.dram_tensor("k_i", [HEADS, P, S], BF16, kind="ExternalInput").ap()
    v_d = nc.dram_tensor("v_i", [HEADS, S, DH], BF16, kind="ExternalInput").ap()
    wo_d = nc.dram_tensor("wo_t", [D, D], F32, kind="ExternalInput").ap()
    mask_d = nc.dram_tensor("mask", [S, HALF], BF16, kind="ExternalInput").ap()
    negc_d = nc.dram_tensor("negc", [P, 1], F32, kind="ExternalInput").ap()
    s_o = nc.dram_tensor("s_o", [HALF, D], BF16, kind="ExternalOutput").ap()

    with contextlib.ExitStack() as es:
        E = es.enter_context
        q_sb = E(nc.sbuf_tensor("q_sb", [P, HEADS * HALF], BF16))
        k_sb = [E(nc.sbuf_tensor(f"k_sb{i}", [P, S], BF16)) for i in range(2)]
        v_sb = [E(nc.sbuf_tensor(f"v_sb{i}", [P, NKT * DH], BF16)) for i in range(2)]
        mask_sb = E(nc.sbuf_tensor("mask_sb", [P, NKT * HALF], BF16))
        negc = E(nc.sbuf_tensor("negc_sb", [P, 1], F32))
        ones_bf = E(nc.sbuf_tensor("ones_bf", [P, P], BF16))
        pt = [E(nc.sbuf_tensor(f"pt{i}", [P, HALF], BF16)) for i in range(2)]
        rec = E(nc.sbuf_tensor("rec", [P, HALF], F32))
        y_fm = E(nc.sbuf_tensor("y_fm", [P, HEADS * HALF], BF16))
        wo_sb = [E(nc.sbuf_tensor(f"wo_sb{i}", [P, D], BF16)) for i in range(2)]
        s_sb = [E(nc.sbuf_tensor(f"s_sb{i}", [P, D], BF16)) for i in range(2)]

        ps_s = [E(nc.psum_tensor(f"ps_s{i}", [P, 512], F32)) for i in range(4)]
        ps_pv = [E(nc.psum_tensor(f"ps_pv{i}", [P, 512], F32)) for i in range(2)]
        ps_d = [E(nc.psum_tensor(f"ps_d{i}", [P, 512], F32)) for i in range(2)]

        pl = Plan(nc, es, ["ld", "ldk", "ldw", "ve", "se", "ge", "te", "st"])
        block = E(nc.Block())

        q_r = q_sb[:].rearrange("p (h q) -> p h q", q=HALF)
        mask_r = mask_sb[:].rearrange("p (t q) -> p t q", q=HALF)

        for hh in range(HEADS):
            pl.dma("sync", "ld", q_r[:, hh, :], q_d[hh, :, :])
        for kt in range(NKT):
            pl.dma("sync", "ld", mask_r[:, kt, :], mask_d[kt * P:(kt + 1) * P, :])
        pl.dma("sync", "ld", negc[:], negc_d)
        LD_ALL = pl.ctr["ld"].v
        pl.inc("gpsimd", "ge", lambda g: g.memset(ones_bf[:], 1.0))
        GE_ONES = pl.ctr["ge"].v
        negC_col = negc[:, 0:1]

        kv_ld = []
        def plan_kv_load(h):
            pl.dma("gpsimd", "ldk", k_sb[h % 2][:], k_d[h, :, :])
            pl.dma("gpsimd", "ldk", v_sb[h % 2][:].rearrange("p (t c) -> p t c", c=DH),
                   v_d[h, :, :].rearrange("(t p) c -> p t c", p=P))
            kv_ld.append(pl.ctr["ldk"].v)
        plan_kv_load(0)
        plan_kv_load(1)

        exp_se = {}
        mask_ve = {}
        pvden_te = {}
        head_evac_ve = []
        pl.wait("scalar", "ld", LD_ALL)
        pl.wait("tensor", "ge", GE_ONES)
        for h in range(HEADS):
            sl_h = h % 2
            k_r = k_sb[sl_h]
            v_r = v_sb[sl_h][:].rearrange("p (t c) -> p t c", c=DH)
            pl.wait("tensor", "ldk", kv_ld[h])
            if h == 0:
                pl.wait("tensor", "ld", LD_ALL)
            else:
                pl.wait("tensor", "ve", head_evac_ve[h - 1])
            for kt in range(NKT):
                sl = kt % 2
                prev = (h, kt - 2) if kt >= 2 else ((h - 1, NKT - 1) if h > 0 else None)
                if prev in exp_se:
                    pl.wait("tensor", "se", exp_se[prev])
                for qc in range(2):
                    pl.inc("tensor", "te", lambda e, h=h, kt=kt, qc=qc, sl=sl, k_r=k_r:
                           e.matmul(ps_s[sl * 2 + qc][:],
                                    k_r[:, kt * P:(kt + 1) * P],
                                    q_r[:, h, qc * 512:(qc + 1) * 512],
                                    start=True, stop=True))
                te_s = pl.ctr["te"].v
                pl.wait("scalar", "te", te_s)
                if prev in pvden_te:
                    pl.wait("scalar", "te", pvden_te[prev])
                for qc in range(2):
                    pl.inc("scalar", "se", lambda s, sl=sl, qc=qc: s.activation(
                        pt[sl][:, qc * 512:(qc + 1) * 512], ps_s[sl * 2 + qc][:],
                        AF.Exp, bias=negC_col, scale=INV_SQRT_DH))
                exp_se[(h, kt)] = pl.ctr["se"].v
                pl.wait("vector", "se", exp_se[(h, kt)])
                for qc in range(2):
                    pl.inc("vector", "ve", lambda v, sl=sl, kt=kt, qc=qc:
                           v.tensor_mul(pt[sl][:, qc * 512:(qc + 1) * 512],
                                        pt[sl][:, qc * 512:(qc + 1) * 512],
                                        mask_r[:, kt, qc * 512:(qc + 1) * 512]))
                mask_ve[(h, kt)] = pl.ctr["ve"].v
                pl.wait("tensor", "ve", mask_ve[(h, kt)])
                for qc in range(2):
                    pl.run("tensor", lambda e, kt=kt, sl=sl, qc=qc, v_r=v_r:
                           e.matmul(ps_pv[qc][:],
                                    v_r[:, kt, :],
                                    pt[sl][:, qc * 512:(qc + 1) * 512],
                                    start=(kt == 0), stop=(kt == NKT - 1)))
                for qc in range(2):
                    pl.inc("tensor", "te", lambda e, kt=kt, sl=sl, qc=qc:
                           e.matmul(ps_d[qc][:],
                                    ones_bf[:],
                                    pt[sl][:, qc * 512:(qc + 1) * 512],
                                    start=(kt == 0), stop=(kt == NKT - 1)))
                pvden_te[(h, kt)] = pl.ctr["te"].v
            if h + 2 < HEADS:
                pl.wait("gpsimd", "te", pvden_te[(h, NKT - 1)])
                plan_kv_load(h + 2)
            pl.wait("vector", "te", pvden_te[(h, NKT - 1)])
            pl.run("vector", lambda v: v.reciprocal(rec[:, 0:512], ps_d[0][:]))
            pl.inc("vector", "ve", lambda v: v.reciprocal(
                rec[:, 512:1024], ps_d[1][:]))
            pl.wait("vector", "ve")
            for qc in range(2):
                pl.inc("vector", "ve", lambda v, h=h, qc=qc: v.tensor_mul(
                    y_fm[:, h * HALF + qc * 512: h * HALF + (qc + 1) * 512],
                    ps_pv[qc][:], rec[:, qc * 512:(qc + 1) * 512]))
            head_evac_ve.append(pl.ctr["ve"].v)
        Y_ALL_VE = head_evac_ve[-1]

        y_r = y_fm[:].rearrange("p (h q) -> p h q", q=HALF)

        s_st = []
        epi_ve = {}
        wo_use_te = {}
        for pair in range(NT // 2):
            t0, t1 = pair * 2, pair * 2 + 1
            ps_o = {t0: [ps_s[0], ps_s[1], ps_s[2], ps_s[3]],
                    t1: [ps_pv[0], ps_pv[1], ps_d[0], ps_d[1]]}
            if pair == 0:
                pl.wait("tensor", "ve", Y_ALL_VE)
            else:
                pl.wait("tensor", "ve", epi_ve[t0 - 1])
            for c in range(NC):
                if (pair, c) != (0, 0) and (pair, c) != (0, 1):
                    key = (pair, c - 2) if c >= 2 else (pair - 1, NC - 2 + c)
                    pl.wait("gpsimd", "te", wo_use_te[key])
                pl.dma("gpsimd", "ldw", wo_sb[c % 2][:], wo_d[c * P:(c + 1) * P, :])
                ldw_v = pl.ctr["ldw"].v
                pl.wait("tensor", "ldw", ldw_v)
                for t in (t0, t1):
                    for oc in range(4):
                        last = (t == t1 and oc == 3)
                        lam = (lambda e, c=c, t=t, oc=oc, o=ps_o[t][oc]: e.matmul(
                            o[:], y_r[:, c, t * P:(t + 1) * P],
                            wo_sb[c % 2][:, oc * 512:(oc + 1) * 512],
                            start=(c == 0), stop=(c == NC - 1)))
                        if last:
                            pl.inc("tensor", "te", lam)
                        else:
                            pl.run("tensor", lam)
                wo_use_te[(pair, c)] = pl.ctr["te"].v
            te_pair = pl.ctr["te"].v
            for t in (t0, t1):
                pl.wait("vector", "te", te_pair)
                if t >= 2:
                    pl.wait("vector", "st", s_st[t - 2])
                for oc in range(4):
                    lam = (lambda v, oc=oc, t=t, o=ps_o[t][oc]: v.tensor_copy(
                        s_sb[t % 2][:, oc * 512:(oc + 1) * 512], o[:]))
                    if oc == 3:
                        pl.inc("vector", "ve", lam)
                    else:
                        pl.run("vector", lam)
                epi_ve[t] = pl.ctr["ve"].v
                pl.wait("sync", "ve", epi_ve[t])
                pl.dma("sync", "st", s_o[t * P:(t + 1) * P, :], s_sb[t % 2][:])
                s_st.append(pl.ctr["st"].v)

        pl.wait("sync", "st", pl.ctr["st"].v)
        pl.finalize(block)
    return nc


def _prep_E(inputs, outD):
    wo_t = np.ascontiguousarray(np.asarray(inputs["L1_attn_Wo"]).T)
    maps = []
    for c in range(NCORE):
        b, h = divmod(c, 2)
        k_full = np.concatenate([outD[2 * b]["k1_o"], outD[2 * b + 1]["k1_o"]], axis=2)
        v_full = np.concatenate([outD[2 * b]["v1_o"], outD[2 * b + 1]["v1_o"]], axis=0)
        v_heads = np.ascontiguousarray(
            v_full.reshape(S, HEADS, DH).transpose(1, 0, 2))
        kpos = np.arange(S)[:, None]
        qpos = h * HALF + np.arange(HALF)[None, :]
        mask = ((kpos <= qpos) & (kpos > qpos - 1024)).astype(BFDT)
        negc = np.ascontiguousarray(
            np.repeat(np.asarray(outD[c]["negc_o"]), P, axis=0))
        maps.append({
            "q_i": outD[c]["q1_o"], "k_i": np.ascontiguousarray(k_full),
            "v_i": v_heads, "wo_t": wo_t, "mask": mask, "negc": negc,
        })
    return maps



def build_F1():
    nc = _bass()
    xn_d = nc.dram_tensor("xn1_fm", [NC, P, HALF], BF16, kind="ExternalInput").ap()
    w1_d = nc.dram_tensor("w1_t", [D, 8192], F32, kind="ExternalInput").ap()
    w2_d = nc.dram_tensor("w2_t", [8192, D], F32, kind="ExternalInput").ap()
    b1_d = nc.dram_tensor("b1_r", [P, 64], F32, kind="ExternalInput").ap()
    parts_d = nc.dram_tensor("parts", [8, 2], F32, kind="ExternalInput").ap()
    h_o = nc.dram_tensor("h_o", [8192, HALF], BF16, kind="ExternalOutput").ap()
    w2q_o = nc.dram_tensor("w2q_o", [8192, D], BF16, kind="ExternalOutput").ap()

    NW1 = np.float32(0.7 / (8192.0 * 2048.0))

    with contextlib.ExitStack() as es:
        E = es.enter_context
        xn_fm = E(nc.sbuf_tensor("xn_fm_sb", [P, NC * HALF], BF16))
        w1q = E(nc.sbuf_tensor("w1q", [P, NC * D], BF16))
        wslab = [E(nc.sbuf_tensor(f"wslab{i}", [P, D], F32)) for i in range(2)]
        abuf = E(nc.sbuf_tensor("abuf", [P, D], BF16))
        w2qb = [E(nc.sbuf_tensor(f"w2qb{i}", [P, D], BF16)) for i in range(2)]
        b1_sb = E(nc.sbuf_tensor("b1_sb", [P, 64], F32))
        parts_sb = E(nc.sbuf_tensor("parts_sb", [8, 2], F32))
        qstat = E(nc.sbuf_tensor("qstat", [P, 4], F32))
        ones_c = E(nc.sbuf_tensor("ones_c", [P, 8], F32))
        ones_cb = E(nc.sbuf_tensor("ones_cb", [P, 8], BF16))
        ones_rb = E(nc.sbuf_tensor("ones_rb", [8, P], BF16))
        ones_rf = E(nc.sbuf_tensor("ones_rf", [8, P], F32))
        nxs = E(nc.sbuf_tensor("nxs", [8, HALF], BF16))
        h_sb = [E(nc.sbuf_tensor(f"h_sb{i}", [P, HALF], BF16)) for i in range(2)]

        ps_h = [E(nc.psum_tensor(f"ps_h{i}", [P, 512], F32)) for i in range(4)]
        ps_x = [E(nc.psum_tensor(f"ps_x{i}", [P, 512], F32)) for i in range(2)]

        pl = Plan(nc, es, ["ld", "ldw", "ve", "se", "ge", "te", "st"])
        block = E(nc.Block())

        xn_r = xn_fm[:].rearrange("p (c q) -> p c q", q=HALF)

        for c in range(NC):
            pl.dma("sync", "ld", xn_r[:, c, :], xn_d[c, :, :])
        pl.dma("sync", "ld", b1_sb[:], b1_d)
        pl.dma("sync", "ld", parts_sb[:], parts_d)
        LD_ALL = pl.ctr["ld"].v
        pl.run("gpsimd", lambda g: g.memset(ones_c[:], 1.0))
        pl.run("gpsimd", lambda g: g.memset(ones_cb[:], 1.0))
        pl.run("gpsimd", lambda g: g.memset(ones_rf[0:1, :], 1.0))
        pl.inc("gpsimd", "ge", lambda g: g.memset(ones_rb[0:1, :], 1.0))
        GE_ONES = pl.ctr["ge"].v

        pl.wait("tensor", "ld", LD_ALL)
        pl.wait("tensor", "ge", GE_ONES)
        pl.inc("tensor", "te", lambda e: e.matmul(
            ps_x[0][0:1, 0:2], ones_c[0:8, 0:1], parts_sb[:],
            start=True, stop=True))
        TE_TOT = pl.ctr["te"].v
        pl.wait("vector", "te", TE_TOT)
        pl.inc("vector", "ve", lambda v: v.tensor_scalar(
            qstat[0:1, 0:2], ps_x[0][0:1, 0:2], float(NW1), None, ALU.mult))
        pl.wait("vector", "ve")
        VE_THR_ROW = pl.ctr["ve"].v
        pl.wait("tensor", "ve", VE_THR_ROW)
        pl.inc("tensor", "te", lambda e: e.matmul(
            ps_x[1][:, 0:2], ones_rf[0:1, :], qstat[0:1, 0:2],
            start=True, stop=True))
        TE_BC = pl.ctr["te"].v
        pl.wait("vector", "te", TE_BC)
        pl.run("vector", lambda v: v.tensor_copy(qstat[:, 0:2], ps_x[1][:, 0:2]))
        pl.inc("vector", "ve", lambda v: v.tensor_scalar(
            qstat[:, 2:4], ps_x[1][:, 0:2], -1.0, None, ALU.mult))
        VE_THR = pl.ctr["ve"].v

        thr1 = qstat[:, 0:1]
        nthr1 = qstat[:, 2:3]
        h_st = []
        quant_ve = {}
        ft_te = {}
        gelu_se = {}
        for g in range(4):
            for kc in range(NC):
                idx = g * NC + kc
                if idx >= 2:
                    pl.wait("gpsimd", "ve", quant_ve[idx - 2])
                pl.dma("gpsimd", "ldw", wslab[idx % 2][:],
                       w1_d[kc * P:(kc + 1) * P, g * D:(g + 1) * D])
                lv = pl.ctr["ldw"].v
                pl.wait("vector", "ldw", lv)
                if idx == 0:
                    pl.wait("vector", "ve", VE_THR)
                if g > 0 and kc == 0:
                    pl.wait("vector", "te", ft_te[(g - 1, NC - 1)])
                pl.run("vector", lambda v, i=idx % 2: v.tensor_scalar(
                    abuf[:], wslab[i][:], thr1, None, ALU.is_ge))
                pl.inc("vector", "ve", lambda v, i=idx % 2, kc=kc:
                       v.scalar_tensor_tensor(
                           w1q[:, kc * D:(kc + 1) * D], wslab[i][:], nthr1,
                           abuf[:], ALU.is_le, ALU.subtract))
                quant_ve[idx] = pl.ctr["ve"].v
            for ft in range(NC):
                gf = g * NC + ft
                sl = ft % 2
                if gf >= 2:
                    pl.wait("tensor", "se", gelu_se[gf - 2])
                if ft == 0:
                    pl.wait("tensor", "ve", quant_ve[g * NC + NC - 1])
                for qc in range(2):
                    for kc in range(NC):
                        last = (qc == 1 and kc == NC - 1)
                        lam = (lambda e, kc=kc, ft=ft, qc=qc, sl=sl: e.matmul(
                            ps_h[sl * 2 + qc][:],
                            w1q[:, kc * D + ft * P: kc * D + (ft + 1) * P],
                            xn_r[:, kc, qc * 512:(qc + 1) * 512],
                            start=(kc == 0), stop=(kc == NC - 1)))
                        if last:
                            pl.inc("tensor", "te", lam)
                        else:
                            pl.run("tensor", lam)
                ft_te[(g, ft)] = pl.ctr["te"].v
                pl.wait("scalar", "te", ft_te[(g, ft)])
                if gf >= 2:
                    pl.wait("scalar", "st", h_st[gf - 2])
                for qc in range(2):
                    lam = (lambda s, qc=qc, sl=sl, gf=gf: s.activation(
                        h_sb[sl][:, qc * 512:(qc + 1) * 512], ps_h[sl * 2 + qc][:],
                        AF.Gelu, bias=b1_sb[:, gf:gf + 1], scale=-1.0))
                    if qc == 1:
                        pl.inc("scalar", "se", lam)
                    else:
                        pl.run("scalar", lam)
                gelu_se[gf] = pl.ctr["se"].v
                pl.wait("sync", "se", gelu_se[gf])
                pl.dma("sync", "st", h_o[gf * P:(gf + 1) * P, :], h_sb[sl][:])
                h_st.append(pl.ctr["st"].v)

        thr2 = qstat[:, 1:2]
        nthr2 = qstat[:, 3:4]
        w2_st = []
        w2q_ve = {}
        for fc in range(64):
            pl.wait("gpsimd", "ve", quant_ve[4 * NC - 2 + fc] if fc < 2 else
                    w2q_ve[fc - 2])
            pl.dma("gpsimd", "ldw", wslab[fc % 2][:],
                   w2_d[fc * P:(fc + 1) * P, :])
            lv = pl.ctr["ldw"].v
            pl.wait("vector", "ldw", lv)
            if fc >= 2:
                pl.wait("vector", "st", w2_st[fc - 2])
            pl.run("vector", lambda v, i=fc % 2: v.tensor_scalar(
                abuf[:], wslab[i][:], thr2, None, ALU.is_ge))
            pl.inc("vector", "ve", lambda v, i=fc % 2: v.scalar_tensor_tensor(
                w2qb[i][:], wslab[i][:], nthr2, abuf[:], ALU.is_le, ALU.subtract))
            w2q_ve[fc] = pl.ctr["ve"].v
            pl.wait("sync", "ve", w2q_ve[fc])
            pl.dma("sync", "st", w2q_o[fc * P:(fc + 1) * P, :], w2qb[fc % 2][:])
            w2_st.append(pl.ctr["st"].v)

        pl.wait("sync", "st", pl.ctr["st"].v)
        pl.finalize(block)
    return nc



def build_F2():
    nc = _bass()
    h_d = nc.dram_tensor("h_i", [8192, HALF], BF16, kind="ExternalInput").ap()
    w2q_d = nc.dram_tensor("w2q", [8192, D], BF16, kind="ExternalInput").ap()
    b2_d = nc.dram_tensor("b2_r", [1, D], F32, kind="ExternalInput").ap()
    x1_d = nc.dram_tensor("x1_i", [HALF, D], F32, kind="ExternalInput").ap()
    s1_d = nc.dram_tensor("s1_i", [HALF, D], BF16, kind="ExternalInput").ap()
    out_o = nc.dram_tensor("out_o", [HALF, D], F32, kind="ExternalOutput").ap()

    with contextlib.ExitStack() as es:
        E = es.enter_context
        h_sb = E(nc.sbuf_tensor("h_sb", [P, 64 * HALF], BF16))
        w2qb = [E(nc.sbuf_tensor(f"w2qb{i}", [P, D], BF16)) for i in range(2)]
        b2_sb = E(nc.sbuf_tensor("b2_sb", [8, D], BF16))
        nhs = E(nc.sbuf_tensor("nhs", [8, HALF], BF16))
        ones_cb = E(nc.sbuf_tensor("ones_cb", [P, 8], BF16))
        mones_rb = E(nc.sbuf_tensor("mones_rb", [8, P], BF16))
        g_sb = [E(nc.sbuf_tensor(f"g_sb{i}", [P, D], BF16)) for i in range(2)]
        tsg = E(nc.sbuf_tensor("tsg", [P, D], BF16))
        xb = [E(nc.sbuf_tensor(f"xb{i}", [P, D], F32)) for i in range(2)]
        ssb = [E(nc.sbuf_tensor(f"ssb{i}", [P, D], BF16)) for i in range(2)]
        x2b = [E(nc.sbuf_tensor(f"x2b{i}", [P, D], F32)) for i in range(2)]

        ps = [E(nc.psum_tensor(f"ps{i}", [P, 512], F32)) for i in range(8)]

        pl = Plan(nc, es, ["ld", "ldw", "ve", "se", "ge", "te", "st"])
        block = E(nc.Block())

        h_r = h_sb[:].rearrange("p (c q) -> p c q", q=HALF)

        for c in range(64):
            pl.dma("sync", "ld", h_r[:, c, :], h_d[c * P:(c + 1) * P, :])
        H_LD = pl.ctr["ld"].v
        pl.dma("gpsimd", "ld", b2_sb[0:1, :], b2_d)
        B2_LD = pl.ctr["ld"].v
        pl.run("gpsimd", lambda g: g.memset(ones_cb[:], 1.0))
        pl.inc("gpsimd", "ge", lambda g: g.memset(mones_rb[0:1, :], -1.0))
        GE_ONES = pl.ctr["ge"].v

        out_st = []
        epi_ve = {}
        w2_use_te = {}
        for pair in range(4):
            t0, t1 = pair * 2, pair * 2 + 1
            if pair == 0:
                pl.wait("tensor", "ld", H_LD)
                pl.wait("tensor", "ge", GE_ONES)
            else:
                pl.wait("tensor", "ve", epi_ve[t0 - 1])
            for fc in range(64):
                if (pair, fc) not in [(0, 0), (0, 1)]:
                    key = (pair, fc - 2) if fc >= 2 else (pair - 1, 62 + fc)
                    pl.wait("gpsimd", "te", w2_use_te[key])
                pl.dma("gpsimd", "ldw", w2qb[fc % 2][:],
                       w2q_d[fc * P:(fc + 1) * P, :])
                lv = pl.ctr["ldw"].v
                pl.wait("tensor", "ldw", lv)
                for ti in range(2):
                    t = pair * 2 + ti
                    for oc in range(4):
                        last = (ti == 1 and oc == 3)
                        lam = (lambda e, fc=fc, t=t, ti=ti, oc=oc, i=fc % 2:
                               e.matmul(ps[ti * 4 + oc][:],
                                        h_r[:, fc, t * P:(t + 1) * P],
                                        w2qb[i][:, oc * 512:(oc + 1) * 512],
                                        start=(fc == 0), stop=False))
                        if last:
                            pl.inc("tensor", "te", lam)
                        else:
                            pl.run("tensor", lam)
                w2_use_te[(pair, fc)] = pl.ctr["te"].v
            pl.wait("tensor", "ld", B2_LD)
            for ti in range(2):
                for oc in range(4):
                    last = (ti == 1 and oc == 3)
                    lam = (lambda e, ti=ti, oc=oc: e.matmul(
                        ps[ti * 4 + oc][:], mones_rb[0:1, :],
                        b2_sb[0:1, oc * 512:(oc + 1) * 512],
                        start=False, stop=True))
                    if last:
                        pl.inc("tensor", "te", lam)
                    else:
                        pl.run("tensor", lam)
            te_pair = pl.ctr["te"].v
            for ti in range(2):
                t = pair * 2 + ti
                pl.dma("sync", "ld", xb[ti][:], x1_d[t * P:(t + 1) * P, :])
                pl.dma("sync", "ld", ssb[ti][:], s1_d[t * P:(t + 1) * P, :])
                ld_xs = pl.ctr["ld"].v
                pl.wait("scalar", "te", te_pair)
                if pair > 0:
                    pl.wait("scalar", "st", out_st[2 * (pair - 1) + ti])
                for oc in range(4):
                    lam = (lambda s, ti=ti, oc=oc: s.activation(
                        g_sb[ti][:, oc * 512:(oc + 1) * 512], ps[ti * 4 + oc][:],
                        AF.Silu, scale=-1.0))
                    if oc == 3:
                        pl.inc("scalar", "se", lam)
                    else:
                        pl.run("scalar", lam)
                se_g = pl.ctr["se"].v
                pl.wait("vector", "se", se_g)
                pl.wait("vector", "ld", ld_xs)
                pl.inc("vector", "ve", lambda v, ti=ti: v.tensor_mul(
                    tsg[:], ssb[ti][:], g_sb[ti][:]))
                pl.wait("vector", "ve")
                pl.inc("vector", "ve", lambda v, ti=ti: v.tensor_add(
                    x2b[ti][:], xb[ti][:], tsg[:]))
                epi_ve[t] = pl.ctr["ve"].v
                pl.wait("sync", "ve", epi_ve[t])
                pl.dma("sync", "st", out_o[t * P:(t + 1) * P, :], x2b[ti][:])
                out_st.append(pl.ctr["st"].v)

        pl.wait("sync", "st", pl.ctr["st"].v)
        pl.finalize(block)
    return nc


def _prep_F1(inputs, outD, w1_t, w2_t):
    b1 = np.asarray(inputs["L1_ffn_b1"])
    b1_r = np.ascontiguousarray(b1.reshape(64, P).T)
    parts = np.ascontiguousarray(
        np.concatenate([np.asarray(outD[c]["parts_o"]) for c in range(NCORE)],
                       axis=0))
    maps = []
    for c in range(NCORE):
        maps.append({
            "xn1_fm": outD[c]["xn1_fm"],
            "w1_t": w1_t, "w2_t": w2_t, "b1_r": b1_r, "parts": parts,
        })
    return maps


def _prep_F2(inputs, outC, outE, outF1):
    b2 = np.ascontiguousarray(np.asarray(inputs["L1_ffn_b2"]).reshape(1, D))
    maps = []
    for c in range(NCORE):
        maps.append({
            "h_i": outF1[c]["h_o"],
            "w2q": outF1[c]["w2q_o"],
            "b2_r": b2,
            "x1_i": outC[c]["x1_o"],
            "s1_i": outE[c]["s_o"],
        })
    return maps



def kernel(**inputs):
    cores = list(range(NCORE))
    mapsA = _prep_A(inputs)
    resA = run_bass_kernel_spmd(build_A(), mapsA, core_ids=cores)
    outA = [resA.results[c] for c in cores]
    mapsB = _prep_B(inputs, outA)
    resB = run_bass_kernel_spmd(build_B(), mapsB, core_ids=cores)
    outB = [resB.results[c] for c in cores]
    mapsC = _prep_C2(inputs, outA, outB)
    resC = run_bass_kernel_spmd(build_C2(), mapsC, core_ids=cores)
    outC = [resC.results[c] for c in cores]
    mapsD, w1_t, w2_t = _prep_D(inputs, outC)
    resD = run_bass_kernel_spmd(build_D(), mapsD, core_ids=cores)
    outD = [resD.results[c] for c in cores]
    mapsE = _prep_E(inputs, outD)
    resE = run_bass_kernel_spmd(build_E(), mapsE, core_ids=cores)
    outE = [resE.results[c] for c in cores]
    mapsF1 = _prep_F1(inputs, outD, w1_t, w2_t)
    resF1 = run_bass_kernel_spmd(build_F1(), mapsF1, core_ids=cores)
    outF1 = [resF1.results[c] for c in cores]
    mapsF2 = _prep_F2(inputs, outC, outE, outF1)
    resF2 = run_bass_kernel_spmd(build_F2(), mapsF2, core_ids=cores)
    outF2 = [resF2.results[c] for c in cores]
    out = np.stack([
        np.concatenate([np.asarray(outF2[2 * b]["out_o"]),
                        np.asarray(outF2[2 * b + 1]["out_o"])], axis=0)
        for b in range(B)]).astype(np.float32)
    return out


CAP = 256
NGT = 2 * 8
BIGIDX = float(1 << 20)


def build_C2(debug=False):
    from concourse.bass import IndirectOffsetOnAxis
    I32 = mybir.dt.int32
    nc = _bass()
    xn_d = nc.dram_tensor("xn_fm", [NC, P, HALF], BF16, kind="ExternalInput").ap()
    xntm_d = nc.dram_tensor("xn_tm", [HALF, D], F32, kind="ExternalInput").ap()
    x_d = nc.dram_tensor("x_i", [HALF, D], F32, kind="ExternalInput").ap()
    s_d = nc.dram_tensor("s_i", [HALF, D], BF16, kind="ExternalInput").ap()
    wg_d = nc.dram_tensor("wg", [P, P], F32, kind="ExternalInput").ap()
    W_d = nc.dram_tensor("W_t", [8, D, D], F32, kind="ExternalInput").ap()
    b_d = nc.dram_tensor("b_moe", [1, 8 * D], F32, kind="ExternalInput").ap()
    tokid_d = nc.dram_tensor("tokid", [P, NT], F32, kind="ExternalInput").ap()
    ecol_d = nc.dram_tensor("ecol", [8, 2], F32, kind="ExternalInput").ap()
    x1_o = nc.dram_tensor("x1_o", [HALF, D], F32, kind="ExternalOutput").ap()
    xng_d = nc.dram_tensor("xng", [8 * CAP, D], F32, kind="Internal").ap()
    ids_d = nc.dram_tensor("ids", [8 * CAP, 1], F32, kind="Internal").ap()
    gts_d = nc.dram_tensor("gts", [8 * CAP, 1], F32, kind="Internal").ap()
    moe_d = nc.dram_tensor("moe", [HALF, D], BF16, kind="Internal").ap()
    idxr_d = nc.dram_tensor("idxr", [1, HALF], F32, kind="Internal").ap()
    slot_d = nc.dram_tensor("slotd", [1, HALF], F32, kind="Internal").ap()
    if debug:
        dbg_moe = nc.dram_tensor("dbg_moe", [HALF, D], BF16, kind="ExternalOutput").ap()
        dbg_slot = nc.dram_tensor("dbg_slot", [1, HALF], F32, kind="ExternalOutput").ap()
        dbg_idx = nc.dram_tensor("dbg_idx", [1, HALF], F32, kind="ExternalOutput").ap()
        dbg_gts = nc.dram_tensor("dbg_gts", [P, NGT], F32, kind="ExternalOutput").ap()
        dbg_ids = nc.dram_tensor("dbg_ids", [P, NGT], I32, kind="ExternalOutput").ap()

    with contextlib.ExitStack() as es:
        E = es.enter_context
        xn_fm = E(nc.sbuf_tensor("xn_fm_sb", [P, NC * HALF], BF16))
        xng_fm = E(nc.sbuf_tensor("xng_fm", [P, NC * 8 * P], BF16))
        wg_sb = E(nc.sbuf_tensor("wg_sb", [P, P], BF16))
        bbuf = E(nc.sbuf_tensor("bbuf", [8, D], BF16))
        ones_t = E(nc.sbuf_tensor("ones_row", [8, P], BF16))
        idf = E(nc.sbuf_tensor("idf", [P, P], F32))
        gex = E(nc.sbuf_tensor("gex", [P, 8], F32))
        gs = E(nc.sbuf_tensor("gs", [P, 8], F32))
        idxmat = E(nc.sbuf_tensor("idxmat", [P, 8], F32))
        gatemat = E(nc.sbuf_tensor("gatemat", [P, 8], F32))
        rows8 = E(nc.sbuf_tensor("rows8", [8, P], F32))
        idxrow = E(nc.sbuf_tensor("idxrow", [8, HALF], F32))
        idx8 = E(nc.sbuf_tensor("idx8", [8, HALF], F32))
        flags8 = E(nc.sbuf_tensor("flags8", [8, HALF], F32))
        csum8 = E(nc.sbuf_tensor("csum8", [8, HALF], F32))
        ones8 = E(nc.sbuf_tensor("ones8", [8, HALF], F32))
        slotrow = E(nc.sbuf_tensor("slotrow", [8, HALF], F32))
        sloti = E(nc.sbuf_tensor("sloti", [8, HALF], F32))
        slot8 = E(nc.sbuf_tensor("slot8", [8, P], F32))
        slotmat = E(nc.sbuf_tensor("slotmat", [P, 8], I32))
        tokid = E(nc.sbuf_tensor("tokid_sb", [P, NT], F32))
        xtile = [E(nc.sbuf_tensor(f"xtile{i}", [P, D], F32)) for i in range(2)]
        gtile = [E(nc.sbuf_tensor(f"gtile{i}", [P, D], F32)) for i in range(2)]
        wrow = [E(nc.sbuf_tensor(f"wrow{i}", [P, D], BF16)) for i in range(2)]
        ysb = [E(nc.sbuf_tensor(f"ysb{i}", [P, D], BF16)) for i in range(2)]
        gates_sb = E(nc.sbuf_tensor("gates_sb", [P, NGT], F32))
        ids_sb = E(nc.sbuf_tensor("ids_sb", [P, NGT], I32))
        gt16 = E(nc.sbuf_tensor("gt16", [16, P], F32))
        id16 = E(nc.sbuf_tensor("id16", [16, P], F32))
        zt = E(nc.sbuf_tensor("zt", [P, D], F32))
        g_sb = [E(nc.sbuf_tensor(f"g_sb{i}", [P, D], BF16)) for i in range(2)]
        tsg = E(nc.sbuf_tensor("tsg", [P, D], BF16))
        mtile = [E(nc.sbuf_tensor(f"mtile{i}", [P, D], BF16)) for i in range(2)]
        ssb = [E(nc.sbuf_tensor(f"ssb{i}", [P, D], BF16)) for i in range(2)]
        x1b = [E(nc.sbuf_tensor(f"x1b{i}", [P, D], F32)) for i in range(2)]

        ps_e = [E(nc.psum_tensor(f"ps_e{i}", [P, 512], F32)) for i in range(8)]

        pl = Plan(nc, es, ["ld", "ldw", "ve", "se", "te", "st", "ge", "sc"])
        block = E(nc.Block())

        xn_r = xn_fm[:].rearrange("p (c q) -> p c q", q=HALF)
        xng_r = xng_fm[:].rearrange("p (c g) -> p c g", g=8 * P)

        for c in range(NC):
            pl.dma("sync", "ld", xn_r[:, c, :], xn_d[c, :, :])
        pl.dma("sync", "ld", tokid[:], tokid_d)
        pl.dma("sync", "ld", gs[0:8, 4:6], ecol_d)
        LD_MAIN = pl.ctr["ld"].v
        pl.dma("gpsimd", "ldw", wg_sb[:], wg_d)
        pl.dma("gpsimd", "ldw", idf[:], nc.dram_tensor("identf", [P, P], F32, kind="ExternalInput").ap())
        CONST_LDW = pl.ctr["ldw"].v
        pl.run("gpsimd", lambda g: g.memset(ones_t[0:1, :], 1.0))
        pl.run("gpsimd", lambda g: g.memset(ones8[:], 1.0))
        pl.run("gpsimd", lambda g: g.memset(zt[:], 0.0))
        pl.run("gpsimd", lambda g: g.memset(id16[:], float(1 << 20)))
        pl.inc("gpsimd", "ge", lambda g: g.memset(gt16[:], 0.0))
        GE_INIT = pl.ctr["ge"].v
        pl.wait("gpsimd", "ge", GE_INIT)
        pl.dma("gpsimd", "ldw", ids_d.rearrange("(c p) one -> c (p one)", p=P), id16[:])
        pl.dma("gpsimd", "ldw", gts_d.rearrange("(c p) one -> c (p one)", p=P), gt16[:])
        for i in range(NGT):
            pl.dma("gpsimd", "ldw", xng_d[i * P:(i + 1) * P, :], zt[:])
        for t in range(NT):
            pl.dma("gpsimd", "ldw", moe_d[t * P:(t + 1) * P, :], zt[:])
        PREFILL_LDW = pl.ctr["ldw"].v

        ps_l = ps_e[0][:, 0:8]
        ecol = gs[:, 4:5]
        eC1 = gs[:, 5:6]

        gate_ve = {}
        pl.wait("tensor", "ld", LD_MAIN)
        pl.wait("tensor", "ldw", CONST_LDW)
        for t in range(NT):
            if t > 0:
                pl.wait("tensor", "ve", gate_ve[t - 1])
            for c in range(NC):
                wgr = wg_sb[:].rearrange("p (c e) -> p c e", e=8)
                if c < NC - 1:
                    pl.run("tensor", lambda e, c=c, t=t, w=wgr: e.matmul(
                        ps_l, xn_r[:, c, t * P:(t + 1) * P], w[:, c, :],
                        start=(c == 0), stop=False))
                else:
                    pl.inc("tensor", "te", lambda e, c=c, t=t, w=wgr: e.matmul(
                        ps_l, xn_r[:, c, t * P:(t + 1) * P], w[:, c, :],
                        start=False, stop=True))
            te_l = pl.ctr["te"].v
            pl.wait("vector", "te", te_l)
            pl.inc("vector", "ve", lambda v: v.reduce_max(gs[:, 0:1], ps_l, axis=AX.X))
            pl.wait("vector", "ve")
            pl.inc("vector", "ve", lambda v: v.tensor_scalar(
                gs[:, 1:2], gs[:, 0:1], -1.0, None, ALU.mult))
            ve_m = pl.ctr["ve"].v
            pl.wait("scalar", "ve", ve_m)
            pl.inc("scalar", "se", lambda s: s.activation(
                gex[:], ps_l, AF.Exp, bias=gs[:, 1:2]))
            pl.wait("vector", "se")
            pl.inc("vector", "ve", lambda v: v.reduce_sum(gs[:, 2:3], gex[:], axis=AX.X))
            pl.wait("vector", "ve")
            pl.inc("vector", "ve", lambda v, t=t: v.reciprocal(
                gatemat[:, t:t + 1], gs[:, 2:3]))
            pl.wait("vector", "ve")
            for e_ in range(8):
                if e_ == 0:
                    pl.inc("vector", "ve", lambda v, t=t: v.tensor_scalar(
                        idxmat[:, t:t + 1], ps_l[:, 0:1], gs[:, 0:1], 0.0,
                        ALU.is_equal, ALU.mult))
                else:
                    pl.inc("vector", "ve", lambda v, e_=e_: v.tensor_scalar(
                        gs[:, 3:4], ps_l[:, e_:e_ + 1], gs[:, 0:1], float(e_),
                        ALU.is_equal, ALU.mult))
                    pl.wait("vector", "ve")
                    pl.inc("vector", "ve", lambda v, t=t: v.tensor_add(
                        idxmat[:, t:t + 1], idxmat[:, t:t + 1], gs[:, 3:4]))
                pl.wait("vector", "ve")
            gate_ve[t] = pl.ctr["ve"].v
        GATE_ALL = pl.ctr["ve"].v

        pl.wait("tensor", "ve", GATE_ALL)
        pl.wait("tensor", "ldw", CONST_LDW)
        pl.inc("tensor", "te", lambda e: e.transpose(
            ps_e[1][0:8, 0:P], idxmat[:], idf[:]))
        TE_TR = pl.ctr["te"].v
        pl.wait("vector", "te", TE_TR)
        pl.inc("vector", "ve", lambda v: v.tensor_copy(rows8[:], ps_e[1][0:8, 0:P]))
        VE_ROWS = pl.ctr["ve"].v
        pl.wait("gpsimd", "ve", VE_ROWS)
        pl.inc("gpsimd", "sc", lambda g: g.dma_start(
            out=idxr_d.rearrange("one (t p) -> (one t) p", p=P), in_=rows8[:]), 16)
        pl.wait("gpsimd", "sc")
        pl.inc("gpsimd", "sc", lambda g: g.dma_start(
            out=idx8[:], in_=bass.broadcast_tensor_aps(idx8[:], idxr_d)[1]), 16)
        IDX8_SC = pl.ctr["sc"].v
        pl.wait("vector", "sc", IDX8_SC)
        pl.inc("vector", "ve", lambda v: v.tensor_scalar(
            flags8[:], idx8[:], gs[0:8, 4:5], None, ALU.is_equal))
        pl.wait("vector", "ve")
        pl.inc("vector", "ve", lambda v: v.tensor_tensor_scan(
            csum8[:], ones8[:], flags8[:], 0.0, ALU.mult, ALU.add))
        pl.wait("vector", "ve")
        pl.inc("vector", "ve", lambda v: v.scalar_tensor_tensor(
            slotrow[:], csum8[:], gs[0:8, 5:6], flags8[:], ALU.add, ALU.mult))
        pl.wait("vector", "ve")
        pl.inc("vector", "ve", lambda v: v.tensor_scalar(
            csum8[:], csum8[:], float(CAP) + 0.5, BIGIDX, ALU.is_gt, ALU.mult))
        pl.wait("vector", "ve")
        pl.inc("vector", "ve", lambda v: v.scalar_tensor_tensor(
            slotrow[:], csum8[:], 1.0, slotrow[:], ALU.mult, ALU.add))
        SLOT8_VE = pl.ctr["ve"].v
        pl.wait("tensor", "ve", SLOT8_VE)
        for qc in range(2):
            lam = (lambda e, qc=qc: e.matmul(
                ps_e[2 + qc][0:1, :], ones8[0:8, 0:1],
                slotrow[:, qc * 512:(qc + 1) * 512], start=True, stop=True))
            if qc == 1:
                pl.inc("tensor", "te", lam)
            else:
                pl.run("tensor", lam)
        TE_SLOT = pl.ctr["te"].v
        pl.wait("vector", "te", TE_SLOT)
        for qc in range(2):
            lam = (lambda v, qc=qc: v.tensor_copy(
                sloti[0:1, qc * 512:(qc + 1) * 512], ps_e[2 + qc][0:1, :]))
            if qc == 1:
                pl.inc("vector", "ve", lam)
            else:
                pl.run("vector", lam)
        SLOTI_VE = pl.ctr["ve"].v
        pl.wait("gpsimd", "ve", SLOTI_VE)
        pl.inc("gpsimd", "sc", lambda g: g.dma_start(
            out=slot_d, in_=sloti[0:1, :]), 16)
        pl.wait("gpsimd", "sc")
        pl.inc("gpsimd", "sc", lambda g: g.dma_start(
            out=slot8[:], in_=slot_d.rearrange("one (t p) -> t (one p)", p=P)), 16)
        SLOT8_SC = pl.ctr["sc"].v
        pl.wait("tensor", "sc", SLOT8_SC)
        pl.inc("tensor", "te", lambda e: e.transpose(
            ps_e[1][:, 0:8], slot8[:], idf[0:8, 0:8]))
        TE_SLOTM = pl.ctr["te"].v
        pl.wait("vector", "te", TE_SLOTM)
        pl.inc("vector", "ve", lambda v: v.tensor_copy(slotmat[:], ps_e[1][:, 0:8]))
        SLOTMAT_VE = pl.ctr["ve"].v
        SLOTMAT_SC = pl.ctr["sc"].v

        pl.wait("gpsimd", "ve", SLOTMAT_VE)
        pl.wait("gpsimd", "ldw", PREFILL_LDW)
        xt_st = []
        for t in range(NT):
            if t >= 2:
                pl.wait("sync", "sc", xt_st[t - 2])
            pl.dma("sync", "ld", xtile[t % 2][:], xntm_d[t * P:(t + 1) * P, :])
            lv = pl.ctr["ld"].v
            pl.wait("gpsimd", "ld", lv)
            pl.inc("gpsimd", "sc", lambda e, t=t: e.indirect_dma_start(
                xng_d, IndirectOffsetOnAxis(ap=slotmat[:, t:t + 1], axis=0),
                xtile[t % 2][:], None), 16)
            pl.inc("gpsimd", "sc", lambda e, t=t: e.indirect_dma_start(
                ids_d, IndirectOffsetOnAxis(ap=slotmat[:, t:t + 1], axis=0),
                tokid[:, t:t + 1], None), 16)
            pl.inc("gpsimd", "sc", lambda e, t=t: e.indirect_dma_start(
                gts_d, IndirectOffsetOnAxis(ap=slotmat[:, t:t + 1], axis=0),
                gatemat[:, t:t + 1], None), 16)
            xt_st.append(pl.ctr["sc"].v)
        SCATTER_SC = pl.ctr["sc"].v

        pl.wait("sync", "sc", SCATTER_SC)
        pl.wait("sync", "ldw", PREFILL_LDW)
        pl.dma("sync", "ld", gt16[:], gts_d.rearrange("(c p) one -> c (p one)", p=P))
        pl.dma("sync", "ld", id16[:], ids_d.rearrange("(c p) one -> c (p one)", p=P))
        GATES_LD0 = pl.ctr["ld"].v
        pl.wait("tensor", "ld", GATES_LD0)
        pl.wait("tensor", "ve", SLOTI_VE)
        pl.run("tensor", lambda e: e.transpose(
            ps_e[6][:, 0:16], gt16[:], idf[0:16, 0:16]))
        pl.inc("tensor", "te", lambda e: e.transpose(
            ps_e[6][:, 16:32], id16[:], idf[0:16, 0:16]))
        TE_G16 = pl.ctr["te"].v
        pl.wait("vector", "te", TE_G16)
        pl.run("vector", lambda v: v.tensor_copy(gates_sb[:], ps_e[6][:, 0:16]))
        pl.inc("vector", "ve", lambda v: v.tensor_copy(ids_sb[:], ps_e[6][:, 16:32]))
        GATES_LD = pl.ctr["ve"].v

        yscat_sc = {}
        ev_ve = {}
        wuse_te = {}
        bias_ld = {}
        trev_ve = []
        gt_last_te = {}
        pl.wait("sync", "sc", SCATTER_SC)
        for half in range(2):
            for jl in range(8):
                j = half * 8 + jl
                if j >= 2:
                    pl.wait("sync", "te", gt_last_te[j - 2])
                pl.dma("sync", "ld", gtile[j % 2][:], xng_d[j * P:(j + 1) * P, :])
                lv = pl.ctr["ld"].v
                pl.wait("tensor", "ld", lv)
                if j == 0:
                    pl.wait("tensor", "ve", SLOTI_VE)
                if j == 8:
                    pl.wait("tensor", "ve", ev_ve[3])
                for cb in range(4):
                    bank = 4 + (j * 4 + cb) % 2
                    if j * 4 + cb >= 2:
                        pl.wait("tensor", "ve", trev_ve[(j * 4 + cb) - 2])
                    for cc in range(4):
                        c = cb * 4 + cc
                        lam = (lambda e, bank=bank, cc=cc, c=c, j=j: e.transpose(
                            ps_e[bank][:, cc * P:(cc + 1) * P],
                            gtile[j % 2][:, c * P:(c + 1) * P], idf[:]))
                        if cc == 3:
                            pl.inc("tensor", "te", lam)
                        else:
                            pl.run("tensor", lam)
                    te_b = pl.ctr["te"].v
                    pl.wait("vector", "te", te_b)
                    dst = xng_r[:, cb * 4:(cb + 1) * 4, jl * P:(jl + 1) * P]
                    src_ = ps_e[bank][:].rearrange("p (c q) -> p c q", q=P)
                    pl.inc("vector", "ve", lambda v, d=dst, s=src_: v.tensor_copy(d, s))
                    trev_ve.append(pl.ctr["ve"].v)
                gt_last_te[j] = pl.ctr["te"].v
            GT_HALF_VE = pl.ctr["ve"].v
            for e_ in range(4 * half, 4 * half + 4):
                if e_ >= 1:
                    pl.wait("gpsimd", "te", bias_used_te)
                pl.dma("gpsimd", "ldw", bbuf[0:1, :], b_d[0:1, e_ * D:(e_ + 1) * D])
                bias_ld[e_] = pl.ctr["ldw"].v
                for c in range(NC):
                    idx_ = e_ * NC + c
                    if idx_ >= 2:
                        pl.wait("gpsimd", "te", wuse_te[idx_ - 2])
                    pl.dma("gpsimd", "ldw", wrow[idx_ % 2][:],
                           W_d[e_, c * P:(c + 1) * P, :])
                    lv = pl.ctr["ldw"].v
                    pl.wait("tensor", "ldw", lv)
                    if c == 0:
                        if e_ == 4 * half:
                            pl.wait("tensor", "ve", GT_HALF_VE)
                        if e_ > 0:
                            pl.wait("tensor", "ve", ev_ve[e_ - 1])
                    for ti in range(2):
                        gl = (e_ - 4 * half) * 2 + ti
                        for oc in range(4):
                            last = (ti == 1 and oc == 3)
                            lam = (lambda e, c=c, gl=gl, ti=ti, oc=oc, i=idx_ % 2:
                                   e.matmul(ps_e[ti * 4 + oc][:],
                                            xng_r[:, c, gl * P:(gl + 1) * P],
                                            wrow[i][:, oc * 512:(oc + 1) * 512],
                                            start=(c == 0), stop=False))
                            if last:
                                pl.inc("tensor", "te", lam)
                            else:
                                pl.run("tensor", lam)
                    wuse_te[idx_] = pl.ctr["te"].v
                pl.wait("tensor", "ldw", bias_ld[e_])
                for ti in range(2):
                    for oc in range(4):
                        last = (ti == 1 and oc == 3)
                        lam = (lambda e, ti=ti, oc=oc: e.matmul(
                            ps_e[ti * 4 + oc][:], ones_t[0:1, :],
                            bbuf[0:1, oc * 512:(oc + 1) * 512],
                            start=False, stop=True))
                        if last:
                            pl.inc("tensor", "te", lam)
                        else:
                            pl.run("tensor", lam)
                te_e = pl.ctr["te"].v
                bias_used_te = te_e
                pl.wait("vector", "te", te_e)
                pl.wait("vector", "ve", GATES_LD)
                if e_ >= 1:
                    pl.wait("vector", "sc", yscat_sc[e_ - 1])
                for ti in range(2):
                    gt = e_ * 2 + ti
                    for oc in range(4):
                        lam = (lambda v, ti=ti, gt=gt, oc=oc: v.tensor_scalar(
                            ysb[ti][:, oc * 512:(oc + 1) * 512], ps_e[ti * 4 + oc][:],
                            gates_sb[:, gt:gt + 1], None, ALU.mult))
                        if ti == 1 and oc == 3:
                            pl.inc("vector", "ve", lam)
                        else:
                            pl.run("vector", lam)
                ev_ve[e_] = pl.ctr["ve"].v
                pl.wait("gpsimd", "ve", ev_ve[e_])
                for ti in range(2):
                    gt = e_ * 2 + ti
                    pl.inc("gpsimd", "sc", lambda e, ti=ti, gt=gt: e.indirect_dma_start(
                        moe_d, IndirectOffsetOnAxis(ap=ids_sb[:, gt:gt + 1], axis=0),
                        ysb[ti][:], None), 16)
                yscat_sc[e_] = pl.ctr["sc"].v
        YS_ALL = pl.ctr["sc"].v

        if debug:
            pl.wait("sync", "sc", YS_ALL)
            for t in range(NT):
                pl.dma("sync", "st", dbg_moe[t * P:(t + 1) * P, :],
                       mtile[0][:]) if False else None
            pl.dma("sync", "st", dbg_slot, slot_d)
            pl.dma("sync", "st", dbg_idx, idxr_d)
            pl.wait("sync", "ve", GATES_LD)
            pl.dma("sync", "st", dbg_gts, gates_sb[:])
            pl.dma("sync", "st", dbg_ids, ids_sb[:])

        out_st = []
        resid_ve = {}
        for t in range(NT):
            if t == 0:
                pl.wait("sync", "sc", YS_ALL)
            if t >= 2:
                pl.wait("sync", "st", out_st[t - 2])
            pl.dma("sync", "ld", mtile[t % 2][:], moe_d[t * P:(t + 1) * P, :])
            if debug:
                lvm = pl.ctr["ld"].v
                pl.wait("sync", "ld", lvm)
                pl.dma("sync", "st", dbg_moe[t * P:(t + 1) * P, :], mtile[t % 2][:])
            pl.dma("sync", "ld", xtile[t % 2][:], x_d[t * P:(t + 1) * P, :])
            pl.dma("sync", "ld", ssb[t % 2][:], s_d[t * P:(t + 1) * P, :])
            lv = pl.ctr["ld"].v
            pl.wait("scalar", "ld", lv)
            if t >= 2:
                pl.wait("scalar", "ve", resid_ve[t - 2])
            pl.inc("scalar", "se", lambda s, t=t: s.activation(
                g_sb[t % 2][:], mtile[t % 2][:], AF.Silu))
            se_g = pl.ctr["se"].v
            pl.wait("vector", "se", se_g)
            pl.inc("vector", "ve", lambda v, t=t: v.tensor_mul(
                tsg[:], ssb[t % 2][:], g_sb[t % 2][:]))
            pl.wait("vector", "ve")
            pl.inc("vector", "ve", lambda v, t=t: v.tensor_add(
                x1b[t % 2][:], xtile[t % 2][:], tsg[:]))
            resid_ve[t] = pl.ctr["ve"].v
            pl.wait("sync", "ve", resid_ve[t])
            pl.dma("sync", "st", x1_o[t * P:(t + 1) * P, :], x1b[t % 2][:])
            out_st.append(pl.ctr["st"].v)

        pl.wait("sync", "st", pl.ctr["st"].v)
        pl.wait("sync", "sc", pl.ctr["sc"].v)
        pl.finalize(block)
    return nc


def _prep_C2(inputs, outA, outB):
    x = np.asarray(inputs["L0_attn_x"])
    wg = np.asarray(inputs["L0_ffn_wg"])
    W = np.asarray(inputs["L0_ffn_W"])
    W_t = np.ascontiguousarray(np.transpose(W, (0, 2, 1)))
    b = np.ascontiguousarray(np.asarray(inputs["L0_ffn_b"]).reshape(1, 8 * D))
    tokid = np.ascontiguousarray(
        (np.arange(NT)[None, :] * P + np.arange(P)[:, None]).astype(np.float32))
    ecol = np.zeros((8, 2), np.float32)
    ecol[:, 0] = np.arange(8)
    ecol[:, 1] = np.arange(8) * CAP - 1.0
    identf = np.eye(P, dtype=np.float32)
    maps = []
    for c in range(NCORE):
        b_, h = divmod(c, 2)
        maps.append({
            "xn_fm": outA[c]["xn_fm"],
            "xn_tm": outA[c]["xn_tm_o"],
            "x_i": np.ascontiguousarray(x[b_, h * HALF:(h + 1) * HALF, :]),
            "s_i": outB[c]["s_o"],
            "wg": np.ascontiguousarray(
                wg.reshape(NC, P, 8).transpose(1, 0, 2).reshape(P, P)),
            "W_t": W_t, "b_moe": b, "tokid": tokid, "ecol": ecol,
            "identf": identf,
        })
    return maps
